# version 16
# baseline (speedup 1.0000x reference)
import os
import numpy as np
import concourse.bass as bass
import concourse.mybir as mybir
from concourse.bass_utils import run_bass_kernel_spmd
from concourse.bass import IndirectOffsetOnAxis
from contextlib import ExitStack
from math import prod

F32 = mybir.dt.float32
BF16 = mybir.dt.bfloat16
U16 = mybir.dt.uint16
I32 = mybir.dt.int32
AF = mybir.ActivationFunctionType
ALU = mybir.AluOpType
AX = mybir.AxisListType

T = 2048
D = 1024
NT = T // 128
NC = D // 128
DTSZ = {F32: 4, BF16: 2, U16: 2, I32: 4}


class Prog:
    ENGS = ("sync", "gpsimd", "vector", "scalar", "tensor")

    def __init__(self, nc, es):
        self.nc = nc
        self.plan = {e: [] for e in self.ENGS}
        self.sem = {e: es.enter_context(nc.semaphore(f"S{e}")) for e in self.ENGS}
        self.cnt = {e: 0 for e in self.ENGS}
        self.marks = {}

    def emit(self, eng, fn, deps=(), label=None, dma=0):
        waits = tuple(self.marks[d] for d in deps)
        if dma:
            self.cnt[eng] += 16 * dma
        elif label:
            self.cnt[eng] += 1
        if label:
            self.marks[label] = (eng, self.cnt[eng])
        self.plan[eng].append((fn, waits, (label is not None and not dma), dma))

    def mark_here(self, eng, label):
        self.marks[label] = (eng, self.cnt[eng])

    def barrier(self, name):
        cur = {e: (e, self.cnt[e]) for e in self.ENGS}
        for e in self.ENGS:
            self.marks[f"__b_{name}_{e}"] = cur[e]
        for e in self.ENGS:
            deps = tuple(f"__b_{name}_{x}" for x in self.ENGS if x != e)
            def f(en, sem):
                return en.nop()
            self.emit(e, f, deps=deps, label=f"{name}_{e}")

    def build(self):
        nc = self.nc
        with nc.Block() as block:
            for eng in self.ENGS:
                def make(eng):
                    def body(e):
                        for fn, waits, inc1, dma in self.plan[eng]:
                            for weng, val in waits:
                                e.wait_ge(self.sem[weng], val)
                            last = fn(e, self.sem[eng])
                            if inc1:
                                assert last is not None, f"label group on {eng} returned None"
                                last.then_inc(self.sem[eng], 1)
                        e.wait_ge(self.sem[eng], self.cnt[eng])
                    return body
                getattr(block, eng)(make(eng))


class Arena:

    def __init__(self, nc, es, words):
        self.t = es.enter_context(nc.sbuf_tensor("arena", [128, words], F32))
        self.words = words
        self.off = 0

    def alloc(self, shape, dt=F32, rows=128):
        assert shape[0] == rows
        elems = prod(shape[1:])
        w = (elems * DTSZ[dt] + 3) // 4
        assert self.off + w <= self.words, f"arena overflow: {self.off + w} > {self.words}"
        ap = self.t[:, self.off:self.off + w]
        self.off += w
        if dt != F32:
            ap = ap.bitcast(dt)
        n = elems * (4 * w == elems * DTSZ[dt] or True)
        ap = ap[:, 0:elems] if ap.shape[1] != elems else ap
        if len(shape) > 2:
            ap = ap.rearrange("p (" + " ".join(f"a{i}" for i in range(len(shape) - 1)) + ") -> p " +
                              " ".join(f"a{i}" for i in range(len(shape) - 1)),
                              **{f"a{i}": shape[i + 1] for i in range(len(shape) - 1)})
        if rows != 128:
            ap = ap[0:rows]
        return ap

    def alloc_at(self, off, shape, dt=F32, rows=128):
        save = self.off
        self.off = off
        ap = self.alloc(shape, dt, rows)
        self.off = max(save, self.off) if False else save
        return ap

    def mark(self):
        return self.off

    def reset(self, m):
        self.off = m


def build_program(nphase=99, debug=False):
    nc = bass.Bass("TRN2", target_bir_lowering=False)
    es = ExitStack()

    def din(name, shape, dt=F32):
        return nc.dram_tensor(name, shape, dt, kind="ExternalInput").ap()

    def dscratch(name, shape, dt=F32):
        kind = "ExternalOutput" if debug else "Internal"
        return nc.dram_tensor(name, shape, dt, kind=kind).ap()

    x_in = din("x", [T, D])
    w_qkv = din("w_qkv", [D, 3 * D])
    proj_in = din("proj", [128, 64])
    w_out = din("w_out", [D, D])
    b_out = din("b_out", [1, D])
    cn_w1 = din("cn_w1", [D, D]); cn_b1 = din("cn_b1", [128, 8])
    cn_w2 = din("cn_w2", [D, 512]); cn_b2 = din("cn_b2", [128, 4])
    cn_w3 = din("cn_w3", [512, 256]); cn_b3 = din("cn_b3", [128, 2])
    cn_w4 = din("cn_w4", [128, 2]); cn_b4 = din("cn_b4", [1, 1])
    gc_w1 = din("gc_w1", [D, 512]); gc_b1 = din("gc_b1", [128, 4])
    gc_w2 = din("gc_w2", [512, 256]); gc_b2 = din("gc_b2", [128, 2])
    gc_w3 = din("gc_w3", [128, 2]); gc_b3 = din("gc_b3", [1, 1])
    a_range = din("a_range", [1, 1])
    ident_f = din("ident_f", [128, 128])
    bandmask = din("bandmask", [128, 3 * 128])
    wqkv1 = din("wqkv1", [D, 3 * D])
    wproj1 = din("wproj1", [D, D])
    gamma = din("gamma", [1, D])
    wqT = din("wqT", [D, 8192])
    keysT = din("keysT", [128, 64 * 256])
    tab_du = din("tab_du", [65536, 2 * D])
    cosq8 = din("cosq8", [128, T]); sinq8 = din("sinq8", [128, T])
    cosk = din("cosk", [128, T]); sink = din("sink", [128, T])
    trimask = din("trimask", [128, 128])

    out = nc.dram_tensor("out", [T, D], F32, kind="ExternalOutput").ap()

    xn_d = dscratch("xn_d", [128, NT * D])
    xnbf_f_d = dscratch("xnbf_f_d", [128, NC * T], U16)
    v_d = dscratch("v_d", [128, NT * D], U16)
    attnf_d = dscratch("attnf_d", [128, NC * T], U16)
    g_d = dscratch("g_d", [128, NT * D])
    x1_d = dscratch("x1_d", [128, NT * D])
    gf_d = dscratch("gf_d", [1, 1])
    tabdu_bf = dscratch("tabdu_bf", [65536, 2 * D], U16)
    dbg1 = dscratch("dbg1", [128, 16 * 16 * 16])
    dbg2 = dscratch("dbg2", [128, 16 * 16 * 16], U16)
    dbg3 = dscratch("dbg3", [128, 16 * 8 * 16])
    dbg4 = dscratch("dbg4", [128, 16 * 8 * 16], I32)

    P = Prog(nc, es)
    A = Arena(nc, es, 51000)

    pb = [es.enter_context(nc.psum_tensor(f"pb{i}", [128, 512], F32)) for i in range(8)]
    def pbf(i):
        return pb[i][:, 0:64].bitcast(BF16)

    c_identb = A.alloc([128, 128], BF16)
    c_identf = A.alloc([128, 128], F32)
    gf_sb = A.alloc([1, 1], F32, rows=1)
    ln1_rstd = A.alloc([128, NT], F32)
    ln1_nmr = A.alloc([128, NT], F32)
    base_min = A.mark()
    xn_t = A.alloc([128, NT, D], F32)
    m_xn = A.mark()
    fmaj_bf = A.alloc([128, NC, T], BF16)
    cast_scr = A.alloc([128, D], BF16)
    x_tile = A.alloc([128, D], F32)

    def ld_consts(e, sem):
        e.dma_start(out=c_identf, in_=ident_f).then_inc(sem, 16)
    P.emit("sync", ld_consts, dma=1, label="consts0")
    def cast_ident(e, sem):
        e.tensor_copy(c_identb, c_identf)
        return e.drain()
    P.emit("vector", cast_ident, deps=("consts0",), label="consts")

    ph_base = A.mark()

    st_sum = A.alloc([128, NT]); st_sq = A.alloc([128, NT])
    st_mean = A.alloc([128, NT]); st_var = A.alloc([128, NT])
    st_rstd = A.alloc([128, NT]); st_nmr = A.alloc([128, NT])
    sq_scr = A.alloc([128, D])

    for i in range(NT):
        def fx(e, sem, i=i):
            e.dma_start(out=x_tile, in_=x_in.rearrange("(t p) d -> p t d", p=128)[:, i, :]).then_inc(sem, 16)
        P.emit("sync", fx, deps=() if i == 0 else (f"sq_{i-1}",), dma=1, label=f"xld_{i}")
        def f(e, sem, i=i):
            return e.activation(sq_scr, x_tile, AF.Square)
        P.emit("scalar", f, deps=(f"xld_{i}",) + (() if i == 0 else (f"sq_{i-1}",)), label=f"sqdone_{i}")
        def f2(e, sem, i=i):
            e.reduce_sum(st_sum[:, i:i + 1], x_tile, axis=AX.X)
            e.reduce_sum(st_sq[:, i:i + 1], sq_scr, axis=AX.X)
            return e.tensor_copy(xn_t[:, i, :], x_tile)
        P.emit("vector", f2, deps=(f"sqdone_{i}",), label=f"sq_{i}")

    def stats(e, sem):
        e.tensor_scalar_mul(st_mean, st_sum, 1.0 / D)
        e.drain()
        e.tensor_scalar_mul(st_var, st_sq, 1.0 / D)
        e.drain()
        e.tensor_tensor(st_nmr, st_mean, st_mean, ALU.mult)
        e.drain()
        e.tensor_sub(st_var, st_var, st_nmr)
        e.drain()
        return e.tensor_scalar_add(st_var, st_var, 1e-5)
    P.emit("vector", stats, deps=(f"sq_{NT-1}",), label="var_done")
    def rstd1(e, sem):
        return e.activation(st_rstd, st_var, AF.Sqrt)
    P.emit("scalar", rstd1, deps=("var_done",), label="sqrt_done")
    def rstd2(e, sem):
        e.reciprocal(st_rstd, st_rstd)
        e.drain()
        e.tensor_tensor(st_nmr, st_mean, st_rstd, ALU.mult)
        e.drain()
        return e.tensor_scalar_mul(st_nmr, st_nmr, -1.0)
    P.emit("vector", rstd2, deps=("sqrt_done",), label="rstd_done")

    ntp = 0
    for i in range(NT):
        def f(e, sem, i=i):
            e.tensor_scalar(xn_t[:, i, :], xn_t[:, i, :], st_rstd[:, i:i + 1],
                            st_nmr[:, i:i + 1], ALU.mult, ALU.add)
            e.drain()
            e.tensor_copy(cast_scr, xn_t[:, i, :])
            return e.drain()
        P.emit("vector", f, deps=("rstd_done",) if i == 0 else (f"tpev_{ntp-1}",), label=f"xn_{i}")
        TPB0 = (2, 3, 6, 7)
        for c in range(NC):
            b = TPB0[ntp % 4]
            def ftp(e, sem, c=c, b=b):
                return e.transpose(pbf(b), cast_scr[:, c * 128:(c + 1) * 128], c_identb)
            deps = ["consts", f"xn_{i}"]
            if ntp >= 4:
                deps.append(f"tpev_{ntp-4}")
            P.emit("tensor", ftp, deps=tuple(deps), label=f"tp_{ntp}")
            def fev(e, sem, i=i, c=c, b=b):
                return e.tensor_copy(fmaj_bf[:, c, i * 128:(i + 1) * 128], pbf(b))
            P.emit("vector", fev, deps=(f"tp_{ntp}",), label=f"tpev_{ntp}")
            ntp += 1
    P.mark_here("vector", "xnbf_f_done")
    P.mark_here("vector", "xn_all")

    def st_xn(e, sem):
        e.dma_start(out=xn_d, in_=xn_t.rearrange("p a b -> p (a b)")).then_inc(sem, 16)
        e.dma_start(out=xnbf_f_d, in_=fmaj_bf.rearrange("p a b -> p (a b)").bitcast(U16)).then_inc(sem, 16)
    P.emit("sync", st_xn, deps=("xn_all", "xnbf_f_done"), dma=2, label="xn_stored")

    gcw1_bf = A.alloc([128, NC, 512], BF16)
    gcw2_bf = A.alloc([128, 4, 256], BF16)
    gcw3_bf = A.alloc([128, 2, 1], BF16)
    gcb1_t = A.alloc([128, 4]); gcb2_t = A.alloc([128, 2]); gcb3_t = A.alloc([1, 1], rows=1)
    gc_bf = A.alloc([128, NC], BF16)
    gc_col = A.alloc([128, NC])
    g1_col = A.alloc([128, 4]); g1_bf = A.alloc([128, 4], BF16)
    g2_col = A.alloc([128, 2]); g2_bf = A.alloc([128, 2], BF16)

    def ld_gcw(e, sem):
        e.dma_start(out=gcw1_bf, in_=gc_w1.rearrange("(c p) o -> p c o", p=128)).then_inc(sem, 16)
        e.dma_start(out=gcw2_bf, in_=gc_w2.rearrange("(c p) o -> p c o", p=128)).then_inc(sem, 16)
        e.dma_start(out=gcw3_bf, in_=gc_w3.rearrange("p (c o) -> p c o", o=1)).then_inc(sem, 16)
        e.dma_start(out=gcb1_t, in_=gc_b1).then_inc(sem, 16)
        e.dma_start(out=gcb2_t, in_=gc_b2).then_inc(sem, 16)
        e.dma_start(out=gcb3_t, in_=gc_b3).then_inc(sem, 16)
    P.emit("gpsimd", ld_gcw, dma=6, label="gcw_loaded")

    def gc_mean(e, sem):
        for c in range(NC):
            e.reduce_sum(gc_col[:, c:c + 1], fmaj_bf[:, c, :], axis=AX.X)
        e.drain()
        e.tensor_scalar_mul(gc_col, gc_col, 1.0 / T)
        e.drain()
        e.tensor_copy(gc_bf, gc_col)
        return e.drain()
    P.emit("vector", gc_mean, deps=("xnbf_f_done",), label="gc_done")

    p_gc = pb[5][:, 0:4]
    p_gf = pb[5][0:1, 4:5]
    for oc in range(4):
        def f(e, sem, oc=oc):
            for kc in range(NC):
                last = e.matmul(p_gc[:, oc:oc + 1], gcw1_bf[:, kc, oc * 128:(oc + 1) * 128],
                                gc_bf[:, kc:kc + 1], start=(kc == 0), stop=(kc == NC - 1))
            return last
        P.emit("tensor", f, deps=("gcw_loaded", "gc_done") if oc == 0 else (f"gfh1_{oc-1}",), label=f"gfh1_{oc}")
        def fe(e, sem, oc=oc):
            return e.activation(g1_col[:, oc:oc + 1], p_gc[:, oc:oc + 1], AF.Gelu,
                                bias=gcb1_t[:, oc:oc + 1])
        P.emit("scalar", fe, deps=(f"gfh1_{oc}",), label=f"gfh1e_{oc}")
    def f2c(e, sem):
        e.tensor_copy(g1_bf, g1_col)
        return e.drain()
    P.emit("vector", f2c, deps=tuple(f"gfh1e_{oc}" for oc in range(4)), label="g1bf")
    for oc in range(2):
        def f(e, sem, oc=oc):
            for kc in range(4):
                last = e.matmul(p_gc[:, oc:oc + 1], gcw2_bf[:, kc, oc * 128:(oc + 1) * 128],
                                g1_bf[:, kc:kc + 1], start=(kc == 0), stop=(kc == 3))
            return last
        P.emit("tensor", f, deps=("g1bf",) if oc == 0 else (f"gfh2_{oc-1}",), label=f"gfh2_{oc}")
        def fe(e, sem, oc=oc):
            return e.activation(g2_col[:, oc:oc + 1], p_gc[:, oc:oc + 1], AF.Gelu,
                                bias=gcb2_t[:, oc:oc + 1])
        P.emit("scalar", fe, deps=(f"gfh2_{oc}",), label=f"gfh2e_{oc}")
    def f3c(e, sem):
        e.tensor_copy(g2_bf, g2_col)
        return e.drain()
    P.emit("vector", f3c, deps=("gfh2e_0", "gfh2e_1"), label="g2bf")
    def f4(e, sem):
        for kc in range(2):
            last = e.matmul(p_gf, gcw3_bf[0:128, kc, 0:1], g2_bf[:, kc:kc + 1],
                            start=(kc == 0), stop=(kc == 1))
        return last
    P.emit("tensor", f4, deps=("g2bf",), label="gf_mm")
    def f4e(e, sem):
        return e.activation(gf_sb, p_gf, AF.Sigmoid, bias=gcb3_t)
    P.emit("scalar", f4e, deps=("gf_mm",), label="gf_done")
    def st_gf(e, sem):
        e.dma_start(out=gf_d, in_=gf_sb).then_inc(sem, 16)
    P.emit("sync", st_gf, deps=("gf_done",), dma=1, label="gf_stored")

    if nphase < 1:
        _finish_stub(P, out, xn_t)
        P.build()
        return nc, es

    P.barrier("p1")
    A.reset(ph_base)

    c_band = A.alloc([128, 3 * 128], F32)
    proj_bf = A.alloc([128, 64], BF16)
    def ld_p1a(e, sem):
        e.dma_start(out=c_band, in_=bandmask).then_inc(sem, 16)
        e.dma_start(out=proj_bf, in_=proj_in).then_inc(sem, 16)
        e.dma_start(out=wv_bf, in_=w_qkv.rearrange("(c p) o -> p c o", p=128)[:, :, 2 * D:3 * D]).then_inc(sem, 16)
    P.emit("gpsimd", ld_p1a, deps=("p1_gpsimd",), dma=3, label="p1w_loaded")

    v_base = A.mark()
    wv_bf = A.alloc([128, NC, D], BF16)
    v_tile = A.alloc([128, D], BF16)
    nv = 0
    for i in range(NT):
        for hf in range(2):
            b = nv % 2
            def f(e, sem, i=i, hf=hf, b=b):
                for kc in range(NC):
                    last = e.matmul(pb[b][:], fmaj_bf[:, kc, i * 128:(i + 1) * 128],
                                    wv_bf[:, kc, hf * 512:(hf + 1) * 512],
                                    start=(kc == 0), stop=(kc == NC - 1))
                return last
            deps = ["p1w_loaded", "p1_tensor"]
            if nv >= 2:
                deps.append(f"vev_{nv-2}")
            P.emit("tensor", f, deps=tuple(deps), label=f"vmm_{nv}")
            def fe(e, sem, i=i, hf=hf, b=b):
                return e.tensor_copy(v_tile[:, hf * 512:(hf + 1) * 512], pb[b][:])
            depv = [f"vmm_{nv}"]
            if hf == 0 and i > 0:
                depv.append(f"vst_{i-1}")
            P.emit("vector", fe, deps=tuple(depv), label=f"vev_{nv}")
            nv += 1
        def fst(e, sem, i=i):
            e.dma_start(out=v_d.rearrange("p (t d) -> p t d", t=NT)[:, i, :], in_=v_tile.bitcast(U16)).then_inc(sem, 16)
        P.emit("sync", fst, deps=(f"vev_{nv-1}",), dma=1, label=f"vst_{i}")
        if i < NT - 1:
            P.marks[f"vwar_{i}"] = P.marks[f"vst_{i}"]
    for i in range(NT - 1):
        pass

    A.reset(v_base)
    wqk_bf = A.alloc([128, NC, 4 * 128], BF16)
    qk_bf = A.alloc([128, 2, T], BF16)
    qphi = A.alloc([128, T], F32)
    kphi = A.alloc([128, T], F32)
    qphi_bf = A.alloc([128, T], BF16)
    kphi_bf = A.alloc([128, T], BF16)
    ksum = A.alloc([128, T], F32)
    zrow = A.alloc([128, T], F32, rows=128)
    invzT = A.alloc([128, NT, 2], F32)
    phi_min = A.alloc([128, 512], F32)
    phi_max = A.alloc([128, 512], F32)
    phi_exp = A.alloc([128, 512], F32)
    stb = A.alloc([128, 3, 384], BF16)
    attn_sc = A.alloc([128, 3, 128], BF16)
    attn_head = A.alloc([128, 2, T], BF16)
    vh_bf = A.alloc([128, 2, NT, 128], BF16)
    blk1 = A.alloc([128, 2], F32)

    def fb1(e, sem):
        e.memset(blk1, 0.0)
        e.drain()
        e.memset(blk1[0:64, 0:1], 1.0)
        e.drain()
        e.memset(blk1[64:128, 1:2], 1.0)
        return e.drain()
    P.emit("vector", fb1, deps=("p1_vector",), label="blk1_done")

    nqk = 0
    nst = 0
    npv = 0
    ntz = 0
    for ph in range(4):
        def ldw(e, sem, ph=ph):
            w3 = w_qkv.rearrange("(c p) o -> p c o", p=128)
            e.dma_start(out=wqk_bf[:, :, 0:256], in_=w3[:, :, ph * 256:(ph + 1) * 256]).then_inc(sem, 16)
            e.dma_start(out=wqk_bf[:, :, 256:512], in_=w3[:, :, D + ph * 256:D + (ph + 1) * 256]).then_inc(sem, 16)
        deps = ("p1_gpsimd", f"vst_{NT-1}") if ph == 0 else (f"phibf_{ph-1}",)
        P.emit("gpsimd", ldw, deps=deps, dma=2, label=f"wqk_{ph}")

        for qk in range(2):
            for hh in range(2):
                for tc in range(4):
                    b = nqk % 2
                    def f(e, sem, qk=qk, hh=hh, tc=tc, b=b):
                        for kc in range(NC):
                            last = e.matmul(pb[b][:], wqk_bf[:, kc, qk * 256 + hh * 128: qk * 256 + (hh + 1) * 128],
                                            fmaj_bf[:, kc, tc * 512:(tc + 1) * 512],
                                            start=(kc == 0), stop=(kc == NC - 1))
                        return last
                    deps = [f"wqk_{ph}"]
                    if nqk == 0:
                        deps.append(f"vev_{nv-1}")
                    if nqk >= 2:
                        deps.append(f"qkpost_{nqk-2}")
                    if npv >= 1 and qk == 0 and hh == 0 and tc == 0:
                        deps.append(f"pvev_{npv-1}")
                    P.emit("tensor", f, deps=tuple(deps), label=f"qkmm_{nqk}")
                    def fe(e, sem, hh=hh, tc=tc, b=b):
                        return e.tensor_copy(qk_bf[:, hh, tc * 512:(tc + 1) * 512], pb[b][:])
                    P.emit("vector", fe, deps=(f"qkmm_{nqk}",), label=f"qkpost_{nqk}")
                    nqk += 1
            phit = qphi if qk == 0 else kphi
            for tc in range(4):
                b = nqk % 2
                def f(e, sem, tc=tc, b=b):
                    e.matmul(pb[b][0:64, :], proj_bf[:, 0:64], qk_bf[:, 0, tc * 512:(tc + 1) * 512],
                             start=True, stop=True)
                    return e.matmul(pb[b][64:128, :], proj_bf[:, 0:64], qk_bf[:, 1, tc * 512:(tc + 1) * 512],
                                    start=True, stop=True)
                deps = [f"qkpost_{nqk-1}"]
                if nqk >= 2:
                    deps.append(f"qkpost_{nqk-2}")
                P.emit("tensor", f, deps=tuple(deps), label=f"qkmm_{nqk}")
                def fel(e, sem, b=b):
                    e.tensor_scalar_min(phi_min, pb[b][:], 0.0)
                    e.tensor_scalar_max(phi_max, pb[b][:], 0.0)
                    return e.drain()
                P.emit("vector", fel, deps=(f"qkmm_{nqk}",), label=f"qkpost_{nqk}_a")
                def fes(e, sem):
                    return e.activation(phi_exp, phi_min, AF.Exp)
                P.emit("scalar", fes, deps=(f"qkpost_{nqk}_a",), label=f"qkpost_{nqk}_b")
                def fea(e, sem, phit=phit, tc=tc):
                    e.tensor_add(phit[:, tc * 512:(tc + 1) * 512], phi_exp, phi_max)
                    return e.drain()
                P.emit("vector", fea, deps=(f"qkpost_{nqk}_b",), label=f"qkpost_{nqk}")
                nqk += 1
        def fcast(e, sem):
            e.tensor_copy(qphi_bf, qphi)
            e.tensor_copy(kphi_bf, kphi)
            return e.drain()
        P.emit("vector", fcast, deps=(f"qkpost_{nqk-1}",), label=f"phibf_{ph}")

        def fks(e, sem):
            e.tensor_copy(ksum, kphi)
            e.drain()
            for o in (-3, -2, -1, 1, 2, 3):
                if o > 0:
                    e.tensor_add(ksum[:, 0:T - o], ksum[:, 0:T - o], kphi[:, o:T])
                else:
                    e.tensor_add(ksum[:, -o:T], ksum[:, -o:T], kphi[:, 0:T + o])
                e.drain()
            e.tensor_tensor(ksum, ksum, qphi, ALU.mult)
            return e.drain()
        P.emit("vector", fks, deps=(f"phibf_{ph}",), label=f"ks_{ph}")

        for tc in range(4):
            def fz(e, sem, tc=tc):
                return e.matmul(pb[4][0:2, :], blk1, ksum[:, tc * 512:(tc + 1) * 512],
                                start=True, stop=True)
            deps = ["blk1_done", f"ks_{ph}"]
            if not (ph == 0 and tc == 0):
                deps.append(f"zev_{ph}_{tc-1}" if tc > 0 else f"zev_{ph-1}_3")
            if ph == 0 and tc == 0:
                deps.append("gf_done")
            P.emit("tensor", fz, deps=tuple(deps), label=f"zmm_{ph}_{tc}")
            def fzev(e, sem, tc=tc):
                return e.tensor_copy(zrow[0:2, tc * 512:(tc + 1) * 512], pb[4][0:2, :])
            P.emit("vector", fzev, deps=(f"zmm_{ph}_{tc}",), label=f"zev_{ph}_{tc}")

        def finvz(e, sem):
            e.tensor_scalar_add(zrow[0:2, :], zrow[0:2, :], 1e-8)
            e.drain()
            e.reciprocal(zrow[0:2, :], zrow[0:2, :])
            return e.drain()
        P.emit("vector", finvz, deps=(f"zev_{ph}_3",), label=f"invz_{ph}")

        for qt in range(NT):
            def ftz(e, sem, qt=qt):
                return e.transpose(pb[4][0:128, 256:258], zrow[0:2, qt * 128:(qt + 1) * 128], c_identf[0:2, 0:2])
            deps = [f"invz_{ph}"]
            if ntz >= 1:
                deps.append(f"tzev_{ntz-1}")
            P.emit("tensor", ftz, deps=tuple(deps), label=f"tz_{ntz}")
            def ftzev(e, sem, qt=qt):
                return e.tensor_copy(invzT[:, qt, :], pb[4][0:128, 256:258])
            P.emit("vector", ftzev, deps=(f"tz_{ntz}",), label=f"tzev_{ntz}")
            ntz += 1
        P.mark_here("vector", f"invzT_{ph}")

        def ldvh(e, sem, ph=ph):
            v3 = v_d.rearrange("p (t d) -> p t d", t=NT)
            e.dma_start(out=vh_bf[:, 0, :, :].bitcast(U16), in_=v3[:, :, (2 * ph) * 128:(2 * ph + 1) * 128]).then_inc(sem, 16)
            e.dma_start(out=vh_bf[:, 1, :, :].bitcast(U16), in_=v3[:, :, (2 * ph + 1) * 128:(2 * ph + 2) * 128]).then_inc(sem, 16)
        deps = [f"vst_{NT-1}"]
        if ph > 0:
            deps.append(f"pvdone_{ph-1}")
        P.emit("gpsimd", ldvh, deps=tuple(deps), dma=2, label=f"vh_{ph}")

        for hh in range(2):
            rb = hh * 64
            def emit_st(kt):
                nonlocal nst
                qts = [q for q in (kt - 1, kt, kt + 1) if 0 <= q < NT]
                b = nst % 2
                def fst(e, sem, rb=rb, kt=kt, qts=qts, b=b):
                    first = True
                    for q in qts:
                        reg = q - kt + 1
                        last = e.matmul(pb[2 + b][:, 0:384][:, reg * 128:(reg + 1) * 128],
                                        kphi_bf[rb:rb + 64, kt * 128:(kt + 1) * 128],
                                        qphi_bf[rb:rb + 64, q * 128:(q + 1) * 128],
                                        start=first, stop=(q == qts[-1]))
                        first = False
                    return last
                deps = [f"phibf_{ph}", f"invzT_{ph}"]
                if nst >= 2:
                    deps.append(f"stev_{nst-2}")
                if npv >= 1 and kt >= 3:
                    deps.append(f"pvmm_{npv-1}")
                P.emit("tensor", fst, deps=tuple(deps), label=f"stmm_{nst}")
                def fstev(e, sem, kt=kt, qts=qts, b=b):
                    r0 = qts[0] - kt + 1
                    r1 = qts[-1] - kt + 2
                    return e.tensor_tensor(stb[:, kt % 3, r0 * 128:r1 * 128],
                                           pb[2 + b][:, 0:384][:, r0 * 128:r1 * 128],
                                           c_band[:, r0 * 128:r1 * 128], ALU.mult)
                P.emit("vector", fstev, deps=(f"stmm_{nst}",), label=f"stev_{nst}")
                nst += 1

            TPB1 = (5, 6, 7)
            def emit_pv(qt):
                nonlocal npv
                kts = [k for k in (qt - 1, qt, qt + 1) if 0 <= k < NT]
                b = npv % 2
                def fpv(e, sem, hh=hh, qt=qt, kts=kts, b=b):
                    first = True
                    for k in kts:
                        reg = qt - k + 1
                        last = e.matmul(pb[b][:, 0:128], stb[:, k % 3, reg * 128:(reg + 1) * 128],
                                        vh_bf[:, hh, k, :],
                                        start=first, stop=(k == kts[-1]))
                        first = False
                    return last
                deps = [f"stev_{nst-1}", f"vh_{ph}", f"qkpost_{nqk-1}"]
                if npv >= 2:
                    deps.append(f"pvev_{npv-2}")
                P.emit("tensor", fpv, deps=tuple(deps), label=f"pvmm_{npv}")
                bs = npv % 3
                def fpvev(e, sem, hh=hh, qt=qt, b=b, bs=bs):
                    e.tensor_scalar_mul(attn_sc[:, bs, :], pb[b][:, 0:128], invzT[:, qt, hh:hh + 1])
                    return e.drain()
                depv = [f"pvmm_{npv}"]
                if npv >= 3:
                    depv.append(f"pvtp_{npv-3}")
                P.emit("vector", fpvev, deps=tuple(depv), label=f"pvev_{npv}")
                def fpvtp(e, sem, bs=bs):
                    return e.transpose(pbf(TPB1[bs]), attn_sc[:, bs, :], c_identb)
                dept = [f"pvev_{npv}"]
                if npv >= 3:
                    dept.append(f"pvtev_{npv-3}")
                if npv <= 2:
                    dept.append("gf_stored")
                P.emit("tensor", fpvtp, deps=tuple(dept), label=f"pvtp_{npv}")
                def fpvtev(e, sem, hh=hh, qt=qt, bs=bs):
                    return e.tensor_copy(attn_head[:, hh, qt * 128:(qt + 1) * 128], pbf(TPB1[bs]))
                P.emit("vector", fpvtev, deps=(f"pvtp_{npv}",), label=f"pvtev_{npv}")
                npv += 1

            for kt in range(NT):
                emit_st(kt)
                if kt >= 1:
                    emit_pv(kt - 1)
            emit_pv(NT - 1)
        P.mark_here("vector", f"pvdone_{ph}")
        def sthead(e, sem, ph=ph):
            a3 = attnf_d.rearrange("p (c t) -> p c t", c=NC)
            e.dma_start(out=a3[:, 2 * ph, :], in_=attn_head[:, 0, :].bitcast(U16)).then_inc(sem, 16)
            e.dma_start(out=a3[:, 2 * ph + 1, :], in_=attn_head[:, 1, :].bitcast(U16)).then_inc(sem, 16)
        P.emit("sync", sthead, deps=(f"pvdone_{ph}",), dma=2, label=f"ah_{ph}")
    P.mark_here("sync", "attn_stored")

    if nphase < 2:
        _finish_stub(P, out, xn_t)
        P.build()
        return nc, es

    P.barrier("p2")
    A.reset(ph_base)

    cnw1_bf = A.alloc([128, NC, D], BF16)
    cnw2_bf = A.alloc([128, NC, 512], BF16)
    cnw3_bf = A.alloc([128, 4, 256], BF16)
    cnw4_bf = A.alloc([128, 2, 1], BF16)
    cnb1_t = A.alloc([128, NC]); cnb2_t = A.alloc([128, 4])
    cnb3_t = A.alloc([128, 2]); cnb4_t = A.alloc([1, 1], rows=1)
    ar_sb = A.alloc([1, 1], rows=1)
    c0_sb = A.alloc([1, 1], rows=1); c1_sb = A.alloc([1, 1], rows=1)
    tt1 = A.alloc([1, 1], rows=1)
    h1_bf = A.alloc([128, NC, T], BF16)
    h2_bf = A.alloc([128, 4, T], BF16)
    h3_bf = A.alloc([128, 2, T], BF16)
    cp_row = A.alloc([1, T], rows=1)
    cpT = A.alloc([128, NT]); cpTn = A.alloc([128, NT])
    tw_tmp = A.alloc([128, 2, 256])

    def ld_cnw(e, sem):
        e.dma_start(out=cnw1_bf, in_=cn_w1.rearrange("(c p) o -> p c o", p=128)).then_inc(sem, 16)
        e.dma_start(out=cnw2_bf, in_=cn_w2.rearrange("(c p) o -> p c o", p=128)).then_inc(sem, 16)
        e.dma_start(out=cnw3_bf, in_=cn_w3.rearrange("(c p) o -> p c o", p=128)).then_inc(sem, 16)
        e.dma_start(out=cnw4_bf, in_=cn_w4.rearrange("p (c o) -> p c o", o=1)).then_inc(sem, 16)
        e.dma_start(out=cnb1_t, in_=cn_b1).then_inc(sem, 16)
        e.dma_start(out=cnb2_t, in_=cn_b2).then_inc(sem, 16)
        e.dma_start(out=cnb3_t, in_=cn_b3).then_inc(sem, 16)
        e.dma_start(out=cnb4_t, in_=cn_b4).then_inc(sem, 16)
        e.dma_start(out=ar_sb, in_=a_range).then_inc(sem, 16)
    P.emit("gpsimd", ld_cnw, deps=("p2_gpsimd",), dma=9, label="cnw_loaded")

    def fcc(e, sem):
        e.tensor_scalar_mul(tt1, gf_sb, 0.7)
        e.drain()
        e.tensor_scalar_add(tt1, tt1, -0.5)
        e.drain()
        e.tensor_tensor(tt1, tt1, ar_sb, ALU.mult)
        e.drain()
        e.tensor_scalar_mul(tt1, tt1, 2.0)
        e.drain()
        e.tensor_scalar_add(c0_sb, tt1, 0.1)
        e.drain()
        e.tensor_scalar_mul(c1_sb, ar_sb, 0.6)
        return e.drain()
    P.emit("vector", fcc, deps=("gf_done", "cnw_loaded", "p2_vector"), label="cc_done")

    nmm = 0
    nftp = 0
    for cyc in range(3):
        for i in range(NT):
            def fc(e, sem, i=i, cyc=cyc):
                e.tensor_copy(cast_scr, xn_t[:, i, :])
                return e.drain()
            dep0 = ["cc_done"] if (cyc == 0 and i == 0) else [f"ftpev_{nftp-1}"]
            if cyc > 0 and i == 0:
                dep0 = [f"tw_done_{cyc-1}"]
            P.emit("vector", fc, deps=tuple(dep0), label=f"stc_{cyc}_{i}")
            for c in range(NC):
                b = nftp % 4
                def ftp(e, sem, c=c, b=b):
                    return e.transpose(pbf(b), cast_scr[:, c * 128:(c + 1) * 128], c_identb)
                deps = [f"stc_{cyc}_{i}"]
                if nftp >= 4:
                    deps.append(f"ftpev_{nftp-4}")
                if nftp < 4 and cyc == 0:
                    deps += [f"tzev_{ntz-1}", f"pvtev_{npv-1}", f"qkpost_{nqk-1}", f"stev_{nst-1}"]
                P.emit("tensor", ftp, deps=tuple(deps), label=f"ftp_{nftp}")
                def fev(e, sem, i=i, c=c, b=b):
                    return e.tensor_copy(fmaj_bf[:, c, i * 128:(i + 1) * 128], pbf(b))
                P.emit("vector", fev, deps=(f"ftp_{nftp}",), label=f"ftpev_{nftp}")
                nftp += 1
        last_tp = f"ftpev_{nftp-1}"

        def chain(win, kcs, ocs, src, dst, bias_t, prev):
            nonlocal nmm
            for oc in range(ocs):
                for tc in range(4):
                    b = nmm % 4
                    def f(e, sem, oc=oc, tc=tc, b=b, win=win, kcs=kcs, src=src):
                        for kc in range(kcs):
                            last = e.matmul(pb[b][:], win[:, kc, oc * 128:(oc + 1) * 128],
                                            src[:, kc, tc * 512:(tc + 1) * 512],
                                            start=(kc == 0), stop=(kc == kcs - 1))
                        return last
                    deps = list(prev) if (oc == 0 and tc == 0) else []
                    if nmm < 4:
                        deps.append(f"ftpev_{nftp-1}" if False else f"mme_{nmm-2}") if nmm >= 2 else None
                    if nmm >= 4:
                        deps.append(f"mme_{nmm-4}")
                    P.emit("tensor", f, deps=tuple(deps), label=f"mm_{nmm}")
                    def fe(e, sem, oc=oc, tc=tc, b=b, dst=dst, bias_t=bias_t):
                        return e.activation(dst[:, oc, tc * 512:(tc + 1) * 512], pb[b][:],
                                            AF.Gelu, bias=bias_t[:, oc:oc + 1])
                    P.emit("scalar", fe, deps=(f"mm_{nmm}",), label=f"mme_{nmm}")
                    nmm += 1
            return f"mme_{nmm-1}"

        prev0 = ["cnw_loaded", last_tp]
        if cyc == 0:
            prev0 += [f"pvtev_{npv-1}", f"qkpost_{nqk-1}"]
        l1 = chain(cnw1_bf, NC, NC, fmaj_bf, h1_bf, cnb1_t, tuple(prev0))
        l2 = chain(cnw2_bf, NC, 4, h1_bf, h2_bf, cnb2_t, (l1,))
        l3 = chain(cnw3_bf, 4, 2, h2_bf, h3_bf, cnb3_t, (l2,))
        for tc in range(4):
            def f(e, sem, tc=tc):
                for kc in range(2):
                    last = e.matmul(pb[4][0:1, :], cnw4_bf[0:128, kc, 0:1],
                                    h3_bf[:, kc, tc * 512:(tc + 1) * 512],
                                    start=(kc == 0), stop=(kc == 1))
                return last
            deps = [l3] if tc == 0 else [f"t4e_{cyc}_{tc-1}"]
            if cyc == 0 and tc == 0:
                deps.append(f"tzev_{ntz-1}")
            P.emit("tensor", f, deps=tuple(deps), label=f"t4_{cyc}_{tc}")
            def fe(e, sem, tc=tc):
                return e.activation(cp_row[:, tc * 512:(tc + 1) * 512], pb[4][0:1, :], AF.Sigmoid,
                                    bias=cnb4_t)
            P.emit("scalar", fe, deps=(f"t4_{cyc}_{tc}",), label=f"t4e_{cyc}_{tc}")
        def fcp(e, sem):
            e.tensor_scalar(cp_row, cp_row, c1_sb, c0_sb, ALU.mult, ALU.add)
            return e.drain()
        P.emit("vector", fcp, deps=(f"t4e_{cyc}_3",), label=f"cprow_{cyc}")
        for qt in range(NT):
            def ftc(e, sem, qt=qt):
                return e.transpose(pb[4][0:128, 256:257], cp_row[:, qt * 128:(qt + 1) * 128], c_identf[0:1, 0:1])
            deps = [f"cprow_{cyc}"]
            if qt > 0:
                deps.append(f"cpev_{cyc}_{qt-1}")
            P.emit("tensor", ftc, deps=tuple(deps), label=f"cptp_{cyc}_{qt}")
            def ftcev(e, sem, qt=qt):
                return e.tensor_copy(cpT[:, qt:qt + 1], pb[4][0:128, 256:257])
            P.emit("vector", ftcev, deps=(f"cptp_{cyc}_{qt}",), label=f"cpev_{cyc}_{qt}")
        def fneg(e, sem):
            e.tensor_scalar_mul(cpTn, cpT, -1.0)
            return e.drain()
        P.emit("vector", fneg, deps=(f"cpev_{cyc}_{NT-1}",), label=f"cpneg_{cyc}")
        for i in range(NT):
            def ftw(e, sem, i=i):
                q = lambda k: xn_t[:, i, k * 256:(k + 1) * 256]
                e.tensor_copy(tw_tmp[:, 0, :], q(0))
                e.tensor_copy(tw_tmp[:, 1, :], q(1))
                e.drain()
                e.scalar_tensor_tensor(q(0), q(2), cpT[:, i:i + 1], q(0), ALU.mult, ALU.add)
                e.scalar_tensor_tensor(q(1), q(3), cpTn[:, i:i + 1], q(1), ALU.mult, ALU.add)
                e.drain()
                e.scalar_tensor_tensor(q(2), tw_tmp[:, 0, :], cpTn[:, i:i + 1], q(2), ALU.mult, ALU.add)
                e.scalar_tensor_tensor(q(3), tw_tmp[:, 1, :], cpT[:, i:i + 1], q(3), ALU.mult, ALU.add)
                return e.drain()
            P.emit("vector", ftw, deps=(f"cpneg_{cyc}",), label=f"tw_{cyc}_{i}")
        P.mark_here("vector", f"tw_done_{cyc}")

    def st_g(e, sem):
        e.dma_start(out=g_d, in_=xn_t.rearrange("p a b -> p (a b)")).then_inc(sem, 16)
    P.emit("sync", st_g, deps=("tw_done_2",), dma=1, label="g_stored")

    if nphase < 3:
        _finish_stub(P, out, xn_t)
        P.build()
        return nc, es

    P.barrier("p3")
    A.reset(ph_base)

    wout_bf = A.alloc([128, NC, D], BF16)
    bout_row = A.alloc([1, D], rows=1)
    bout_bc = A.alloc([128, D])
    ones1 = A.alloc([1, 128], rows=1)
    sil = A.alloc([128, 512])
    cmb = A.alloc([128, 512])
    af_sl = A.alloc([128, 2, NC, 128], BF16)

    def ld_wout(e, sem):
        e.dma_start(out=wout_bf, in_=w_out.rearrange("(c p) o -> p c o", p=128)).then_inc(sem, 16)
        e.dma_start(out=bout_row, in_=b_out).then_inc(sem, 16)
    P.emit("gpsimd", ld_wout, deps=("p3_gpsimd",), dma=2, label="wout_loaded")
    def fones(e, sem):
        e.memset(ones1, 1.0)
        return e.drain()
    P.emit("vector", fones, deps=("p3_vector",), label="ones_done")
    for hf in range(2):
        def fbb(e, sem, hf=hf):
            return e.matmul(pb[hf][:], ones1, bout_row[:, hf * 512:(hf + 1) * 512],
                            start=True, stop=True)
        P.emit("tensor", fbb, deps=("ones_done", "wout_loaded", "p3_tensor"), label=f"bb_{hf}")
        def fbbe(e, sem, hf=hf):
            return e.tensor_copy(bout_bc[:, hf * 512:(hf + 1) * 512], pb[hf][:])
        P.emit("vector", fbbe, deps=(f"bb_{hf}",), label=f"bbe_{hf}")

    ncm = 0
    for i in range(NT):
        def ldaf(e, sem, i=i):
            a3 = attnf_d.rearrange("p (c t) -> p c t", c=NC)
            e.dma_start(out=af_sl[:, i % 2, :, :].bitcast(U16), in_=a3[:, :, i * 128:(i + 1) * 128]).then_inc(sem, 16)
        depl = ["attn_stored", "p3_gpsimd"]
        if i >= 2:
            depl.append(f"cmb_{2*(i-2)+1}")
        P.emit("gpsimd", ldaf, deps=tuple(depl), dma=1, label=f"afld_{i}")
        def ldx(e, sem, i=i):
            e.dma_start(out=x_tile, in_=x_in.rearrange("(t p) d -> p t d", p=128)[:, i, :]).then_inc(sem, 16)
        depx = ["p3_sync"]
        if i >= 1:
            depx.append(f"cmb_{2*(i-1)+1}")
        P.emit("sync", ldx, deps=tuple(depx), dma=1, label=f"xld2_{i}")
        for hf in range(2):
            b = 2 + (ncm % 2)
            def f(e, sem, i=i, hf=hf, b=b):
                for kc in range(NC):
                    last = e.matmul(pb[b][:], af_sl[:, i % 2, kc, :],
                                    wout_bf[:, kc, hf * 512:(hf + 1) * 512],
                                    start=(kc == 0), stop=(kc == NC - 1))
                return last
            deps = [f"afld_{i}"]
            if ncm == 0:
                deps += ["bbe_0", "bbe_1", "g_stored"]
            if ncm >= 2:
                deps.append(f"cmb_{ncm-2}")
            P.emit("tensor", f, deps=tuple(deps), label=f"smm_{ncm}")
            def fsil(e, sem, i=i, hf=hf):
                return e.activation(sil, xn_t[:, i, hf * 512:(hf + 1) * 512], AF.Silu)
            P.emit("scalar", fsil, deps=(f"smm_{ncm}",) + (() if ncm == 0 else (f"cmb_{ncm-1}",)), label=f"sil_{ncm}")
            def fcmb(e, sem, i=i, hf=hf, b=b):
                sl = slice(hf * 512, (hf + 1) * 512)
                e.tensor_tensor(cmb, pb[b][:], bout_bc[:, sl], ALU.add)
                e.drain()
                e.tensor_tensor(cmb, cmb, sil, ALU.mult)
                e.drain()
                e.tensor_tensor(xn_t[:, i, sl], x_tile[:, sl], cmb, ALU.add)
                return e.drain()
            P.emit("vector", fcmb, deps=(f"sil_{ncm}", f"xld2_{i}"), label=f"cmb_{ncm}")
            ncm += 1
    P.mark_here("vector", "x1_done")

    def st_x1(e, sem):
        e.dma_start(out=x1_d, in_=xn_t.rearrange("p a b -> p (a b)")).then_inc(sem, 16)
    P.emit("sync", st_x1, deps=("x1_done",), dma=1, label="x1_stored")

    if nphase < 4:
        def st_out(e, sem):
            e.dma_start(out=out.rearrange("(t p) d -> p t d", p=128), in_=xn_t).then_inc(sem, 16)
        P.emit("sync", st_out, deps=("x1_done",), dma=1, label="out_stored")
        P.build()
        return nc, es

    P.barrier("p4")
    A.reset(m_xn)
    xn2_t = A.alloc([128, NT, D], BF16)
    p5_base = A.mark()

    st_sum4 = A.alloc([128, NT]); st_sq4 = A.alloc([128, NT])
    st_mean4 = A.alloc([128, NT]); st_var4 = A.alloc([128, NT])
    st_a2 = A.alloc([128, NT]); st_b2 = A.alloc([128, NT])
    st_sv = A.alloc([128, NT])
    sq_scr4 = A.alloc([128, D])
    gam_bc = A.alloc([128, D])
    gam_row = A.alloc([1, D], rows=1)
    ones4 = A.alloc([1, 128], rows=1)
    xn1_scr = A.alloc([128, D])
    conv_buf = A.alloc([128, 2, 2, 2 * D], BF16)

    NCHUNK = 0 if os.environ.get("SKIP_CONV") else 256
    ncv = 0
    for tab_in, tab_out in (((tab_du, tabdu_bf),) if NCHUNK else ()):
        for ch in range(NCHUNK):
            bcv = ncv % 2
            def fldc(e, sem, tab_in=tab_in, ch=ch, bcv=bcv):
                src_ap = tab_in[ch * 256:(ch + 1) * 256, :].rearrange("(r p) d -> p r d", p=128)
                e.dma_start(out=conv_buf[:, bcv, :, :], in_=src_ap).then_inc(sem, 16)
            depc = ["p4_gpsimd"]
            if ncv >= 2:
                depc.append(f"cvst_{ncv-2}")
            P.emit("gpsimd", fldc, deps=tuple(depc), dma=1, label=f"cvld_{ncv}")
            def fstc(e, sem, tab_out=tab_out, ch=ch, bcv=bcv):
                dst_ap = tab_out[ch * 256:(ch + 1) * 256, :].rearrange("(r p) d -> p r d", p=128)
                e.dma_start(out=dst_ap, in_=conv_buf[:, bcv, :, :].bitcast(U16)).then_inc(sem, 16)
            P.emit("sync", fstc, deps=(f"cvld_{ncv}",), dma=1, label=f"cvst_{ncv}")
            ncv += 1
    P.mark_here("sync", "tables_done")
    NPH16 = 1 if os.environ.get("SMALL_A") else 16
    NTI5 = 1 if os.environ.get("SMALL_A") else NT

    for i in range(NT):
        def f(e, sem, i=i):
            return e.activation(sq_scr4, xn_t[:, i, :], AF.Square)
        P.emit("scalar", f, deps=("p4_scalar", "x1_done") if i == 0 else (f"sq4_{i-1}",), label=f"sqd4_{i}")
        def f2(e, sem, i=i):
            e.reduce_sum(st_sum4[:, i:i + 1], xn_t[:, i, :], axis=AX.X)
            return e.reduce_sum(st_sq4[:, i:i + 1], sq_scr4, axis=AX.X)
        P.emit("vector", f2, deps=(f"sqd4_{i}", "p4_vector"), label=f"sq4_{i}")
    def stats4(e, sem):
        e.tensor_scalar_mul(st_mean4, st_sum4, 1.0 / D)
        e.drain()
        e.tensor_scalar_mul(st_var4, st_sq4, 1.0 / D)
        e.drain()
        e.tensor_tensor(st_b2, st_mean4, st_mean4, ALU.mult)
        e.drain()
        e.tensor_sub(st_var4, st_var4, st_b2)
        e.drain()
        return e.tensor_scalar_add(st_a2, st_var4, 1e-5)
    P.emit("vector", stats4, deps=(f"sq4_{NT-1}",), label="var4_done")
    def rstd4a(e, sem):
        e.activation(ln1_rstd, st_a2, AF.Sqrt)
        return e.activation(st_sv, st_var4, AF.Sqrt)
    P.emit("scalar", rstd4a, deps=("var4_done",), label="sqrt4_done")
    def rstd4b(e, sem):
        e.reciprocal(ln1_rstd, ln1_rstd)
        e.drain()
        e.tensor_tensor(ln1_nmr, st_mean4, ln1_rstd, ALU.mult)
        e.drain()
        e.tensor_scalar_mul(ln1_nmr, ln1_nmr, -1.0)
        e.drain()
        e.tensor_tensor(st_a2, ln1_rstd, st_sv, ALU.mult)
        e.drain()
        e.reciprocal(st_a2, st_a2)
        e.drain()
        e.tensor_tensor(st_b2, ln1_nmr, st_a2, ALU.mult)
        e.drain()
        e.tensor_tensor(st_a2, ln1_rstd, st_a2, ALU.mult)
        return e.drain()
    P.emit("vector", rstd4b, deps=("sqrt4_done",), label="ln1_done")

    def fon4(e, sem):
        e.memset(ones4, 1.0)
        return e.drain()
    P.emit("vector", fon4, deps=("p4_vector",), label="ones4")
    def ldgam(e, sem):
        e.dma_start(out=gam_row, in_=gamma).then_inc(sem, 16)
    P.emit("sync", ldgam, deps=("p4_sync",), dma=1, label="gam_loaded")
    for hf in range(2):
        def fgb(e, sem, hf=hf):
            return e.matmul(pb[6][:], ones4, gam_row[:, hf * 512:(hf + 1) * 512], start=True, stop=True)
        P.emit("tensor", fgb, deps=("ones4", "gam_loaded", "p4_tensor") if hf == 0 else (f"gbe_{hf-1}",), label=f"gb_{hf}")
        def fgbe(e, sem, hf=hf):
            return e.tensor_copy(gam_bc[:, hf * 512:(hf + 1) * 512], pb[6][:])
        P.emit("vector", fgbe, deps=(f"gb_{hf}",), label=f"gbe_{hf}")

    for i in range(NT):
        def fxn2(e, sem, i=i):
            e.tensor_scalar(xn1_scr, xn_t[:, i, :], st_a2[:, i:i + 1], st_b2[:, i:i + 1],
                            ALU.mult, ALU.add)
            e.drain()
            e.tensor_tensor(xn2_t[:, i, :], xn1_scr, gam_bc, ALU.mult)
            return e.drain()
        P.emit("vector", fxn2, deps=("ln1_done", "gbe_1") if i == 0 else (f"xn2_{i-1}",), label=f"xn2_{i}")
    P.mark_here("vector", "xn2t_done")

    if nphase < 5:
        _finish_stub(P, out, xn_t)
        P.build()
        return nc, es

    P.barrier("p5")
    A.reset(p5_base)
    xn2_f = A.alloc_at(base_min, [128, NC, T], BF16)
    keysT_sb = A.alloc_at(base_min + 8192, [128, 64, 256], BF16)
    wdu = A.alloc_at(base_min, [128, 2, 8, 2 * D], BF16)
    wqsl_off = A.mark()
    wq_sl = A.alloc([128, 2, NC, 512], BF16)
    qf_sb = A.alloc([128, 4, 128], BF16)
    sim_sb = A.alloc([128, 256], F32)
    sim2_sb = A.alloc([128, 256], F32)
    sx_all = A.alloc([128, NT, 16, 16], F32)
    ix_all = A.alloc([128, NT, 16, 16], U16)
    w16_all = A.alloc([128, NT, 8, 16], F32)
    ixs_all = A.alloc([128, NT, 8, 16], U16)
    iys_all = A.alloc([128, NT, 8, 16], U16)
    A_sb = A.alloc([128, 256], F32)
    sc16 = A.alloc([128, 16], F32)
    pk16 = A.alloc([128, 16], U16)
    iota16 = A.alloc([128, 16], U16)
    isel = A.alloc([128, 16], U16)
    jsel = A.alloc([128, 16], U16)
    oh16 = A.alloc([128, 256], U16)
    pr16 = A.alloc([128, 256], U16)
    m0n = A.alloc([128, 1], F32)
    e16 = A.alloc([128, 16], F32)
    l16 = A.alloc([128, 1], F32)
    eidx = A.alloc([128, 128], I32)
    ei_a = A.alloc([128, 128], I32)
    ei_b = A.alloc([128, 128], I32)
    prod5 = A.alloc([128, 8, D], BF16)
    h16 = A.alloc([128, 16], F32)
    hh16 = A.alloc([128, 16], F32)
    g1_tile = A.alloc([128, D], F32)
    g1_acc = A.alloc_at(wqsl_off, [128, 4, D], F32)

    def ldk5(e, sem):
        e.dma_start(out=keysT_sb, in_=keysT.rearrange("p (a b) -> p a b", a=64)).then_inc(sem, 16)
    P.emit("gpsimd", ldk5, deps=("p5_gpsimd",), dma=1, label="keys_loaded")
    ntp5 = 0
    for i in range(NT):
        for c4 in range(2):
            def ftp(e, sem, i=i, c4=c4):
                for j in range(4):
                    last = e.transpose(pbf(j), xn2_t[:, i, (c4 * 4 + j) * 128:(c4 * 4 + j + 1) * 128], c_identb)
                return last
            deps = ["xn2t_done", "p5_tensor"]
            if ntp5 >= 1:
                deps.append(f"tp5ev_{ntp5-1}")
            P.emit("tensor", ftp, deps=tuple(deps), label=f"tp5_{ntp5}")
            def fev(e, sem, i=i, c4=c4):
                for j in range(4):
                    e.tensor_copy(xn2_f[:, c4 * 4 + j, i * 128:(i + 1) * 128], pbf(j))
                return e.drain()
            P.emit("vector", fev, deps=(f"tp5_{ntp5}",), label=f"tp5ev_{ntp5}")
            ntp5 += 1
    P.mark_here("vector", "xn2_done")
    def fio5(e, sem):
        return e.iota(iota16, pattern=[[1, 16]], base=0, channel_multiplier=0)
    P.emit("gpsimd", fio5, deps=("p5_gpsimd",), label="iota5")

    nq5 = 0
    for ph16 in range(NPH16):
        p_, h_ = ph16 // 8, ph16 % 8
        wb = ph16 % 2
        def ldw5(e, sem, p_=p_, h_=h_, wb=wb):
            col0 = p_ * 4096 + h_ * 512
            e.dma_start(out=wq_sl[:, wb, :, :],
                        in_=wqT.rearrange("(c p) o -> p c o", p=128)[:, :, col0:col0 + 512]).then_inc(sem, 16)
        depw = ["p5_gpsimd"]
        if ph16 >= 2:
            depw.append(f"s1_{ph16-2}_{NT-1}")
        P.emit("gpsimd", ldw5, deps=tuple(depw), dma=1, label=f"wq5_{ph16}")
        for i in range(NTI5):
            bq = nq5 % 2
            def fq(e, sem, i=i, wb=wb, bq=bq):
                for oc4 in range(4):
                    for kc in range(NC):
                        last = e.matmul(pb[bq][:, oc4 * 128:(oc4 + 1) * 128],
                                        wq_sl[:, wb, kc, oc4 * 128:(oc4 + 1) * 128],
                                        xn2_f[:, kc, i * 128:(i + 1) * 128],
                                        start=(oc4 == 0 and kc == 0), stop=(kc == NC - 1))
                return last
            depq = [f"wq5_{ph16}", "xn2_done"]

            if nq5 >= 2:
                depq.append(f"qev5_{nq5-2}")
            P.emit("tensor", fq, deps=tuple(depq), label=f"qmm5_{nq5}")
            def fqe(e, sem, bq=bq):
                return e.tensor_copy(qf_sb.rearrange("p a b -> p (a b)"), pb[bq][:])
            dqe = [f"qmm5_{nq5}"]
            if nq5 >= 1:
                dqe.append(f"sim5_{nq5-1}")
            P.emit("vector", fqe, deps=tuple(dqe), label=f"qev5_{nq5}")
            def fsim(e, sem, ph16=ph16, bq=bq):
                for j in range(4):
                    last = e.matmul(pb[4 + (bq % 2)][:, 0:256], qf_sb[:, j, :],
                                    keysT_sb[:, (ph16 % 8) * 8 + (ph16 // 8) * 4 + j, :],
                                    start=(j == 0), stop=(j == 3))
                return last
            dsim = [f"qev5_{nq5}", "keys_loaded"]
            if nq5 >= 2:
                dsim.append(f"simev5_{nq5-2}")
            P.emit("tensor", fsim, deps=tuple(dsim), label=f"sim5_{nq5}")
            def fsime(e, sem, bq=bq):
                return e.tensor_copy(sim_sb, pb[4 + (bq % 2)][:, 0:256])
            P.emit("vector", fsime, deps=(f"sim5_{nq5}",), label=f"simev5_{nq5}")
            def ftop1(e, sem, i=i, ph16=ph16):
                e.max(out=sx_all[:, i, ph16, 0:8], in_=sim_sb)
                e.drain()
                e.max_index(out=ix_all[:, i, ph16, 0:8], in_max=sx_all[:, i, ph16, 0:8], in_values=sim_sb)
                e.drain()
                e.match_replace(out=sim2_sb, in_to_replace=sx_all[:, i, ph16, 0:8], in_values=sim_sb,
                                imm_value=-3.0e38)
                e.drain()
                e.max(out=sx_all[:, i, ph16, 8:16], in_=sim2_sb)
                e.drain()
                e.max_index(out=ix_all[:, i, ph16, 8:16], in_max=sx_all[:, i, ph16, 8:16], in_values=sim2_sb)
                return e.drain()
            P.emit("vector", ftop1, deps=(f"simev5_{nq5}", "iota5"), label=f"s1_{ph16}_{i}")
            nq5 += 1
    if "s1_15_15" not in P.marks and NPH16 == 16:
        pass
    P.mark_here("vector", "stage1_done")

    NB5 = 0 if os.environ.get("SKIP_B") else NT
    for i in range(NB5):
        for h_ in range(8):
            def fs2(e, sem, i=i, h_=h_):
                sx = sx_all[:, i, h_, :]
                sy = sx_all[:, i, 8 + h_, :]
                e.tensor_tensor(A_sb.rearrange("p (a b) -> p a b", a=16),
                                sx.unsqueeze(2).to_broadcast([128, 16, 16]),
                                sy.unsqueeze(1).to_broadcast([128, 16, 16]), ALU.add)
                e.drain()
                e.max(out=sc16[:, 0:8], in_=A_sb)
                e.drain()
                e.max_index(out=pk16[:, 0:8], in_max=sc16[:, 0:8], in_values=A_sb)
                e.drain()
                e.match_replace(out=A_sb, in_to_replace=sc16[:, 0:8], in_values=A_sb, imm_value=-3.0e38)
                e.drain()
                e.max(out=sc16[:, 8:16], in_=A_sb)
                e.drain()
                e.max_index(out=pk16[:, 8:16], in_max=sc16[:, 8:16], in_values=A_sb)
                e.drain()
                e.tensor_scalar(isel, pk16, 4, None, ALU.logical_shift_right)
                e.tensor_scalar(jsel, pk16, 15, None, ALU.bitwise_and)
                e.drain()
                e.tensor_tensor(oh16.rearrange("p (a b) -> p a b", a=16),
                                isel.unsqueeze(2).to_broadcast([128, 16, 16]),
                                iota16.unsqueeze(1).to_broadcast([128, 16, 16]), ALU.is_equal)
                e.drain()
                e.tensor_tensor(pr16.rearrange("p (a b) -> p a b", a=16),
                                oh16.rearrange("p (a b) -> p a b", a=16),
                                ix_all[:, i, h_, :].unsqueeze(1).to_broadcast([128, 16, 16]), ALU.mult)
                e.drain()
                with nc.allow_low_precision(reason="u16 index select"):
                    e.reduce_sum(ixs_all[:, i, h_, :], pr16.rearrange("p (a b) -> p a b", a=16), axis=AX.X)
                e.drain()
                e.tensor_tensor(oh16.rearrange("p (a b) -> p a b", a=16),
                                jsel.unsqueeze(2).to_broadcast([128, 16, 16]),
                                iota16.unsqueeze(1).to_broadcast([128, 16, 16]), ALU.is_equal)
                e.drain()
                e.tensor_tensor(pr16.rearrange("p (a b) -> p a b", a=16),
                                oh16.rearrange("p (a b) -> p a b", a=16),
                                ix_all[:, i, 8 + h_, :].unsqueeze(1).to_broadcast([128, 16, 16]), ALU.mult)
                e.drain()
                with nc.allow_low_precision(reason="u16 index select"):
                    e.reduce_sum(iys_all[:, i, h_, :], pr16.rearrange("p (a b) -> p a b", a=16), axis=AX.X)
                e.drain()
                e.tensor_scalar_mul(m0n, sc16[:, 0:1], -1.0)
                return e.drain()
            dep_s2 = [f"s1_15_{i}"] if (i == 0 and h_ == 0) else [f"sm2_{i}_{h_-1}" if h_ > 0 else f"sm2_{i-1}_7"]
            if i > 0 and h_ == 0:
                dep_s2.append(f"s1_15_{i}")
            P.emit("vector", fs2, deps=tuple(dep_s2), label=f"s2_{i}_{h_}")
            def fexp(e, sem):
                return e.activation(e16, sc16, AF.Exp, bias=m0n)
            P.emit("scalar", fexp, deps=(f"s2_{i}_{h_}",), label=f"sexp_{i}_{h_}")
            def fsm(e, sem, i=i, h_=h_):
                e.reduce_sum(l16, e16, axis=AX.X)
                e.drain()
                e.reciprocal(l16, l16)
                e.drain()
                e.tensor_scalar_mul(w16_all[:, i, h_, :], e16, l16)
                return e.drain()
            P.emit("vector", fsm, deps=(f"sexp_{i}_{h_}",), label=f"sm2_{i}_{h_}")
    P.mark_here("vector", "stage2_done")

    ng5 = 0
    NCI = 1 if os.environ.get("SMALL_C") else NT
    for i in range(NCI):
        def feid(e, sem, i=i):
            e.tensor_copy(ei_a, ixs_all[:, i, :, :].rearrange("p a b -> p (a b)"))
            e.tensor_copy(ei_b, iys_all[:, i, :, :].rearrange("p a b -> p (a b)"))
            e.tensor_scalar(eidx, ei_a, 256, None, ALU.mult)
            return e.tensor_tensor(eidx, eidx, ei_b, ALU.add)
        depe = [f"sm2_{i}_7"]
        if i > 0:
            depe.append(f"gdone_{i-1}")
        if "stage2_done" not in P.marks:
            P.mark_here("vector", "stage2_done")
        P.emit("gpsimd", feid, deps=tuple(depe), label=f"eid_{i}")
        if debug:
            def fdbg(e, sem, i=i):
                e.dma_start(out=dbg4.rearrange("p (a b) -> p a b", a=16)[:, i, :], in_=eidx).then_inc(sem, 16)
            P.emit("sync", fdbg, deps=(f"eid_{i}",), dma=1, label=f"dbge_{i}")
        def fgz(e, sem, i=i):
            e.memset(g1_acc.rearrange("p a b -> p (a b)"), 0.0)
            return e.drain()
        P.emit("vector", fgz, deps=(f"eid_{i}",) + (() if i == 0 else (f"g1st_{i-1}",)), label=f"g1z_{i}")
        for h_ in range(8):
            for hb in range(2):
                def fgth(e, sem, i=i, h_=h_, hb=hb):
                    last = None
                    for kk in range(8):
                        kg = h_ * 16 + hb * 8 + kk
                        last = e.indirect_dma_start(
                            out=wdu[:, hb, kk, :].bitcast(U16), out_offset=None,
                            in_=tabdu_bf[0:128, :],
                            in_offset=IndirectOffsetOnAxis(ap=eidx[:, kg:kg + 1], axis=0))
                        last.then_inc(sem, 16)
                depg = [f"eid_{i}", "tables_done"]
                if ng5 >= 2:
                    depg.append(f"updot_{ng5-2}")
                P.emit("gpsimd", fgth, deps=tuple(depg), dma=8, label=f"gth_{ng5}")
                def fdown(e, sem, i=i, hb=hb):
                    return e.tensor_tensor(prod5, wdu[:, hb, :, 0:D],
                                           xn2_t[:, i, :].unsqueeze(1).to_broadcast([128, 8, D]),
                                           ALU.mult)
                P.emit("vector", fdown, deps=(f"gth_{ng5}",), label=f"down_{ng5}")
                def fgel(e, sem, hb=hb):
                    for kk in range(8):
                        e.activation(prod5[:, kk, :], prod5[:, kk, :], AF.Identity,
                                     accum_out=h16[:, hb * 8 + kk:hb * 8 + kk + 1])
                    return e.activation(hh16[:, hb * 8:(hb + 1) * 8], h16[:, hb * 8:(hb + 1) * 8],
                                        AF.Gelu)
                P.emit("scalar", fgel, deps=(f"down_{ng5}",), label=f"gel_{ng5}")
                def fup(e, sem, i=i, h_=h_, hb=hb):
                    sl = slice(hb * 8, (hb + 1) * 8)
                    e.tensor_tensor(h16[:, sl], hh16[:, sl], w16_all[:, i, h_, sl], ALU.mult)
                    e.drain()
                    for kk in range(8):
                        e.scalar_tensor_tensor(g1_acc[:, kk % 4, :], wdu[:, hb, kk, D:2 * D],
                                               h16[:, hb * 8 + kk:hb * 8 + kk + 1],
                                               g1_acc[:, kk % 4, :], ALU.mult, ALU.add)
                    return e.drain()
                P.emit("vector", fup, deps=(f"gel_{ng5}", f"g1z_{i}"), label=f"updot_{ng5}")
                ng5 += 1
        def fgsum(e, sem, i=i):
            e.tensor_add(g1_acc[:, 0, :], g1_acc[:, 0, :], g1_acc[:, 1, :])
            e.tensor_add(g1_acc[:, 2, :], g1_acc[:, 2, :], g1_acc[:, 3, :])
            e.drain()
            e.tensor_add(g1_tile, g1_acc[:, 0, :], g1_acc[:, 2, :])
            return e.drain()
        P.emit("vector", fgsum, deps=(f"updot_{ng5-1}",), label=f"gdone_{i}")
        def fg1st(e, sem, i=i):
            e.dma_start(out=g_d.rearrange("p (a b) -> p a b", a=NT)[:, i, :], in_=g1_tile).then_inc(sem, 16)
        P.emit("sync", fg1st, deps=(f"gdone_{i}",), dma=1, label=f"g1st_{i}")
    P.mark_here("sync", "g1_stored")

    if nphase < 6:
        _finish_stub(P, out, xn_t)
        P.build()
        return nc, es

    P.barrier("p6")
    A.reset(base_min)
    s1f = A.alloc([128, NC, T], BF16)
    v65 = A.alloc([128, NT, 16 * 65], BF16)
    xn1f = A.alloc([128, NC, T], BF16)
    Qf = A.alloc([128, NC, T], BF16)
    Kf = A.alloc([128, NC, T], BF16)
    m6 = A.mark()

    xn1sc = A.alloc([128, D], BF16)
    x1_tl = A.alloc([128, D], F32)
    pocket = A.mark()
    ntp6 = 0
    for i in range(NT):
        def fldx(e, sem, i=i):
            e.dma_start(out=x1_tl, in_=x1_d.rearrange("p (a b) -> p a b", a=NT)[:, i, :]).then_inc(sem, 16)
        depl = ["p6_sync", "x1_stored"]
        if i > 0:
            depl.append(f"tp6_{ntp6-1}")
        P.emit("sync", fldx, deps=tuple(depl), dma=1, label=f"x1ld_{i}")
        def fxn1(e, sem, i=i):
            e.tensor_scalar(xn1sc, x1_tl, ln1_rstd[:, i:i + 1], ln1_nmr[:, i:i + 1],
                            ALU.mult, ALU.add)
            return e.drain()
        P.emit("vector", fxn1, deps=(f"x1ld_{i}", "p6_vector"), label=f"xn1_{i}")
        for c4 in range(2):
            def ftp(e, sem, i=i, c4=c4):
                for j in range(4):
                    last = e.transpose(pbf(j), xn1sc[:, (c4 * 4 + j) * 128:(c4 * 4 + j + 1) * 128], c_identb)
                return last
            deps = [f"xn1_{i}", "p6_tensor"]
            if ntp6 >= 1:
                deps.append(f"tp6ev_{ntp6-1}")
            P.emit("tensor", ftp, deps=tuple(deps), label=f"tp6_{ntp6}")
            def fev(e, sem, i=i, c4=c4):
                for j in range(4):
                    e.tensor_copy(xn1f[:, c4 * 4 + j, i * 128:(i + 1) * 128], pbf(j))
                return e.drain()
            P.emit("vector", fev, deps=(f"tp6_{ntp6}",), label=f"tp6ev_{ntp6}")
            ntp6 += 1
    P.mark_here("vector", "xn1f_done")

    wv6 = A.alloc_at(pocket, [128, NC, D], BF16)
    def ldwv(e, sem):
        e.dma_start(out=wv6, in_=wqkv1.rearrange("(c p) o -> p c o", p=128)[:, :, 2 * D:3 * D]).then_inc(sem, 16)
    P.emit("gpsimd", ldwv, deps=("p6_gpsimd",), dma=1, label="wv6_loaded")
    v65r = v65.rearrange("p a (h w) -> p a h w", h=16)
    nv6 = 0
    for i in range(NT):
        for hf in range(2):
            b = nv6 % 2
            def f(e, sem, i=i, hf=hf, b=b):
                for kc in range(NC):
                    last = e.matmul(pb[b][:], xn1f[:, kc, i * 128:(i + 1) * 128],
                                    wv6[:, kc, hf * 512:(hf + 1) * 512],
                                    start=(kc == 0), stop=(kc == NC - 1))
                return last
            deps = ["wv6_loaded", "xn1f_done"]
            if nv6 >= 2:
                deps.append(f"v6ev_{nv6-2}")
            P.emit("tensor", f, deps=tuple(deps), label=f"v6mm_{nv6}")
            def fe(e, sem, i=i, hf=hf, b=b):
                e.tensor_copy(v65r[:, i, hf * 8:(hf + 1) * 8, 0:64],
                              pb[b][:].rearrange("p (a b) -> p a b", a=8))
                return e.drain()
            P.emit("vector", fe, deps=(f"v6mm_{nv6}",), label=f"v6ev_{nv6}")
            nv6 += 1
        def fo(e, sem, i=i):
            e.memset(v65r[:, i, :, 64:65], 1.0)
            return e.drain()
        P.emit("vector", fo, deps=(f"v6ev_{nv6-1}",), label=f"v6o_{i}")
    P.mark_here("vector", "v6_done")

    wg6 = A.alloc_at(pocket, [128, NC, 128], BF16)
    wr6 = A.alloc_at(pocket + 512, [128, NC, 128], BF16)
    cosb = A.alloc_at(pocket + 1024, [128, T], F32)
    sinb = A.alloc_at(pocket + 3072, [128, T], F32)
    rp1 = A.alloc_at(pocket + 5120, [128, 512], F32)
    rp2 = A.alloc_at(pocket + 5632, [128, 512], F32)
    nrk = 0
    for qk in range(2):
        dstf = Qf if qk == 0 else Kf
        cos_in = cosq8 if qk == 0 else cosk
        sin_in = sinq8 if qk == 0 else sink
        def ldcs(e, sem, cos_in=cos_in, sin_in=sin_in):
            e.dma_start(out=cosb, in_=cos_in).then_inc(sem, 16)
            e.dma_start(out=sinb, in_=sin_in).then_inc(sem, 16)
        depc = ["p6_gpsimd", f"v6ev_{nv6-1}", "v6_done"] if qk == 0 else [f"rkev_{nrk-1}"]
        P.emit("gpsimd", ldcs, deps=tuple(depc), dma=2, label=f"cs6_{qk}")
        for grp in range(NC):
            def ldwg(e, sem, qk=qk, grp=grp):
                col0 = qk * D + grp * 128
                e.dma_start(out=wg6, in_=wqkv1.rearrange("(c p) o -> p c o", p=128)[:, :, col0:col0 + 128]).then_inc(sem, 16)
            depw = [f"cs6_{qk}"]
            if nrk >= 1:
                deps_guard = f"rkmm_{nrk-1}"
                depw.append(deps_guard)
            P.emit("gpsimd", ldwg, deps=tuple(depw), dma=1, label=f"wg6_{qk}_{grp}")
            def frot(e, sem):
                w3 = wg6.rearrange("p c (a two) -> p c a two", two=2)
                r3 = wr6.rearrange("p c (a two) -> p c a two", two=2)
                e.tensor_scalar_mul(r3[:, :, :, 0:1], w3[:, :, :, 1:2], -1.0)
                e.drain()
                e.tensor_copy(r3[:, :, :, 1:2], w3[:, :, :, 0:1])
                return e.drain()
            P.emit("vector", frot, deps=(f"wg6_{qk}_{grp}",), label=f"rot_{qk}_{grp}")
            for tc in range(4):
                def fmm(e, sem, grp=grp, tc=tc):
                    for kc in range(NC):
                        e.matmul(pb[0][:], wg6[:, kc, :], xn1f[:, kc, tc * 512:(tc + 1) * 512],
                                 start=(kc == 0), stop=(kc == NC - 1))
                    for kc in range(NC):
                        last = e.matmul(pb[1][:], wr6[:, kc, :], xn1f[:, kc, tc * 512:(tc + 1) * 512],
                                        start=(kc == 0), stop=(kc == NC - 1))
                    return last
                depm = [f"rot_{qk}_{grp}", "v6_done", "xn1f_done"]
                if nrk >= 1:
                    depm.append(f"rkev_{nrk-1}")
                P.emit("tensor", fmm, deps=tuple(depm), label=f"rkmm_{nrk}")
                def fev(e, sem, grp=grp, tc=tc, dstf=dstf):
                    sl = slice(tc * 512, (tc + 1) * 512)
                    e.tensor_tensor(rp1, pb[0][:], cosb[:, sl], ALU.mult)
                    e.tensor_tensor(rp2, pb[1][:], sinb[:, sl], ALU.mult)
                    e.drain()
                    e.tensor_tensor(dstf[:, grp, sl], rp1, rp2, ALU.add)
                    return e.drain()
                P.emit("vector", fev, deps=(f"rkmm_{nrk}",), label=f"rkev_{nrk}")
                nrk += 1
    P.mark_here("vector", "qk6_done")

    Qaug = A.alloc_at(pocket, [128, T], BF16)
    Kaug = A.alloc_at(pocket + 1024, [128, T], BF16)
    PT = A.alloc_at(pocket + 2048, [128, T], BF16)
    PT2 = A.alloc_at(pocket + 3072, [128, T], BF16)
    qqrow = A.alloc_at(pocket + 3072, [1, T], F32, rows=1)
    kkrow = A.alloc_at(pocket + 5120, [1, T], F32, rows=1)
    mkk = A.alloc_at(pocket + 7168, [1, 1], F32, rows=1)
    crow = qqrow
    crow_bf = A.alloc_at(pocket + 5120, [1, T], BF16, rows=1)
    ones64 = A.alloc_at(pocket + 7170, [128, 1], BF16)
    trim6 = A.alloc_at(pocket + 7244, [128, 128], F32)
    rl7 = A.alloc_at(pocket + 7372, [128, 8], F32)
    attn_sc7 = A.alloc_at(pocket + 7380, [128, 7, 64], BF16)
    sq6 = PT.bitcast(F32) if False else None

    def ftr6(e, sem):
        e.dma_start(out=trim6, in_=trimask).then_inc(sem, 16)
    P.emit("gpsimd", ftr6, deps=("p6_gpsimd", "qk6_done"), dma=1, label="tri_loaded")
    def fo64(e, sem):
        e.memset(ones64, 1.0)
        return e.drain()
    P.emit("vector", fo64, deps=("qk6_done",), label="ones64")

    nst6 = 0
    npv6 = 0
    for h in range(16):
        grp, hh = h // 2, h % 2
        rb = hh * 64
        def fsq(e, sem, grp=grp, rb=rb):
            e.activation(PT[0:64, :], Qf[rb:rb + 64, grp, :], AF.Square)
            return e.activation(PT[64:128, :], Kf[rb:rb + 64, grp, :], AF.Square)
        dep0 = ["qk6_done"] if h == 0 else [f"aug_{h-1}", f"pv6c_{npv6-1}"]
        P.emit("scalar", fsq, deps=tuple(dep0), label=f"sq6_{h}")
        for tc in range(4):
            def fqq(e, sem, tc=tc):
                e.matmul(pb[6][0:1, 0:512], ones64[0:64, :], PT[0:64, tc * 512:(tc + 1) * 512],
                         start=True, stop=True)
                return e.matmul(pb[7][0:1, 0:512], ones64[64:128, :],
                                PT[64:128, tc * 512:(tc + 1) * 512], start=True, stop=True)
            deps = [f"sq6_{h}", "ones64"]
            if not (h == 0 and tc == 0):
                deps.append(f"qqe_{h}_{tc-1}" if tc > 0 else f"qqe_{h-1}_3")
            P.emit("tensor", fqq, deps=tuple(deps), label=f"qq_{h}_{tc}")
            def fqe(e, sem, tc=tc):
                e.tensor_copy(qqrow[:, tc * 512:(tc + 1) * 512], pb[6][0:1, 0:512])
                e.tensor_copy(kkrow[:, tc * 512:(tc + 1) * 512], pb[7][0:1, 0:512])
                return e.drain()
            P.emit("vector", fqe, deps=(f"qq_{h}_{tc}",), label=f"qqe_{h}_{tc}")
        def fcrow(e, sem):
            e.reduce_max(mkk, kkrow, axis=AX.X)
            return e.drain()
        P.emit("vector", fcrow, deps=(f"qqe_{h}_3",), label=f"mkk_{h}")
        def fcs(e, sem):
            e.activation(mkk, mkk, AF.Sqrt)
            return e.activation(crow, qqrow, AF.Sqrt)
        P.emit("scalar", fcs, deps=(f"mkk_{h}",), label=f"cs_{h}")
        def fcb(e, sem):
            e.tensor_scalar(crow, crow, mkk, None, ALU.mult)
            e.drain()
            e.tensor_scalar(crow, crow, -1.02, -1.0, ALU.mult, ALU.add)
            e.drain()
            e.tensor_copy(crow_bf, crow)
            return e.drain()
        P.emit("vector", fcb, deps=(f"cs_{h}",), label=f"cb_{h}")
        def faug(e, sem, grp=grp, rb=rb):
            e.tensor_copy(Qaug[0:64, :], Qf[rb:rb + 64, grp, :])
            e.tensor_copy(Kaug[0:64, :], Kf[rb:rb + 64, grp, :])
            e.drain()
            e.tensor_copy(Qaug[64:65, :], crow_bf)
            e.memset(Kaug[64:65, :], 1.0)
            return e.drain()
        P.emit("vector", faug, deps=(f"cb_{h}",), label=f"aug_{h}")
        def acc(qt):
            return pb[4 + qt // 7][:, (qt % 7) * 65:(qt % 7) * 65 + 65]
        def emit_evacs(qts):
            nonlocal npv6
            kt_dep = 6 if qts[-1] <= 6 else (13 if qts[-1] <= 13 else 15)
            def fpve(e, sem, qts=qts):
                for j, qt in enumerate(qts):
                    e.reciprocal(rl7[:, j:j + 1], acc(qt)[:, 64:65])
                e.drain()
                for j, qt in enumerate(qts):
                    e.tensor_scalar_mul(attn_sc7[:, j, :], acc(qt)[:, 0:64], rl7[:, j:j + 1])
                return e.drain()
            depe = [f"pvk_{h}_{min(kt_dep, NT-1)}"]
            if npv6 >= 1:
                depe.append(f"pv6t_{npv6-1}")
            P.emit("vector", fpve, deps=tuple(depe), label=f"pv6e_{npv6}")
            def fpvt(e, sem, qts=qts):
                for j in range(len(qts)):
                    last = e.transpose(pb[7][0:64, j * 64:(j + 1) * 64].bitcast(BF16),
                                       attn_sc7[:, j, :], c_identb)
                return last
            dept = [f"pv6e_{npv6}"]
            if npv6 >= 1:
                dept.append(f"pv6c_{npv6-1}")
            P.emit("tensor", fpvt, deps=tuple(dept), label=f"pv6t_{npv6}")
            def fpvc(e, sem, qts=qts, grp=grp, rb=rb):
                for j, qt in enumerate(qts):
                    e.tensor_copy(s1f[rb:rb + 64, grp, qt * 128:(qt + 1) * 128],
                                  pb[7][0:64, j * 64:(j + 1) * 64].bitcast(BF16))
                return e.drain()
            P.emit("vector", fpvc, deps=(f"pv6t_{npv6}",), label=f"pv6c_{npv6}")
            npv6 += 1
        for kt in range(NT):
            qc0 = (kt * 128) // 512
            pts = PT if kt % 2 == 0 else PT2
            for qc in range(qc0, 4):
                b = nst6 % 4
                def fst(e, sem, kt=kt, qc=qc, b=b):
                    return e.matmul(pb[b][:], Kaug[0:65, kt * 128:(kt + 1) * 128],
                                    Qaug[0:65, qc * 512:(qc + 1) * 512], start=True, stop=True)
                deps = [f"aug_{h}"]
                if nst6 >= 4:
                    deps.append(f"stex_{nst6-4}")
                if h == 0 and nst6 < 4:
                    deps.append(f"rkev_{nrk-1}")
                P.emit("tensor", fst, deps=tuple(deps), label=f"st6_{nst6}")
                def fex(e, sem, qc=qc, b=b, pts=pts):
                    return e.activation(pts[:, qc * 512:(qc + 1) * 512], pb[b][:], AF.Exp)
                dex = [f"st6_{nst6}"]
                if kt > 1 and qc == qc0:
                    dex.append(f"pvk_{h}_{kt-2}")
                P.emit("scalar", fex, deps=tuple(dex), label=f"stex_{nst6}")
                nst6 += 1
            def fmsk(e, sem, kt=kt, pts=pts):
                e.tensor_tensor(pts[:, kt * 128:(kt + 1) * 128], pts[:, kt * 128:(kt + 1) * 128],
                                trim6, ALU.mult)
                return e.drain()
            P.emit("vector", fmsk, deps=(f"stex_{nst6-1}", "tri_loaded"), label=f"msk_{h}_{kt}")
            def fpvk(e, sem, h=h, kt=kt, pts=pts):
                for qt in range(kt, NT):
                    last = e.matmul(acc(qt), pts[:, qt * 128:(qt + 1) * 128],
                                    v65r[:, kt, h, :], start=(kt == 0), stop=(kt == qt))
                return last
            deps = [f"msk_{h}_{kt}", "v6_done"]
            if h > 0 and kt == 0:
                deps.append(f"pv6c_{npv6-1}")
            P.emit("tensor", fpvk, deps=tuple(deps), label=f"pvk_{h}_{kt}")
            if kt == 6:
                emit_evacs(list(range(0, 7)))
            elif kt == 13:
                emit_evacs(list(range(7, 14)))
            elif kt == 15:
                emit_evacs([14, 15])
    P.mark_here("vector", "s1f_done")

    wpj = A.alloc_at(pocket, [128, NC, D], BF16)
    g1t = A.alloc_at(pocket + 4096, [128, D], F32)
    sil6 = A.alloc_at(pocket + 5120, [128, 512], F32)
    cmb6s = A.alloc_at(pocket + 5632, [128, 512], F32)
    def ldwp(e, sem):
        e.dma_start(out=wpj, in_=wproj1.rearrange("(c p) o -> p c o", p=128)).then_inc(sem, 16)
    P.emit("gpsimd", ldwp, deps=("s1f_done",), dma=1, label="wpj_loaded")

    ncm6 = 0
    for i in range(NT):
        def fldx2(e, sem, i=i):
            e.dma_start(out=x1_tl, in_=x1_d.rearrange("p (a b) -> p a b", a=NT)[:, i, :]).then_inc(sem, 16)
            e.dma_start(out=g1t, in_=g_d.rearrange("p (a b) -> p a b", a=NT)[:, i, :]).then_inc(sem, 16)
        depl = ["g1_stored", "s1f_done"]
        if i > 0:
            depl.append(f"cmb6_{2*(i-1)+1}")
        P.emit("sync", fldx2, deps=tuple(depl), dma=2, label=f"xg6_{i}")
        for hf in range(2):
            b = (ncm6 % 2)
            def f(e, sem, i=i, hf=hf, b=b):
                for kc in range(NC):
                    last = e.matmul(pb[b][:], s1f[:, kc, i * 128:(i + 1) * 128],
                                    wpj[:, kc, hf * 512:(hf + 1) * 512],
                                    start=(kc == 0), stop=(kc == NC - 1))
                return last
            deps = [f"xg6_{i}", "wpj_loaded"]
            if ncm6 == 0:
                deps.append(f"rkmm_{nrk-1}")
            if ncm6 >= 2:
                deps.append(f"cmb6_{ncm6-2}")
            P.emit("tensor", f, deps=tuple(deps), label=f"s6mm_{ncm6}")
            def fsil(e, sem, i=i, hf=hf):
                return e.activation(sil6, g1t[:, hf * 512:(hf + 1) * 512], AF.Silu)
            P.emit("scalar", fsil, deps=(f"s6mm_{ncm6}",) + (() if ncm6 == 0 else (f"cmb6_{ncm6-1}",)), label=f"sil6_{ncm6}")
            def fcmb(e, sem, i=i, hf=hf, b=b):
                sl = slice(hf * 512, (hf + 1) * 512)
                e.tensor_tensor(cmb6s, pb[b][:], sil6, ALU.mult)
                e.drain()
                e.tensor_tensor(x1_tl[:, sl], x1_tl[:, sl], cmb6s, ALU.add)
                return e.drain()
            P.emit("vector", fcmb, deps=(f"sil6_{ncm6}",), label=f"cmb6_{ncm6}")
            ncm6 += 1
        def fstout(e, sem, i=i):
            e.dma_start(out=out.rearrange("(t p) d -> p t d", p=128)[:, i, :], in_=x1_tl).then_inc(sem, 16)
        P.emit("sync", fstout, deps=(f"cmb6_{ncm6-1}",), dma=1, label=f"out6_{i}")
    P.mark_here("sync", "out_done")

    P.build()
    return nc, es


def _finish_stub(P, out, xn_t):
    def st_out(e, sem):
        e.dma_start(out=out.rearrange("(t p) d -> p t d", p=128), in_=xn_t).then_inc(sem, 16)
    P.emit("sync", st_out, deps=(), dma=1, label="out_stored")



def make_consts():
    ident = np.eye(128, dtype=np.float32)
    bm = np.zeros((128, 3, 128), dtype=np.float32)
    k_idx = np.arange(128)[:, None]
    q_idx = np.arange(128)[None, :]
    for ri, rel in enumerate((-1, 0, 1)):
        bm[:, ri, :] = (np.abs(128 * rel + q_idx - k_idx) <= 3).astype(np.float32)
    return {
        "ident_f": ident,
        "bandmask": np.ascontiguousarray(bm.reshape(128, 384)),
    }


def make_in_map(inputs, core):
    m = {
        "x": np.ascontiguousarray(inputs["L0_attn_x"][core]),
        "w_qkv": inputs["L0_attn_w_qkv"],
        "proj": inputs["L0_attn_proj"],
        "w_out": inputs["L0_attn_w_out"],
        "b_out": inputs["L0_attn_b_out"].reshape(1, -1),
        "a_range": inputs["L0_ffn_adaptive_range"].reshape(1, 1),
    }
    for k in ("cn_w1", "cn_w2", "cn_w3", "gc_w1", "gc_w2"):
        m[k] = inputs[f"L0_ffn_{k}"]
    m["cn_w4"] = np.ascontiguousarray(inputs["L0_ffn_cn_w4"].reshape(2, 128).T)
    m["gc_w3"] = np.ascontiguousarray(inputs["L0_ffn_gc_w3"].reshape(2, 128).T)
    for k in ("cn_b1", "cn_b2", "cn_b3", "gc_b1", "gc_b2"):
        m[k] = np.ascontiguousarray(inputs[f"L0_ffn_{k}"].reshape(-1, 128).T)
    for k in ("cn_b4", "gc_b3"):
        m[k] = inputs[f"L0_ffn_{k}"].reshape(1, 1)
    m["wqkv1"] = inputs["L1_attn_Wqkv"]
    m["wproj1"] = inputs["L1_attn_Wproj"]
    m["gamma"] = inputs["L1_ffn_gamma"].reshape(1, -1)
    m["wqT"] = np.ascontiguousarray(inputs["L1_ffn_wq"].T)
    k = inputs["L1_ffn_keys"].transpose(0, 2, 3, 1).reshape(8, 2, 4, 128, 256)
    m["keysT"] = np.ascontiguousarray(k.transpose(3, 0, 1, 2, 4).reshape(128, 64 * 256))
    m["tab_du"] = np.ascontiguousarray(
        np.concatenate([inputs["L1_ffn_table_down"], inputs["L1_ffn_table_up"]], axis=1))
    dh = 64
    inv = 1.0 / (10000.0 ** (np.arange(0, dh, 2, dtype=np.float64) / dh))
    fr = np.arange(T, dtype=np.float64)[:, None] * inv[None, :]
    cosf = np.repeat(np.cos(fr), 2, axis=-1).T
    sinf = np.repeat(np.sin(fr), 2, axis=-1).T
    m["cosq8"] = np.tile(cosf / 8.0, (2, 1)).astype(np.float32)
    m["sinq8"] = np.tile(sinf / 8.0, (2, 1)).astype(np.float32)
    m["cosk"] = np.tile(cosf, (2, 1)).astype(np.float32)
    m["sink"] = np.tile(sinf, (2, 1)).astype(np.float32)
    kq = np.arange(128)
    m["trimask"] = (kq[:, None] <= kq[None, :]).astype(np.float32)
    m.update(make_consts())
    return m


def kernel(**inputs):
    nc, _es = build_program(nphase=6)
    in_maps = [make_in_map(inputs, c) for c in range(8)]
    res = run_bass_kernel_spmd(nc, in_maps, core_ids=list(range(8)))
    return np.stack([np.asarray(res.results[c]["out"]) for c in range(8)], axis=0)


if __name__ == "__main__":
    nc, _ = build_program(nphase=0)
    print("built phase0 ok")


# version 17 (final)
# speedup vs baseline: 1.0088x; 1.0088x over previous
import os
import numpy as np
import concourse.bass as bass
import concourse.mybir as mybir
from concourse.bass_utils import run_bass_kernel_spmd
from concourse.bass import IndirectOffsetOnAxis
from contextlib import ExitStack
from math import prod

F32 = mybir.dt.float32
BF16 = mybir.dt.bfloat16
U16 = mybir.dt.uint16
I32 = mybir.dt.int32
AF = mybir.ActivationFunctionType
ALU = mybir.AluOpType
AX = mybir.AxisListType

T = 2048
D = 1024
NT = T // 128
NC = D // 128
DTSZ = {F32: 4, BF16: 2, U16: 2, I32: 4}


class Prog:
    ENGS = ("sync", "gpsimd", "vector", "scalar", "tensor")

    def __init__(self, nc, es):
        self.nc = nc
        self.plan = {e: [] for e in self.ENGS}
        self.sem = {e: es.enter_context(nc.semaphore(f"S{e}")) for e in self.ENGS}
        self.cnt = {e: 0 for e in self.ENGS}
        self.marks = {}

    def emit(self, eng, fn, deps=(), label=None, dma=0):
        waits = tuple(self.marks[d] for d in deps)
        if dma:
            self.cnt[eng] += 16 * dma
        elif label:
            self.cnt[eng] += 1
        if label:
            self.marks[label] = (eng, self.cnt[eng])
        self.plan[eng].append((fn, waits, (label is not None and not dma), dma))

    def mark_here(self, eng, label):
        self.marks[label] = (eng, self.cnt[eng])

    def barrier(self, name):
        cur = {e: (e, self.cnt[e]) for e in self.ENGS}
        for e in self.ENGS:
            self.marks[f"__b_{name}_{e}"] = cur[e]
        for e in self.ENGS:
            deps = tuple(f"__b_{name}_{x}" for x in self.ENGS if x != e)
            def f(en, sem):
                return en.nop()
            self.emit(e, f, deps=deps, label=f"{name}_{e}")

    def build(self):
        nc = self.nc
        with nc.Block() as block:
            for eng in self.ENGS:
                def make(eng):
                    def body(e):
                        for fn, waits, inc1, dma in self.plan[eng]:
                            for weng, val in waits:
                                e.wait_ge(self.sem[weng], val)
                            last = fn(e, self.sem[eng])
                            if inc1:
                                assert last is not None, f"label group on {eng} returned None"
                                last.then_inc(self.sem[eng], 1)
                        e.wait_ge(self.sem[eng], self.cnt[eng])
                    return body
                getattr(block, eng)(make(eng))


class Arena:

    def __init__(self, nc, es, words):
        self.t = es.enter_context(nc.sbuf_tensor("arena", [128, words], F32))
        self.words = words
        self.off = 0

    def alloc(self, shape, dt=F32, rows=128):
        assert shape[0] == rows
        elems = prod(shape[1:])
        w = (elems * DTSZ[dt] + 3) // 4
        assert self.off + w <= self.words, f"arena overflow: {self.off + w} > {self.words}"
        ap = self.t[:, self.off:self.off + w]
        self.off += w
        if dt != F32:
            ap = ap.bitcast(dt)
        n = elems * (4 * w == elems * DTSZ[dt] or True)
        ap = ap[:, 0:elems] if ap.shape[1] != elems else ap
        if len(shape) > 2:
            ap = ap.rearrange("p (" + " ".join(f"a{i}" for i in range(len(shape) - 1)) + ") -> p " +
                              " ".join(f"a{i}" for i in range(len(shape) - 1)),
                              **{f"a{i}": shape[i + 1] for i in range(len(shape) - 1)})
        if rows != 128:
            ap = ap[0:rows]
        return ap

    def alloc_at(self, off, shape, dt=F32, rows=128):
        save = self.off
        self.off = off
        ap = self.alloc(shape, dt, rows)
        self.off = max(save, self.off) if False else save
        return ap

    def mark(self):
        return self.off

    def reset(self, m):
        self.off = m


def build_program(nphase=99, debug=False):
    nc = bass.Bass("TRN2", target_bir_lowering=False)
    es = ExitStack()

    def din(name, shape, dt=F32):
        return nc.dram_tensor(name, shape, dt, kind="ExternalInput").ap()

    def dscratch(name, shape, dt=F32):
        kind = "ExternalOutput" if debug else "Internal"
        return nc.dram_tensor(name, shape, dt, kind=kind).ap()

    x_in = din("x", [T, D])
    w_qkv = din("w_qkv", [D, 3 * D])
    proj_in = din("proj", [128, 64])
    w_out = din("w_out", [D, D])
    b_out = din("b_out", [1, D])
    cn_w1 = din("cn_w1", [D, D]); cn_b1 = din("cn_b1", [128, 8])
    cn_w2 = din("cn_w2", [D, 512]); cn_b2 = din("cn_b2", [128, 4])
    cn_w3 = din("cn_w3", [512, 256]); cn_b3 = din("cn_b3", [128, 2])
    cn_w4 = din("cn_w4", [128, 2]); cn_b4 = din("cn_b4", [1, 1])
    gc_w1 = din("gc_w1", [D, 512]); gc_b1 = din("gc_b1", [128, 4])
    gc_w2 = din("gc_w2", [512, 256]); gc_b2 = din("gc_b2", [128, 2])
    gc_w3 = din("gc_w3", [128, 2]); gc_b3 = din("gc_b3", [1, 1])
    a_range = din("a_range", [1, 1])
    ident_f = din("ident_f", [128, 128])
    bandmask = din("bandmask", [128, 3 * 128])
    wqkv1 = din("wqkv1", [D, 3 * D])
    wproj1 = din("wproj1", [D, D])
    gamma = din("gamma", [1, D])
    wqT = din("wqT", [D, 8192])
    keysT = din("keysT", [128, 64 * 256])
    tab_du = din("tab_du", [65536, 2 * D])
    cosq8 = din("cosq8", [128, T]); sinq8 = din("sinq8", [128, T])
    cosk = din("cosk", [128, T]); sink = din("sink", [128, T])
    trimask = din("trimask", [128, 128])

    out = nc.dram_tensor("out", [T, D], F32, kind="ExternalOutput").ap()

    xn_d = dscratch("xn_d", [128, NT * D])
    xnbf_f_d = dscratch("xnbf_f_d", [128, NC * T], U16)
    v_d = dscratch("v_d", [128, NT * D], U16)
    attnf_d = dscratch("attnf_d", [128, NC * T], U16)
    g_d = dscratch("g_d", [128, NT * D])
    x1_d = dscratch("x1_d", [128, NT * D])
    gf_d = dscratch("gf_d", [1, 1])
    tabdu_bf = dscratch("tabdu_bf", [65536, 2 * D], U16)
    dbg1 = dscratch("dbg1", [128, 16 * 16 * 16])
    dbg2 = dscratch("dbg2", [128, 16 * 16 * 16], U16)
    dbg3 = dscratch("dbg3", [128, 16 * 8 * 16])
    dbg4 = dscratch("dbg4", [128, 16 * 8 * 16], I32)

    P = Prog(nc, es)
    A = Arena(nc, es, 51000)

    pb = [es.enter_context(nc.psum_tensor(f"pb{i}", [128, 512], F32)) for i in range(8)]
    def pbf(i):
        return pb[i][:, 0:64].bitcast(BF16)

    c_identb = A.alloc([128, 128], BF16)
    c_identf = A.alloc([128, 128], F32)
    gf_sb = A.alloc([1, 1], F32, rows=1)
    ln1_rstd = A.alloc([128, NT], F32)
    ln1_nmr = A.alloc([128, NT], F32)
    base_min = A.mark()
    xn_t = A.alloc([128, NT, D], F32)
    m_xn = A.mark()
    fmaj_bf = A.alloc([128, NC, T], BF16)
    cast_scr = A.alloc([128, D], BF16)
    x_tile = A.alloc([128, D], F32)

    def ld_consts(e, sem):
        e.dma_start(out=c_identf, in_=ident_f).then_inc(sem, 16)
    P.emit("sync", ld_consts, dma=1, label="consts0")
    def cast_ident(e, sem):
        e.tensor_copy(c_identb, c_identf)
        return e.drain()
    P.emit("vector", cast_ident, deps=("consts0",), label="consts")

    ph_base = A.mark()

    st_sum = A.alloc([128, NT]); st_sq = A.alloc([128, NT])
    st_mean = A.alloc([128, NT]); st_var = A.alloc([128, NT])
    st_rstd = A.alloc([128, NT]); st_nmr = A.alloc([128, NT])
    sq_scr = A.alloc([128, D])

    for i in range(NT):
        def fx(e, sem, i=i):
            e.dma_start(out=x_tile, in_=x_in.rearrange("(t p) d -> p t d", p=128)[:, i, :]).then_inc(sem, 16)
        P.emit("sync", fx, deps=() if i == 0 else (f"sq_{i-1}",), dma=1, label=f"xld_{i}")
        def f(e, sem, i=i):
            return e.activation(sq_scr, x_tile, AF.Square)
        P.emit("scalar", f, deps=(f"xld_{i}",) + (() if i == 0 else (f"sq_{i-1}",)), label=f"sqdone_{i}")
        def f2(e, sem, i=i):
            e.reduce_sum(st_sum[:, i:i + 1], x_tile, axis=AX.X)
            e.reduce_sum(st_sq[:, i:i + 1], sq_scr, axis=AX.X)
            return e.tensor_copy(xn_t[:, i, :], x_tile)
        P.emit("vector", f2, deps=(f"sqdone_{i}",), label=f"sq_{i}")

    def stats(e, sem):
        e.tensor_scalar_mul(st_mean, st_sum, 1.0 / D)
        e.drain()
        e.tensor_scalar_mul(st_var, st_sq, 1.0 / D)
        e.drain()
        e.tensor_tensor(st_nmr, st_mean, st_mean, ALU.mult)
        e.drain()
        e.tensor_sub(st_var, st_var, st_nmr)
        e.drain()
        return e.tensor_scalar_add(st_var, st_var, 1e-5)
    P.emit("vector", stats, deps=(f"sq_{NT-1}",), label="var_done")
    def rstd1(e, sem):
        return e.activation(st_rstd, st_var, AF.Sqrt)
    P.emit("scalar", rstd1, deps=("var_done",), label="sqrt_done")
    def rstd2(e, sem):
        e.reciprocal(st_rstd, st_rstd)
        e.drain()
        e.tensor_tensor(st_nmr, st_mean, st_rstd, ALU.mult)
        e.drain()
        return e.tensor_scalar_mul(st_nmr, st_nmr, -1.0)
    P.emit("vector", rstd2, deps=("sqrt_done",), label="rstd_done")

    ntp = 0
    for i in range(NT):
        def f(e, sem, i=i):
            e.tensor_scalar(xn_t[:, i, :], xn_t[:, i, :], st_rstd[:, i:i + 1],
                            st_nmr[:, i:i + 1], ALU.mult, ALU.add)
            e.drain()
            e.tensor_copy(cast_scr, xn_t[:, i, :])
            return e.drain()
        P.emit("vector", f, deps=("rstd_done",) if i == 0 else (f"tpev_{ntp-1}",), label=f"xn_{i}")
        TPB0 = (2, 3, 6, 7)
        for c in range(NC):
            b = TPB0[ntp % 4]
            def ftp(e, sem, c=c, b=b):
                return e.transpose(pbf(b), cast_scr[:, c * 128:(c + 1) * 128], c_identb)
            deps = ["consts", f"xn_{i}"]
            if ntp >= 4:
                deps.append(f"tpev_{ntp-4}")
            P.emit("tensor", ftp, deps=tuple(deps), label=f"tp_{ntp}")
            def fev(e, sem, i=i, c=c, b=b):
                return e.tensor_copy(fmaj_bf[:, c, i * 128:(i + 1) * 128], pbf(b))
            P.emit("vector", fev, deps=(f"tp_{ntp}",), label=f"tpev_{ntp}")
            ntp += 1
    P.mark_here("vector", "xnbf_f_done")
    P.mark_here("vector", "xn_all")

    def st_xn(e, sem):
        e.dma_start(out=xn_d, in_=xn_t.rearrange("p a b -> p (a b)")).then_inc(sem, 16)
        e.dma_start(out=xnbf_f_d, in_=fmaj_bf.rearrange("p a b -> p (a b)").bitcast(U16)).then_inc(sem, 16)
    P.emit("sync", st_xn, deps=("xn_all", "xnbf_f_done"), dma=2, label="xn_stored")

    gcw1_bf = A.alloc([128, NC, 512], BF16)
    gcw2_bf = A.alloc([128, 4, 256], BF16)
    gcw3_bf = A.alloc([128, 2, 1], BF16)
    gcb1_t = A.alloc([128, 4]); gcb2_t = A.alloc([128, 2]); gcb3_t = A.alloc([1, 1], rows=1)
    gc_bf = A.alloc([128, NC], BF16)
    gc_col = A.alloc([128, NC])
    g1_col = A.alloc([128, 4]); g1_bf = A.alloc([128, 4], BF16)
    g2_col = A.alloc([128, 2]); g2_bf = A.alloc([128, 2], BF16)

    def ld_gcw(e, sem):
        e.dma_start(out=gcw1_bf, in_=gc_w1.rearrange("(c p) o -> p c o", p=128)).then_inc(sem, 16)
        e.dma_start(out=gcw2_bf, in_=gc_w2.rearrange("(c p) o -> p c o", p=128)).then_inc(sem, 16)
        e.dma_start(out=gcw3_bf, in_=gc_w3.rearrange("p (c o) -> p c o", o=1)).then_inc(sem, 16)
        e.dma_start(out=gcb1_t, in_=gc_b1).then_inc(sem, 16)
        e.dma_start(out=gcb2_t, in_=gc_b2).then_inc(sem, 16)
        e.dma_start(out=gcb3_t, in_=gc_b3).then_inc(sem, 16)
    P.emit("gpsimd", ld_gcw, dma=6, label="gcw_loaded")

    def gc_mean(e, sem):
        for c in range(NC):
            e.reduce_sum(gc_col[:, c:c + 1], fmaj_bf[:, c, :], axis=AX.X)
        e.drain()
        e.tensor_scalar_mul(gc_col, gc_col, 1.0 / T)
        e.drain()
        e.tensor_copy(gc_bf, gc_col)
        return e.drain()
    P.emit("vector", gc_mean, deps=("xnbf_f_done",), label="gc_done")

    p_gc = pb[5][:, 0:4]
    p_gf = pb[5][0:1, 4:5]
    for oc in range(4):
        def f(e, sem, oc=oc):
            for kc in range(NC):
                last = e.matmul(p_gc[:, oc:oc + 1], gcw1_bf[:, kc, oc * 128:(oc + 1) * 128],
                                gc_bf[:, kc:kc + 1], start=(kc == 0), stop=(kc == NC - 1))
            return last
        P.emit("tensor", f, deps=("gcw_loaded", "gc_done") if oc == 0 else (f"gfh1_{oc-1}",), label=f"gfh1_{oc}")
        def fe(e, sem, oc=oc):
            return e.activation(g1_col[:, oc:oc + 1], p_gc[:, oc:oc + 1], AF.Gelu,
                                bias=gcb1_t[:, oc:oc + 1])
        P.emit("scalar", fe, deps=(f"gfh1_{oc}",), label=f"gfh1e_{oc}")
    def f2c(e, sem):
        e.tensor_copy(g1_bf, g1_col)
        return e.drain()
    P.emit("vector", f2c, deps=tuple(f"gfh1e_{oc}" for oc in range(4)), label="g1bf")
    for oc in range(2):
        def f(e, sem, oc=oc):
            for kc in range(4):
                last = e.matmul(p_gc[:, oc:oc + 1], gcw2_bf[:, kc, oc * 128:(oc + 1) * 128],
                                g1_bf[:, kc:kc + 1], start=(kc == 0), stop=(kc == 3))
            return last
        P.emit("tensor", f, deps=("g1bf",) if oc == 0 else (f"gfh2_{oc-1}",), label=f"gfh2_{oc}")
        def fe(e, sem, oc=oc):
            return e.activation(g2_col[:, oc:oc + 1], p_gc[:, oc:oc + 1], AF.Gelu,
                                bias=gcb2_t[:, oc:oc + 1])
        P.emit("scalar", fe, deps=(f"gfh2_{oc}",), label=f"gfh2e_{oc}")
    def f3c(e, sem):
        e.tensor_copy(g2_bf, g2_col)
        return e.drain()
    P.emit("vector", f3c, deps=("gfh2e_0", "gfh2e_1"), label="g2bf")
    def f4(e, sem):
        for kc in range(2):
            last = e.matmul(p_gf, gcw3_bf[0:128, kc, 0:1], g2_bf[:, kc:kc + 1],
                            start=(kc == 0), stop=(kc == 1))
        return last
    P.emit("tensor", f4, deps=("g2bf",), label="gf_mm")
    def f4e(e, sem):
        return e.activation(gf_sb, p_gf, AF.Sigmoid, bias=gcb3_t)
    P.emit("scalar", f4e, deps=("gf_mm",), label="gf_done")
    def st_gf(e, sem):
        e.dma_start(out=gf_d, in_=gf_sb).then_inc(sem, 16)
    P.emit("sync", st_gf, deps=("gf_done",), dma=1, label="gf_stored")

    if nphase < 1:
        _finish_stub(P, out, xn_t)
        P.build()
        return nc, es

    P.barrier("p1")
    A.reset(ph_base)

    c_band = A.alloc([128, 3 * 128], F32)
    proj_bf = A.alloc([128, 64], BF16)
    def ld_p1a(e, sem):
        e.dma_start(out=c_band, in_=bandmask).then_inc(sem, 16)
        e.dma_start(out=proj_bf, in_=proj_in).then_inc(sem, 16)
        e.dma_start(out=wv_bf, in_=w_qkv.rearrange("(c p) o -> p c o", p=128)[:, :, 2 * D:3 * D]).then_inc(sem, 16)
    P.emit("gpsimd", ld_p1a, deps=("p1_gpsimd",), dma=3, label="p1w_loaded")

    v_base = A.mark()
    wv_bf = A.alloc([128, NC, D], BF16)
    v_tile = A.alloc([128, D], BF16)
    nv = 0
    for i in range(NT):
        for hf in range(2):
            b = nv % 2
            def f(e, sem, i=i, hf=hf, b=b):
                for kc in range(NC):
                    last = e.matmul(pb[b][:], fmaj_bf[:, kc, i * 128:(i + 1) * 128],
                                    wv_bf[:, kc, hf * 512:(hf + 1) * 512],
                                    start=(kc == 0), stop=(kc == NC - 1))
                return last
            deps = ["p1w_loaded", "p1_tensor"]
            if nv >= 2:
                deps.append(f"vev_{nv-2}")
            P.emit("tensor", f, deps=tuple(deps), label=f"vmm_{nv}")
            def fe(e, sem, i=i, hf=hf, b=b):
                return e.tensor_copy(v_tile[:, hf * 512:(hf + 1) * 512], pb[b][:])
            depv = [f"vmm_{nv}"]
            if hf == 0 and i > 0:
                depv.append(f"vst_{i-1}")
            P.emit("vector", fe, deps=tuple(depv), label=f"vev_{nv}")
            nv += 1
        def fst(e, sem, i=i):
            e.dma_start(out=v_d.rearrange("p (t d) -> p t d", t=NT)[:, i, :], in_=v_tile.bitcast(U16)).then_inc(sem, 16)
        P.emit("sync", fst, deps=(f"vev_{nv-1}",), dma=1, label=f"vst_{i}")
        if i < NT - 1:
            P.marks[f"vwar_{i}"] = P.marks[f"vst_{i}"]
    for i in range(NT - 1):
        pass

    A.reset(v_base)
    wqk_bf = A.alloc([128, NC, 4 * 128], BF16)
    qk_bf = A.alloc([128, 2, T], BF16)
    qphi = A.alloc([128, T], F32)
    kphi = A.alloc([128, T], F32)
    qphi_bf = A.alloc([128, T], BF16)
    kphi_bf = A.alloc([128, T], BF16)
    ksum = A.alloc([128, T], F32)
    zrow = A.alloc([128, T], F32, rows=128)
    invzT = A.alloc([128, NT, 2], F32)
    phi_min = A.alloc([128, 512], F32)
    phi_max = A.alloc([128, 512], F32)
    phi_exp = A.alloc([128, 512], F32)
    stb = A.alloc([128, 3, 384], BF16)
    attn_sc = A.alloc([128, 3, 128], BF16)
    attn_head = A.alloc([128, 2, T], BF16)
    vh_bf = A.alloc([128, 2, NT, 128], BF16)
    blk1 = A.alloc([128, 2], F32)

    def fb1(e, sem):
        e.memset(blk1, 0.0)
        e.drain()
        e.memset(blk1[0:64, 0:1], 1.0)
        e.drain()
        e.memset(blk1[64:128, 1:2], 1.0)
        return e.drain()
    P.emit("vector", fb1, deps=("p1_vector",), label="blk1_done")

    nqk = 0
    nst = 0
    npv = 0
    ntz = 0
    for ph in range(4):
        def ldw(e, sem, ph=ph):
            w3 = w_qkv.rearrange("(c p) o -> p c o", p=128)
            e.dma_start(out=wqk_bf[:, :, 0:256], in_=w3[:, :, ph * 256:(ph + 1) * 256]).then_inc(sem, 16)
            e.dma_start(out=wqk_bf[:, :, 256:512], in_=w3[:, :, D + ph * 256:D + (ph + 1) * 256]).then_inc(sem, 16)
        deps = ("p1_gpsimd", f"vst_{NT-1}") if ph == 0 else (f"phibf_{ph-1}",)
        P.emit("gpsimd", ldw, deps=deps, dma=2, label=f"wqk_{ph}")

        for qk in range(2):
            for hh in range(2):
                for tc in range(4):
                    b = nqk % 2
                    def f(e, sem, qk=qk, hh=hh, tc=tc, b=b):
                        for kc in range(NC):
                            last = e.matmul(pb[b][:], wqk_bf[:, kc, qk * 256 + hh * 128: qk * 256 + (hh + 1) * 128],
                                            fmaj_bf[:, kc, tc * 512:(tc + 1) * 512],
                                            start=(kc == 0), stop=(kc == NC - 1))
                        return last
                    deps = [f"wqk_{ph}"]
                    if nqk == 0:
                        deps.append(f"vev_{nv-1}")
                    if nqk >= 2:
                        deps.append(f"qkpost_{nqk-2}")
                    if npv >= 1 and qk == 0 and hh == 0 and tc == 0:
                        deps.append(f"pvev_{npv-1}")
                    P.emit("tensor", f, deps=tuple(deps), label=f"qkmm_{nqk}")
                    def fe(e, sem, hh=hh, tc=tc, b=b):
                        return e.tensor_copy(qk_bf[:, hh, tc * 512:(tc + 1) * 512], pb[b][:])
                    P.emit("vector", fe, deps=(f"qkmm_{nqk}",), label=f"qkpost_{nqk}")
                    nqk += 1
            phit = qphi if qk == 0 else kphi
            for tc in range(4):
                b = nqk % 2
                def f(e, sem, tc=tc, b=b):
                    e.matmul(pb[b][0:64, :], proj_bf[:, 0:64], qk_bf[:, 0, tc * 512:(tc + 1) * 512],
                             start=True, stop=True)
                    return e.matmul(pb[b][64:128, :], proj_bf[:, 0:64], qk_bf[:, 1, tc * 512:(tc + 1) * 512],
                                    start=True, stop=True)
                deps = [f"qkpost_{nqk-1}"]
                if nqk >= 2:
                    deps.append(f"qkpost_{nqk-2}")
                P.emit("tensor", f, deps=tuple(deps), label=f"qkmm_{nqk}")
                def fel(e, sem, b=b):
                    e.tensor_scalar_min(phi_min, pb[b][:], 0.0)
                    e.tensor_scalar_max(phi_max, pb[b][:], 0.0)
                    return e.drain()
                P.emit("vector", fel, deps=(f"qkmm_{nqk}",), label=f"qkpost_{nqk}_a")
                def fes(e, sem):
                    return e.activation(phi_exp, phi_min, AF.Exp)
                P.emit("scalar", fes, deps=(f"qkpost_{nqk}_a",), label=f"qkpost_{nqk}_b")
                def fea(e, sem, phit=phit, tc=tc):
                    e.tensor_add(phit[:, tc * 512:(tc + 1) * 512], phi_exp, phi_max)
                    return e.drain()
                P.emit("vector", fea, deps=(f"qkpost_{nqk}_b",), label=f"qkpost_{nqk}")
                nqk += 1
        def fcast(e, sem):
            e.tensor_copy(qphi_bf, qphi)
            e.tensor_copy(kphi_bf, kphi)
            return e.drain()
        P.emit("vector", fcast, deps=(f"qkpost_{nqk-1}",), label=f"phibf_{ph}")

        def fks(e, sem):
            e.tensor_copy(ksum, kphi)
            e.drain()
            for o in (-3, -2, -1, 1, 2, 3):
                if o > 0:
                    e.tensor_add(ksum[:, 0:T - o], ksum[:, 0:T - o], kphi[:, o:T])
                else:
                    e.tensor_add(ksum[:, -o:T], ksum[:, -o:T], kphi[:, 0:T + o])
                e.drain()
            e.tensor_tensor(ksum, ksum, qphi, ALU.mult)
            return e.drain()
        P.emit("vector", fks, deps=(f"phibf_{ph}",), label=f"ks_{ph}")

        for tc in range(4):
            def fz(e, sem, tc=tc):
                return e.matmul(pb[4][0:2, :], blk1, ksum[:, tc * 512:(tc + 1) * 512],
                                start=True, stop=True)
            deps = ["blk1_done", f"ks_{ph}"]
            if not (ph == 0 and tc == 0):
                deps.append(f"zev_{ph}_{tc-1}" if tc > 0 else f"zev_{ph-1}_3")
            if ph == 0 and tc == 0:
                deps.append("gf_done")
            P.emit("tensor", fz, deps=tuple(deps), label=f"zmm_{ph}_{tc}")
            def fzev(e, sem, tc=tc):
                return e.tensor_copy(zrow[0:2, tc * 512:(tc + 1) * 512], pb[4][0:2, :])
            P.emit("vector", fzev, deps=(f"zmm_{ph}_{tc}",), label=f"zev_{ph}_{tc}")

        def finvz(e, sem):
            e.tensor_scalar_add(zrow[0:2, :], zrow[0:2, :], 1e-8)
            e.drain()
            e.reciprocal(zrow[0:2, :], zrow[0:2, :])
            return e.drain()
        P.emit("vector", finvz, deps=(f"zev_{ph}_3",), label=f"invz_{ph}")

        for qt in range(NT):
            def ftz(e, sem, qt=qt):
                return e.transpose(pb[4][0:128, 256:258], zrow[0:2, qt * 128:(qt + 1) * 128], c_identf[0:2, 0:2])
            deps = [f"invz_{ph}"]
            if ntz >= 1:
                deps.append(f"tzev_{ntz-1}")
            P.emit("tensor", ftz, deps=tuple(deps), label=f"tz_{ntz}")
            def ftzev(e, sem, qt=qt):
                return e.tensor_copy(invzT[:, qt, :], pb[4][0:128, 256:258])
            P.emit("vector", ftzev, deps=(f"tz_{ntz}",), label=f"tzev_{ntz}")
            ntz += 1
        P.mark_here("vector", f"invzT_{ph}")

        def ldvh(e, sem, ph=ph):
            v3 = v_d.rearrange("p (t d) -> p t d", t=NT)
            e.dma_start(out=vh_bf[:, 0, :, :].bitcast(U16), in_=v3[:, :, (2 * ph) * 128:(2 * ph + 1) * 128]).then_inc(sem, 16)
            e.dma_start(out=vh_bf[:, 1, :, :].bitcast(U16), in_=v3[:, :, (2 * ph + 1) * 128:(2 * ph + 2) * 128]).then_inc(sem, 16)
        deps = [f"vst_{NT-1}"]
        if ph > 0:
            deps.append(f"pvdone_{ph-1}")
        P.emit("gpsimd", ldvh, deps=tuple(deps), dma=2, label=f"vh_{ph}")

        for hh in range(2):
            rb = hh * 64
            def emit_st(kt):
                nonlocal nst
                qts = [q for q in (kt - 1, kt, kt + 1) if 0 <= q < NT]
                b = nst % 2
                def fst(e, sem, rb=rb, kt=kt, qts=qts, b=b):
                    first = True
                    for q in qts:
                        reg = q - kt + 1
                        last = e.matmul(pb[2 + b][:, 0:384][:, reg * 128:(reg + 1) * 128],
                                        kphi_bf[rb:rb + 64, kt * 128:(kt + 1) * 128],
                                        qphi_bf[rb:rb + 64, q * 128:(q + 1) * 128],
                                        start=first, stop=(q == qts[-1]))
                        first = False
                    return last
                deps = [f"phibf_{ph}", f"invzT_{ph}"]
                if nst >= 2:
                    deps.append(f"stev_{nst-2}")
                if npv >= 1 and kt >= 3:
                    deps.append(f"pvmm_{npv-1}")
                P.emit("tensor", fst, deps=tuple(deps), label=f"stmm_{nst}")
                def fstev(e, sem, kt=kt, qts=qts, b=b):
                    r0 = qts[0] - kt + 1
                    r1 = qts[-1] - kt + 2
                    return e.tensor_tensor(stb[:, kt % 3, r0 * 128:r1 * 128],
                                           pb[2 + b][:, 0:384][:, r0 * 128:r1 * 128],
                                           c_band[:, r0 * 128:r1 * 128], ALU.mult)
                P.emit("vector", fstev, deps=(f"stmm_{nst}",), label=f"stev_{nst}")
                nst += 1

            TPB1 = (5, 6, 7)
            def emit_pv(qt):
                nonlocal npv
                kts = [k for k in (qt - 1, qt, qt + 1) if 0 <= k < NT]
                b = npv % 2
                def fpv(e, sem, hh=hh, qt=qt, kts=kts, b=b):
                    first = True
                    for k in kts:
                        reg = qt - k + 1
                        last = e.matmul(pb[b][:, 0:128], stb[:, k % 3, reg * 128:(reg + 1) * 128],
                                        vh_bf[:, hh, k, :],
                                        start=first, stop=(k == kts[-1]))
                        first = False
                    return last
                deps = [f"stev_{nst-1}", f"vh_{ph}", f"qkpost_{nqk-1}"]
                if npv >= 2:
                    deps.append(f"pvev_{npv-2}")
                P.emit("tensor", fpv, deps=tuple(deps), label=f"pvmm_{npv}")
                bs = npv % 3
                def fpvev(e, sem, hh=hh, qt=qt, b=b, bs=bs):
                    e.tensor_scalar_mul(attn_sc[:, bs, :], pb[b][:, 0:128], invzT[:, qt, hh:hh + 1])
                    return e.drain()
                depv = [f"pvmm_{npv}"]
                if npv >= 3:
                    depv.append(f"pvtp_{npv-3}")
                P.emit("vector", fpvev, deps=tuple(depv), label=f"pvev_{npv}")
                def fpvtp(e, sem, bs=bs):
                    return e.transpose(pbf(TPB1[bs]), attn_sc[:, bs, :], c_identb)
                dept = [f"pvev_{npv}"]
                if npv >= 3:
                    dept.append(f"pvtev_{npv-3}")
                if npv <= 2:
                    dept.append("gf_stored")
                P.emit("tensor", fpvtp, deps=tuple(dept), label=f"pvtp_{npv}")
                def fpvtev(e, sem, hh=hh, qt=qt, bs=bs):
                    return e.tensor_copy(attn_head[:, hh, qt * 128:(qt + 1) * 128], pbf(TPB1[bs]))
                P.emit("vector", fpvtev, deps=(f"pvtp_{npv}",), label=f"pvtev_{npv}")
                npv += 1

            for kt in range(NT):
                emit_st(kt)
                if kt >= 1:
                    emit_pv(kt - 1)
            emit_pv(NT - 1)
        P.mark_here("vector", f"pvdone_{ph}")
        def sthead(e, sem, ph=ph):
            a3 = attnf_d.rearrange("p (c t) -> p c t", c=NC)
            e.dma_start(out=a3[:, 2 * ph, :], in_=attn_head[:, 0, :].bitcast(U16)).then_inc(sem, 16)
            e.dma_start(out=a3[:, 2 * ph + 1, :], in_=attn_head[:, 1, :].bitcast(U16)).then_inc(sem, 16)
        P.emit("sync", sthead, deps=(f"pvdone_{ph}",), dma=2, label=f"ah_{ph}")
    P.mark_here("sync", "attn_stored")

    if nphase < 2:
        _finish_stub(P, out, xn_t)
        P.build()
        return nc, es

    P.barrier("p2")
    A.reset(ph_base)

    cnw1_bf = A.alloc([128, NC, D], BF16)
    cnw2_bf = A.alloc([128, NC, 512], BF16)
    cnw3_bf = A.alloc([128, 4, 256], BF16)
    cnw4_bf = A.alloc([128, 2, 1], BF16)
    cnb1_t = A.alloc([128, NC]); cnb2_t = A.alloc([128, 4])
    cnb3_t = A.alloc([128, 2]); cnb4_t = A.alloc([1, 1], rows=1)
    ar_sb = A.alloc([1, 1], rows=1)
    c0_sb = A.alloc([1, 1], rows=1); c1_sb = A.alloc([1, 1], rows=1)
    tt1 = A.alloc([1, 1], rows=1)
    h1_bf = A.alloc([128, NC, T], BF16)
    h2_bf = A.alloc([128, 4, T], BF16)
    h3_bf = A.alloc([128, 2, T], BF16)
    cp_row = A.alloc([1, T], rows=1)
    cpT = A.alloc([128, NT]); cpTn = A.alloc([128, NT])
    tw_tmp = A.alloc([128, 2, 256])

    def ld_cnw(e, sem):
        e.dma_start(out=cnw1_bf, in_=cn_w1.rearrange("(c p) o -> p c o", p=128)).then_inc(sem, 16)
        e.dma_start(out=cnw2_bf, in_=cn_w2.rearrange("(c p) o -> p c o", p=128)).then_inc(sem, 16)
        e.dma_start(out=cnw3_bf, in_=cn_w3.rearrange("(c p) o -> p c o", p=128)).then_inc(sem, 16)
        e.dma_start(out=cnw4_bf, in_=cn_w4.rearrange("p (c o) -> p c o", o=1)).then_inc(sem, 16)
        e.dma_start(out=cnb1_t, in_=cn_b1).then_inc(sem, 16)
        e.dma_start(out=cnb2_t, in_=cn_b2).then_inc(sem, 16)
        e.dma_start(out=cnb3_t, in_=cn_b3).then_inc(sem, 16)
        e.dma_start(out=cnb4_t, in_=cn_b4).then_inc(sem, 16)
        e.dma_start(out=ar_sb, in_=a_range).then_inc(sem, 16)
    P.emit("gpsimd", ld_cnw, deps=("p2_gpsimd",), dma=9, label="cnw_loaded")

    def fcc(e, sem):
        e.tensor_scalar_mul(tt1, gf_sb, 0.7)
        e.drain()
        e.tensor_scalar_add(tt1, tt1, -0.5)
        e.drain()
        e.tensor_tensor(tt1, tt1, ar_sb, ALU.mult)
        e.drain()
        e.tensor_scalar_mul(tt1, tt1, 2.0)
        e.drain()
        e.tensor_scalar_add(c0_sb, tt1, 0.1)
        e.drain()
        e.tensor_scalar_mul(c1_sb, ar_sb, 0.6)
        return e.drain()
    P.emit("vector", fcc, deps=("gf_done", "cnw_loaded", "p2_vector"), label="cc_done")

    nmm = 0
    nftp = 0
    for cyc in range(3):
        for i in range(NT):
            def fc(e, sem, i=i, cyc=cyc):
                e.tensor_copy(cast_scr, xn_t[:, i, :])
                return e.drain()
            dep0 = ["cc_done"] if (cyc == 0 and i == 0) else [f"ftpev_{nftp-1}"]
            if cyc > 0 and i == 0:
                dep0 = [f"tw_done_{cyc-1}"]
            P.emit("vector", fc, deps=tuple(dep0), label=f"stc_{cyc}_{i}")
            for c in range(NC):
                b = nftp % 4
                def ftp(e, sem, c=c, b=b):
                    return e.transpose(pbf(b), cast_scr[:, c * 128:(c + 1) * 128], c_identb)
                deps = [f"stc_{cyc}_{i}"]
                if nftp >= 4:
                    deps.append(f"ftpev_{nftp-4}")
                if nftp < 4 and cyc == 0:
                    deps += [f"tzev_{ntz-1}", f"pvtev_{npv-1}", f"qkpost_{nqk-1}", f"stev_{nst-1}"]
                P.emit("tensor", ftp, deps=tuple(deps), label=f"ftp_{nftp}")
                def fev(e, sem, i=i, c=c, b=b):
                    return e.tensor_copy(fmaj_bf[:, c, i * 128:(i + 1) * 128], pbf(b))
                P.emit("vector", fev, deps=(f"ftp_{nftp}",), label=f"ftpev_{nftp}")
                nftp += 1
        last_tp = f"ftpev_{nftp-1}"

        def chain(win, kcs, ocs, src, dst, bias_t, prev):
            nonlocal nmm
            for oc in range(ocs):
                for tc in range(4):
                    b = nmm % 4
                    def f(e, sem, oc=oc, tc=tc, b=b, win=win, kcs=kcs, src=src):
                        for kc in range(kcs):
                            last = e.matmul(pb[b][:], win[:, kc, oc * 128:(oc + 1) * 128],
                                            src[:, kc, tc * 512:(tc + 1) * 512],
                                            start=(kc == 0), stop=(kc == kcs - 1))
                        return last
                    deps = list(prev) if (oc == 0 and tc == 0) else []
                    if nmm < 4:
                        deps.append(f"ftpev_{nftp-1}" if False else f"mme_{nmm-2}") if nmm >= 2 else None
                    if nmm >= 4:
                        deps.append(f"mme_{nmm-4}")
                    P.emit("tensor", f, deps=tuple(deps), label=f"mm_{nmm}")
                    def fe(e, sem, oc=oc, tc=tc, b=b, dst=dst, bias_t=bias_t):
                        return e.activation(dst[:, oc, tc * 512:(tc + 1) * 512], pb[b][:],
                                            AF.Gelu, bias=bias_t[:, oc:oc + 1])
                    P.emit("scalar", fe, deps=(f"mm_{nmm}",), label=f"mme_{nmm}")
                    nmm += 1
            return f"mme_{nmm-1}"

        prev0 = ["cnw_loaded", last_tp]
        if cyc == 0:
            prev0 += [f"pvtev_{npv-1}", f"qkpost_{nqk-1}"]
        l1 = chain(cnw1_bf, NC, NC, fmaj_bf, h1_bf, cnb1_t, tuple(prev0))
        l2 = chain(cnw2_bf, NC, 4, h1_bf, h2_bf, cnb2_t, (l1,))
        l3 = chain(cnw3_bf, 4, 2, h2_bf, h3_bf, cnb3_t, (l2,))
        for tc in range(4):
            def f(e, sem, tc=tc):
                for kc in range(2):
                    last = e.matmul(pb[4][0:1, :], cnw4_bf[0:128, kc, 0:1],
                                    h3_bf[:, kc, tc * 512:(tc + 1) * 512],
                                    start=(kc == 0), stop=(kc == 1))
                return last
            deps = [l3] if tc == 0 else [f"t4e_{cyc}_{tc-1}"]
            if cyc == 0 and tc == 0:
                deps.append(f"tzev_{ntz-1}")
            P.emit("tensor", f, deps=tuple(deps), label=f"t4_{cyc}_{tc}")
            def fe(e, sem, tc=tc):
                return e.activation(cp_row[:, tc * 512:(tc + 1) * 512], pb[4][0:1, :], AF.Sigmoid,
                                    bias=cnb4_t)
            P.emit("scalar", fe, deps=(f"t4_{cyc}_{tc}",), label=f"t4e_{cyc}_{tc}")
        def fcp(e, sem):
            e.tensor_scalar(cp_row, cp_row, c1_sb, c0_sb, ALU.mult, ALU.add)
            return e.drain()
        P.emit("vector", fcp, deps=(f"t4e_{cyc}_3",), label=f"cprow_{cyc}")
        for qt in range(NT):
            def ftc(e, sem, qt=qt):
                return e.transpose(pb[4][0:128, 256:257], cp_row[:, qt * 128:(qt + 1) * 128], c_identf[0:1, 0:1])
            deps = [f"cprow_{cyc}"]
            if qt > 0:
                deps.append(f"cpev_{cyc}_{qt-1}")
            P.emit("tensor", ftc, deps=tuple(deps), label=f"cptp_{cyc}_{qt}")
            def ftcev(e, sem, qt=qt):
                return e.tensor_copy(cpT[:, qt:qt + 1], pb[4][0:128, 256:257])
            P.emit("vector", ftcev, deps=(f"cptp_{cyc}_{qt}",), label=f"cpev_{cyc}_{qt}")
        def fneg(e, sem):
            e.tensor_scalar_mul(cpTn, cpT, -1.0)
            return e.drain()
        P.emit("vector", fneg, deps=(f"cpev_{cyc}_{NT-1}",), label=f"cpneg_{cyc}")
        for i in range(NT):
            def ftw(e, sem, i=i):
                q = lambda k: xn_t[:, i, k * 256:(k + 1) * 256]
                e.tensor_copy(tw_tmp[:, 0, :], q(0))
                e.tensor_copy(tw_tmp[:, 1, :], q(1))
                e.drain()
                e.scalar_tensor_tensor(q(0), q(2), cpT[:, i:i + 1], q(0), ALU.mult, ALU.add)
                e.scalar_tensor_tensor(q(1), q(3), cpTn[:, i:i + 1], q(1), ALU.mult, ALU.add)
                e.drain()
                e.scalar_tensor_tensor(q(2), tw_tmp[:, 0, :], cpTn[:, i:i + 1], q(2), ALU.mult, ALU.add)
                e.scalar_tensor_tensor(q(3), tw_tmp[:, 1, :], cpT[:, i:i + 1], q(3), ALU.mult, ALU.add)
                return e.drain()
            P.emit("vector", ftw, deps=(f"cpneg_{cyc}",), label=f"tw_{cyc}_{i}")
        P.mark_here("vector", f"tw_done_{cyc}")

    def st_g(e, sem):
        e.dma_start(out=g_d, in_=xn_t.rearrange("p a b -> p (a b)")).then_inc(sem, 16)
    P.emit("sync", st_g, deps=("tw_done_2",), dma=1, label="g_stored")

    if nphase < 3:
        _finish_stub(P, out, xn_t)
        P.build()
        return nc, es

    P.barrier("p3")
    A.reset(ph_base)

    wout_bf = A.alloc([128, NC, D], BF16)
    bout_row = A.alloc([1, D], rows=1)
    bout_bc = A.alloc([128, D])
    ones1 = A.alloc([1, 128], rows=1)
    sil = A.alloc([128, 512])
    cmb = A.alloc([128, 512])
    af_sl = A.alloc([128, 2, NC, 128], BF16)

    def ld_wout(e, sem):
        e.dma_start(out=wout_bf, in_=w_out.rearrange("(c p) o -> p c o", p=128)).then_inc(sem, 16)
        e.dma_start(out=bout_row, in_=b_out).then_inc(sem, 16)
    P.emit("gpsimd", ld_wout, deps=("p3_gpsimd",), dma=2, label="wout_loaded")
    def fones(e, sem):
        e.memset(ones1, 1.0)
        return e.drain()
    P.emit("vector", fones, deps=("p3_vector",), label="ones_done")
    for hf in range(2):
        def fbb(e, sem, hf=hf):
            return e.matmul(pb[hf][:], ones1, bout_row[:, hf * 512:(hf + 1) * 512],
                            start=True, stop=True)
        P.emit("tensor", fbb, deps=("ones_done", "wout_loaded", "p3_tensor"), label=f"bb_{hf}")
        def fbbe(e, sem, hf=hf):
            return e.tensor_copy(bout_bc[:, hf * 512:(hf + 1) * 512], pb[hf][:])
        P.emit("vector", fbbe, deps=(f"bb_{hf}",), label=f"bbe_{hf}")

    ncm = 0
    for i in range(NT):
        def ldaf(e, sem, i=i):
            a3 = attnf_d.rearrange("p (c t) -> p c t", c=NC)
            e.dma_start(out=af_sl[:, i % 2, :, :].bitcast(U16), in_=a3[:, :, i * 128:(i + 1) * 128]).then_inc(sem, 16)
        depl = ["attn_stored", "p3_gpsimd"]
        if i >= 2:
            depl.append(f"cmb_{2*(i-2)+1}")
        P.emit("gpsimd", ldaf, deps=tuple(depl), dma=1, label=f"afld_{i}")
        def ldx(e, sem, i=i):
            e.dma_start(out=x_tile, in_=x_in.rearrange("(t p) d -> p t d", p=128)[:, i, :]).then_inc(sem, 16)
        depx = ["p3_sync"]
        if i >= 1:
            depx.append(f"cmb_{2*(i-1)+1}")
        P.emit("sync", ldx, deps=tuple(depx), dma=1, label=f"xld2_{i}")
        for hf in range(2):
            b = 2 + (ncm % 2)
            def f(e, sem, i=i, hf=hf, b=b):
                for kc in range(NC):
                    last = e.matmul(pb[b][:], af_sl[:, i % 2, kc, :],
                                    wout_bf[:, kc, hf * 512:(hf + 1) * 512],
                                    start=(kc == 0), stop=(kc == NC - 1))
                return last
            deps = [f"afld_{i}"]
            if ncm == 0:
                deps += ["bbe_0", "bbe_1", "g_stored"]
            if ncm >= 2:
                deps.append(f"cmb_{ncm-2}")
            P.emit("tensor", f, deps=tuple(deps), label=f"smm_{ncm}")
            def fsil(e, sem, i=i, hf=hf):
                return e.activation(sil, xn_t[:, i, hf * 512:(hf + 1) * 512], AF.Silu)
            P.emit("scalar", fsil, deps=(f"smm_{ncm}",) + (() if ncm == 0 else (f"cmb_{ncm-1}",)), label=f"sil_{ncm}")
            def fcmb(e, sem, i=i, hf=hf, b=b):
                sl = slice(hf * 512, (hf + 1) * 512)
                e.tensor_tensor(cmb, pb[b][:], bout_bc[:, sl], ALU.add)
                e.drain()
                e.tensor_tensor(cmb, cmb, sil, ALU.mult)
                e.drain()
                e.tensor_tensor(xn_t[:, i, sl], x_tile[:, sl], cmb, ALU.add)
                return e.drain()
            P.emit("vector", fcmb, deps=(f"sil_{ncm}", f"xld2_{i}"), label=f"cmb_{ncm}")
            ncm += 1
    P.mark_here("vector", "x1_done")

    def st_x1(e, sem):
        e.dma_start(out=x1_d, in_=xn_t.rearrange("p a b -> p (a b)")).then_inc(sem, 16)
    P.emit("sync", st_x1, deps=("x1_done",), dma=1, label="x1_stored")

    if nphase < 4:
        def st_out(e, sem):
            e.dma_start(out=out.rearrange("(t p) d -> p t d", p=128), in_=xn_t).then_inc(sem, 16)
        P.emit("sync", st_out, deps=("x1_done",), dma=1, label="out_stored")
        P.build()
        return nc, es

    P.barrier("p4")
    A.reset(m_xn)
    xn2_t = A.alloc([128, NT, D], BF16)
    p5_base = A.mark()

    st_sum4 = A.alloc([128, NT]); st_sq4 = A.alloc([128, NT])
    st_mean4 = A.alloc([128, NT]); st_var4 = A.alloc([128, NT])
    st_a2 = A.alloc([128, NT]); st_b2 = A.alloc([128, NT])
    st_sv = A.alloc([128, NT])
    sq_scr4 = A.alloc([128, D])
    gam_bc = A.alloc([128, D])
    gam_row = A.alloc([1, D], rows=1)
    ones4 = A.alloc([1, 128], rows=1)
    xn1_scr = A.alloc([128, D])
    conv_buf = A.alloc([128, 2, 2, 2 * D], BF16)

    NCHUNK = 0 if os.environ.get("SKIP_CONV") else 256
    ncv = 0
    for tab_in, tab_out in (((tab_du, tabdu_bf),) if NCHUNK else ()):
        for ch in range(NCHUNK):
            bcv = ncv % 2
            def fldc(e, sem, tab_in=tab_in, ch=ch, bcv=bcv):
                src_ap = tab_in[ch * 256:(ch + 1) * 256, :].rearrange("(r p) d -> p r d", p=128)
                e.dma_start(out=conv_buf[:, bcv, :, :], in_=src_ap).then_inc(sem, 16)
            depc = ["p4_gpsimd"]
            if ncv >= 2:
                depc.append(f"cvst_{ncv-2}")
            P.emit("gpsimd", fldc, deps=tuple(depc), dma=1, label=f"cvld_{ncv}")
            def fstc(e, sem, tab_out=tab_out, ch=ch, bcv=bcv):
                dst_ap = tab_out[ch * 256:(ch + 1) * 256, :].rearrange("(r p) d -> p r d", p=128)
                e.dma_start(out=dst_ap, in_=conv_buf[:, bcv, :, :].bitcast(U16)).then_inc(sem, 16)
            P.emit("sync", fstc, deps=(f"cvld_{ncv}",), dma=1, label=f"cvst_{ncv}")
            ncv += 1
    P.mark_here("sync", "tables_done")
    NPH16 = 1 if os.environ.get("SMALL_A") else 16
    NTI5 = 1 if os.environ.get("SMALL_A") else NT

    for i in range(NT):
        def f(e, sem, i=i):
            return e.activation(sq_scr4, xn_t[:, i, :], AF.Square)
        P.emit("scalar", f, deps=("p4_scalar", "x1_done") if i == 0 else (f"sq4_{i-1}",), label=f"sqd4_{i}")
        def f2(e, sem, i=i):
            e.reduce_sum(st_sum4[:, i:i + 1], xn_t[:, i, :], axis=AX.X)
            return e.reduce_sum(st_sq4[:, i:i + 1], sq_scr4, axis=AX.X)
        P.emit("vector", f2, deps=(f"sqd4_{i}", "p4_vector"), label=f"sq4_{i}")
    def stats4(e, sem):
        e.tensor_scalar_mul(st_mean4, st_sum4, 1.0 / D)
        e.drain()
        e.tensor_scalar_mul(st_var4, st_sq4, 1.0 / D)
        e.drain()
        e.tensor_tensor(st_b2, st_mean4, st_mean4, ALU.mult)
        e.drain()
        e.tensor_sub(st_var4, st_var4, st_b2)
        e.drain()
        return e.tensor_scalar_add(st_a2, st_var4, 1e-5)
    P.emit("vector", stats4, deps=(f"sq4_{NT-1}",), label="var4_done")
    def rstd4a(e, sem):
        e.activation(ln1_rstd, st_a2, AF.Sqrt)
        return e.activation(st_sv, st_var4, AF.Sqrt)
    P.emit("scalar", rstd4a, deps=("var4_done",), label="sqrt4_done")
    def rstd4b(e, sem):
        e.reciprocal(ln1_rstd, ln1_rstd)
        e.drain()
        e.tensor_tensor(ln1_nmr, st_mean4, ln1_rstd, ALU.mult)
        e.drain()
        e.tensor_scalar_mul(ln1_nmr, ln1_nmr, -1.0)
        e.drain()
        e.tensor_tensor(st_a2, ln1_rstd, st_sv, ALU.mult)
        e.drain()
        e.reciprocal(st_a2, st_a2)
        e.drain()
        e.tensor_tensor(st_b2, ln1_nmr, st_a2, ALU.mult)
        e.drain()
        e.tensor_tensor(st_a2, ln1_rstd, st_a2, ALU.mult)
        return e.drain()
    P.emit("vector", rstd4b, deps=("sqrt4_done",), label="ln1_done")

    def fon4(e, sem):
        e.memset(ones4, 1.0)
        return e.drain()
    P.emit("vector", fon4, deps=("p4_vector",), label="ones4")
    def ldgam(e, sem):
        e.dma_start(out=gam_row, in_=gamma).then_inc(sem, 16)
    P.emit("sync", ldgam, deps=("p4_sync",), dma=1, label="gam_loaded")
    for hf in range(2):
        def fgb(e, sem, hf=hf):
            return e.matmul(pb[6][:], ones4, gam_row[:, hf * 512:(hf + 1) * 512], start=True, stop=True)
        P.emit("tensor", fgb, deps=("ones4", "gam_loaded", "p4_tensor") if hf == 0 else (f"gbe_{hf-1}",), label=f"gb_{hf}")
        def fgbe(e, sem, hf=hf):
            return e.tensor_copy(gam_bc[:, hf * 512:(hf + 1) * 512], pb[6][:])
        P.emit("vector", fgbe, deps=(f"gb_{hf}",), label=f"gbe_{hf}")

    for i in range(NT):
        def fxn2(e, sem, i=i):
            e.tensor_scalar(xn1_scr, xn_t[:, i, :], st_a2[:, i:i + 1], st_b2[:, i:i + 1],
                            ALU.mult, ALU.add)
            e.drain()
            e.tensor_tensor(xn2_t[:, i, :], xn1_scr, gam_bc, ALU.mult)
            return e.drain()
        P.emit("vector", fxn2, deps=("ln1_done", "gbe_1") if i == 0 else (f"xn2_{i-1}",), label=f"xn2_{i}")
    P.mark_here("vector", "xn2t_done")

    if nphase < 5:
        _finish_stub(P, out, xn_t)
        P.build()
        return nc, es

    P.barrier("p5")
    A.reset(p5_base)
    xn2_f = A.alloc_at(base_min, [128, NC, T], BF16)
    keysT_sb = A.alloc_at(base_min + 8192, [128, 64, 256], BF16)
    wdu = A.alloc_at(base_min, [128, 2, 8, 2 * D], BF16)
    wqsl_off = A.mark()
    wq_sl = A.alloc([128, 2, NC, 512], BF16)
    qf_sb = A.alloc([128, 4, 128], BF16)
    sim_sb = A.alloc([128, 256], F32)
    sim2_sb = A.alloc([128, 256], F32)
    sx_all = A.alloc([128, NT, 16, 16], F32)
    ix_all = A.alloc([128, NT, 16, 16], U16)
    w16_all = A.alloc([128, NT, 8, 16], F32)
    ixs_all = A.alloc([128, NT, 8, 16], U16)
    iys_all = A.alloc([128, NT, 8, 16], U16)
    A_sb = A.alloc([128, 256], F32)
    sc16 = A.alloc([128, 8, 16], F32)
    pk16 = A.alloc([128, 16], U16)
    iota16 = A.alloc([128, 16], U16)
    isel = A.alloc([128, 16], U16)
    jsel = A.alloc([128, 16], U16)
    oh16 = A.alloc([128, 256], U16)
    pr16 = A.alloc([128, 256], U16)
    m0n8 = A.alloc([128, 8], F32)
    e16 = A.alloc([128, 8, 16], F32)
    l8 = A.alloc([128, 8], F32)
    eidx = A.alloc([128, 128], I32)
    ei_a = A.alloc([128, 128], I32)
    ei_b = A.alloc([128, 128], I32)
    prod5 = A.alloc([128, 8, D], BF16)
    h16 = A.alloc([128, 16], F32)
    hh16 = A.alloc([128, 16], F32)
    g1_tile = A.alloc([128, D], F32)
    g1_acc = A.alloc_at(wqsl_off, [128, 4, D], F32)

    def ldk5(e, sem):
        e.dma_start(out=keysT_sb, in_=keysT.rearrange("p (a b) -> p a b", a=64)).then_inc(sem, 16)
    P.emit("gpsimd", ldk5, deps=("p5_gpsimd",), dma=1, label="keys_loaded")
    ntp5 = 0
    for i in range(NT):
        for c4 in range(2):
            def ftp(e, sem, i=i, c4=c4):
                for j in range(4):
                    last = e.transpose(pbf(j), xn2_t[:, i, (c4 * 4 + j) * 128:(c4 * 4 + j + 1) * 128], c_identb)
                return last
            deps = ["xn2t_done", "p5_tensor"]
            if ntp5 >= 1:
                deps.append(f"tp5ev_{ntp5-1}")
            P.emit("tensor", ftp, deps=tuple(deps), label=f"tp5_{ntp5}")
            def fev(e, sem, i=i, c4=c4):
                for j in range(4):
                    e.tensor_copy(xn2_f[:, c4 * 4 + j, i * 128:(i + 1) * 128], pbf(j))
                return e.drain()
            P.emit("vector", fev, deps=(f"tp5_{ntp5}",), label=f"tp5ev_{ntp5}")
            ntp5 += 1
    P.mark_here("vector", "xn2_done")
    def fio5(e, sem):
        return e.iota(iota16, pattern=[[1, 16]], base=0, channel_multiplier=0)
    P.emit("gpsimd", fio5, deps=("p5_gpsimd",), label="iota5")

    nq5 = 0
    for ph16 in range(NPH16):
        p_, h_ = ph16 // 8, ph16 % 8
        wb = ph16 % 2
        def ldw5(e, sem, p_=p_, h_=h_, wb=wb):
            col0 = p_ * 4096 + h_ * 512
            e.dma_start(out=wq_sl[:, wb, :, :],
                        in_=wqT.rearrange("(c p) o -> p c o", p=128)[:, :, col0:col0 + 512]).then_inc(sem, 16)
        depw = ["p5_gpsimd"]
        if ph16 >= 2:
            depw.append(f"s1_{ph16-2}_{NT-1}")
        P.emit("gpsimd", ldw5, deps=tuple(depw), dma=1, label=f"wq5_{ph16}")
        for i in range(NTI5):
            bq = nq5 % 2
            def fq(e, sem, i=i, wb=wb, bq=bq):
                for oc4 in range(4):
                    for kc in range(NC):
                        last = e.matmul(pb[bq][:, oc4 * 128:(oc4 + 1) * 128],
                                        wq_sl[:, wb, kc, oc4 * 128:(oc4 + 1) * 128],
                                        xn2_f[:, kc, i * 128:(i + 1) * 128],
                                        start=(oc4 == 0 and kc == 0), stop=(kc == NC - 1))
                return last
            depq = [f"wq5_{ph16}", "xn2_done"]

            if nq5 >= 2:
                depq.append(f"qev5_{nq5-2}")
            P.emit("tensor", fq, deps=tuple(depq), label=f"qmm5_{nq5}")
            def fqe(e, sem, bq=bq):
                return e.tensor_copy(qf_sb.rearrange("p a b -> p (a b)"), pb[bq][:])
            dqe = [f"qmm5_{nq5}"]
            if nq5 >= 1:
                dqe.append(f"sim5_{nq5-1}")
            P.emit("vector", fqe, deps=tuple(dqe), label=f"qev5_{nq5}")
            def fsim(e, sem, ph16=ph16, bq=bq):
                for j in range(4):
                    last = e.matmul(pb[4 + (bq % 2)][:, 0:256], qf_sb[:, j, :],
                                    keysT_sb[:, (ph16 % 8) * 8 + (ph16 // 8) * 4 + j, :],
                                    start=(j == 0), stop=(j == 3))
                return last
            dsim = [f"qev5_{nq5}", "keys_loaded"]
            if nq5 >= 2:
                dsim.append(f"simev5_{nq5-2}")
            P.emit("tensor", fsim, deps=tuple(dsim), label=f"sim5_{nq5}")
            def fsime(e, sem, bq=bq):
                return e.tensor_copy(sim_sb, pb[4 + (bq % 2)][:, 0:256])
            P.emit("vector", fsime, deps=(f"sim5_{nq5}",), label=f"simev5_{nq5}")
            def ftop1(e, sem, i=i, ph16=ph16):
                e.max(out=sx_all[:, i, ph16, 0:8], in_=sim_sb)
                e.drain()
                e.max_index(out=ix_all[:, i, ph16, 0:8], in_max=sx_all[:, i, ph16, 0:8], in_values=sim_sb)
                e.drain()
                e.match_replace(out=sim2_sb, in_to_replace=sx_all[:, i, ph16, 0:8], in_values=sim_sb,
                                imm_value=-3.0e38)
                e.drain()
                e.max(out=sx_all[:, i, ph16, 8:16], in_=sim2_sb)
                e.drain()
                e.max_index(out=ix_all[:, i, ph16, 8:16], in_max=sx_all[:, i, ph16, 8:16], in_values=sim2_sb)
                return e.drain()
            P.emit("vector", ftop1, deps=(f"simev5_{nq5}", "iota5"), label=f"s1_{ph16}_{i}")
            nq5 += 1
    if "s1_15_15" not in P.marks and NPH16 == 16:
        pass
    P.mark_here("vector", "stage1_done")

    NB5 = 0 if os.environ.get("SKIP_B") else NT
    for i in range(NB5):
        def fs2(e, sem, i=i):
            for h_ in range(8):
                sx = sx_all[:, i, h_, :]
                sy = sx_all[:, i, 8 + h_, :]
                e.tensor_tensor(A_sb.rearrange("p (a b) -> p a b", a=16),
                                sx.unsqueeze(2).to_broadcast([128, 16, 16]),
                                sy.unsqueeze(1).to_broadcast([128, 16, 16]), ALU.add)
                e.drain()
                e.max(out=sc16[:, h_, 0:8], in_=A_sb)
                e.drain()
                e.max_index(out=pk16[:, 0:8], in_max=sc16[:, h_, 0:8], in_values=A_sb)
                e.drain()
                e.match_replace(out=A_sb, in_to_replace=sc16[:, h_, 0:8], in_values=A_sb, imm_value=-3.0e38)
                e.drain()
                e.max(out=sc16[:, h_, 8:16], in_=A_sb)
                e.drain()
                e.max_index(out=pk16[:, 8:16], in_max=sc16[:, h_, 8:16], in_values=A_sb)
                e.drain()
                e.tensor_scalar(isel, pk16, 4, None, ALU.logical_shift_right)
                e.tensor_scalar(jsel, pk16, 15, None, ALU.bitwise_and)
                e.drain()
                e.tensor_tensor(oh16.rearrange("p (a b) -> p a b", a=16),
                                isel.unsqueeze(2).to_broadcast([128, 16, 16]),
                                iota16.unsqueeze(1).to_broadcast([128, 16, 16]), ALU.is_equal)
                e.drain()
                e.tensor_tensor(pr16.rearrange("p (a b) -> p a b", a=16),
                                oh16.rearrange("p (a b) -> p a b", a=16),
                                ix_all[:, i, h_, :].unsqueeze(1).to_broadcast([128, 16, 16]), ALU.mult)
                e.drain()
                with nc.allow_low_precision(reason="u16 index select"):
                    e.reduce_sum(ixs_all[:, i, h_, :], pr16.rearrange("p (a b) -> p a b", a=16), axis=AX.X)
                e.drain()
                e.tensor_tensor(oh16.rearrange("p (a b) -> p a b", a=16),
                                jsel.unsqueeze(2).to_broadcast([128, 16, 16]),
                                iota16.unsqueeze(1).to_broadcast([128, 16, 16]), ALU.is_equal)
                e.drain()
                e.tensor_tensor(pr16.rearrange("p (a b) -> p a b", a=16),
                                oh16.rearrange("p (a b) -> p a b", a=16),
                                ix_all[:, i, 8 + h_, :].unsqueeze(1).to_broadcast([128, 16, 16]), ALU.mult)
                e.drain()
                with nc.allow_low_precision(reason="u16 index select"):
                    e.reduce_sum(iys_all[:, i, h_, :], pr16.rearrange("p (a b) -> p a b", a=16), axis=AX.X)
                e.drain()
                e.tensor_scalar_mul(m0n8[:, h_:h_ + 1], sc16[:, h_, 0:1], -1.0)
                e.drain()
            return e.drain()
        dep_s2 = [f"s1_15_{i}"]
        if i > 0:
            dep_s2.append(f"sexp_{i-1}")
        P.emit("vector", fs2, deps=tuple(dep_s2), label=f"s2_{i}")
        def fexp(e, sem):
            for h_ in range(8):
                last = e.activation(e16[:, h_, :], sc16[:, h_, :], AF.Exp, bias=m0n8[:, h_:h_ + 1])
            return last
        dep_e = [f"s2_{i}"]
        if i > 0:
            dep_e.append(f"sm2_{i-1}")
        P.emit("scalar", fexp, deps=tuple(dep_e), label=f"sexp_{i}")
        def fsm(e, sem, i=i):
            for h_ in range(8):
                e.reduce_sum(l8[:, h_:h_ + 1], e16[:, h_, :], axis=AX.X)
            e.drain()
            e.reciprocal(l8, l8)
            e.drain()
            for h_ in range(8):
                e.tensor_scalar_mul(w16_all[:, i, h_, :], e16[:, h_, :], l8[:, h_:h_ + 1])
            return e.drain()
        P.emit("vector", fsm, deps=(f"sexp_{i}",), label=f"sm2_{i}")
    P.mark_here("vector", "stage2_done")

    ng5 = 0
    NCI = 1 if os.environ.get("SMALL_C") else NT
    for i in range(NCI):
        def feid(e, sem, i=i):
            e.tensor_copy(ei_a, ixs_all[:, i, :, :].rearrange("p a b -> p (a b)"))
            e.tensor_copy(ei_b, iys_all[:, i, :, :].rearrange("p a b -> p (a b)"))
            e.tensor_scalar(eidx, ei_a, 256, None, ALU.mult)
            return e.tensor_tensor(eidx, eidx, ei_b, ALU.add)
        depe = [f"sm2_{i}"]
        if i > 0:
            depe.append(f"gdone_{i-1}")
        if "stage2_done" not in P.marks:
            P.mark_here("vector", "stage2_done")
        P.emit("gpsimd", feid, deps=tuple(depe), label=f"eid_{i}")
        if debug:
            def fdbg(e, sem, i=i):
                e.dma_start(out=dbg4.rearrange("p (a b) -> p a b", a=16)[:, i, :], in_=eidx).then_inc(sem, 16)
            P.emit("sync", fdbg, deps=(f"eid_{i}",), dma=1, label=f"dbge_{i}")
        def fgz(e, sem, i=i):
            e.memset(g1_acc.rearrange("p a b -> p (a b)"), 0.0)
            return e.drain()
        P.emit("vector", fgz, deps=(f"eid_{i}",) + (() if i == 0 else (f"g1st_{i-1}",)), label=f"g1z_{i}")
        for h_ in range(8):
            for hb in range(2):
                def fgth(e, sem, i=i, h_=h_, hb=hb):
                    last = None
                    for kk in range(8):
                        kg = h_ * 16 + hb * 8 + kk
                        last = e.indirect_dma_start(
                            out=wdu[:, hb, kk, :].bitcast(U16), out_offset=None,
                            in_=tabdu_bf[0:128, :],
                            in_offset=IndirectOffsetOnAxis(ap=eidx[:, kg:kg + 1], axis=0))
                        last.then_inc(sem, 16)
                depg = [f"eid_{i}", "tables_done"]
                if ng5 >= 2:
                    depg.append(f"updot_{ng5-2}")
                P.emit("gpsimd", fgth, deps=tuple(depg), dma=8, label=f"gth_{ng5}")
                def fdown(e, sem, i=i, hb=hb):
                    return e.tensor_tensor(prod5, wdu[:, hb, :, 0:D],
                                           xn2_t[:, i, :].unsqueeze(1).to_broadcast([128, 8, D]),
                                           ALU.mult)
                P.emit("vector", fdown, deps=(f"gth_{ng5}",), label=f"down_{ng5}")
                def fgel(e, sem, hb=hb):
                    for kk in range(8):
                        e.activation(prod5[:, kk, :], prod5[:, kk, :], AF.Identity,
                                     accum_out=h16[:, hb * 8 + kk:hb * 8 + kk + 1])
                    return e.activation(hh16[:, hb * 8:(hb + 1) * 8], h16[:, hb * 8:(hb + 1) * 8],
                                        AF.Gelu)
                P.emit("scalar", fgel, deps=(f"down_{ng5}",), label=f"gel_{ng5}")
                def fup(e, sem, i=i, h_=h_, hb=hb):
                    sl = slice(hb * 8, (hb + 1) * 8)
                    e.tensor_tensor(h16[:, sl], hh16[:, sl], w16_all[:, i, h_, sl], ALU.mult)
                    e.drain()
                    for kk in range(8):
                        e.scalar_tensor_tensor(g1_acc[:, kk % 4, :], wdu[:, hb, kk, D:2 * D],
                                               h16[:, hb * 8 + kk:hb * 8 + kk + 1],
                                               g1_acc[:, kk % 4, :], ALU.mult, ALU.add)
                    return e.drain()
                P.emit("vector", fup, deps=(f"gel_{ng5}", f"g1z_{i}"), label=f"updot_{ng5}")
                ng5 += 1
        def fgsum(e, sem, i=i):
            e.tensor_add(g1_acc[:, 0, :], g1_acc[:, 0, :], g1_acc[:, 1, :])
            e.tensor_add(g1_acc[:, 2, :], g1_acc[:, 2, :], g1_acc[:, 3, :])
            e.drain()
            e.tensor_add(g1_tile, g1_acc[:, 0, :], g1_acc[:, 2, :])
            return e.drain()
        P.emit("vector", fgsum, deps=(f"updot_{ng5-1}",), label=f"gdone_{i}")
        def fg1st(e, sem, i=i):
            e.dma_start(out=g_d.rearrange("p (a b) -> p a b", a=NT)[:, i, :], in_=g1_tile).then_inc(sem, 16)
        P.emit("sync", fg1st, deps=(f"gdone_{i}",), dma=1, label=f"g1st_{i}")
    P.mark_here("sync", "g1_stored")

    if nphase < 6:
        _finish_stub(P, out, xn_t)
        P.build()
        return nc, es

    P.barrier("p6")
    A.reset(base_min)
    s1f = A.alloc([128, NC, T], BF16)
    v65 = A.alloc([128, NT, 16 * 65], BF16)
    xn1f = A.alloc([128, NC, T], BF16)
    Qf = A.alloc([128, NC, T], BF16)
    Kf = A.alloc([128, NC, T], BF16)
    m6 = A.mark()

    xn1sc = A.alloc([128, D], BF16)
    x1_tl = A.alloc([128, D], F32)
    pocket = A.mark()
    ntp6 = 0
    for i in range(NT):
        def fldx(e, sem, i=i):
            e.dma_start(out=x1_tl, in_=x1_d.rearrange("p (a b) -> p a b", a=NT)[:, i, :]).then_inc(sem, 16)
        depl = ["p6_sync", "x1_stored"]
        if i > 0:
            depl.append(f"tp6_{ntp6-1}")
        P.emit("sync", fldx, deps=tuple(depl), dma=1, label=f"x1ld_{i}")
        def fxn1(e, sem, i=i):
            e.tensor_scalar(xn1sc, x1_tl, ln1_rstd[:, i:i + 1], ln1_nmr[:, i:i + 1],
                            ALU.mult, ALU.add)
            return e.drain()
        P.emit("vector", fxn1, deps=(f"x1ld_{i}", "p6_vector"), label=f"xn1_{i}")
        for c4 in range(2):
            def ftp(e, sem, i=i, c4=c4):
                for j in range(4):
                    last = e.transpose(pbf(j), xn1sc[:, (c4 * 4 + j) * 128:(c4 * 4 + j + 1) * 128], c_identb)
                return last
            deps = [f"xn1_{i}", "p6_tensor"]
            if ntp6 >= 1:
                deps.append(f"tp6ev_{ntp6-1}")
            P.emit("tensor", ftp, deps=tuple(deps), label=f"tp6_{ntp6}")
            def fev(e, sem, i=i, c4=c4):
                for j in range(4):
                    e.tensor_copy(xn1f[:, c4 * 4 + j, i * 128:(i + 1) * 128], pbf(j))
                return e.drain()
            P.emit("vector", fev, deps=(f"tp6_{ntp6}",), label=f"tp6ev_{ntp6}")
            ntp6 += 1
    P.mark_here("vector", "xn1f_done")

    wv6 = A.alloc_at(pocket, [128, NC, D], BF16)
    def ldwv(e, sem):
        e.dma_start(out=wv6, in_=wqkv1.rearrange("(c p) o -> p c o", p=128)[:, :, 2 * D:3 * D]).then_inc(sem, 16)
    P.emit("gpsimd", ldwv, deps=("p6_gpsimd",), dma=1, label="wv6_loaded")
    v65r = v65.rearrange("p a (h w) -> p a h w", h=16)
    nv6 = 0
    for i in range(NT):
        for hf in range(2):
            b = nv6 % 2
            def f(e, sem, i=i, hf=hf, b=b):
                for kc in range(NC):
                    last = e.matmul(pb[b][:], xn1f[:, kc, i * 128:(i + 1) * 128],
                                    wv6[:, kc, hf * 512:(hf + 1) * 512],
                                    start=(kc == 0), stop=(kc == NC - 1))
                return last
            deps = ["wv6_loaded", "xn1f_done"]
            if nv6 >= 2:
                deps.append(f"v6ev_{nv6-2}")
            P.emit("tensor", f, deps=tuple(deps), label=f"v6mm_{nv6}")
            def fe(e, sem, i=i, hf=hf, b=b):
                e.tensor_copy(v65r[:, i, hf * 8:(hf + 1) * 8, 0:64],
                              pb[b][:].rearrange("p (a b) -> p a b", a=8))
                return e.drain()
            P.emit("vector", fe, deps=(f"v6mm_{nv6}",), label=f"v6ev_{nv6}")
            nv6 += 1
        def fo(e, sem, i=i):
            e.memset(v65r[:, i, :, 64:65], 1.0)
            return e.drain()
        P.emit("vector", fo, deps=(f"v6ev_{nv6-1}",), label=f"v6o_{i}")
    P.mark_here("vector", "v6_done")

    wg6 = A.alloc_at(pocket, [128, NC, 128], BF16)
    wr6 = A.alloc_at(pocket + 512, [128, NC, 128], BF16)
    cosb = A.alloc_at(pocket + 1024, [128, T], F32)
    sinb = A.alloc_at(pocket + 3072, [128, T], F32)
    rp1 = A.alloc_at(pocket + 5120, [128, 512], F32)
    rp2 = A.alloc_at(pocket + 5632, [128, 512], F32)
    nrk = 0
    for qk in range(2):
        dstf = Qf if qk == 0 else Kf
        cos_in = cosq8 if qk == 0 else cosk
        sin_in = sinq8 if qk == 0 else sink
        def ldcs(e, sem, cos_in=cos_in, sin_in=sin_in):
            e.dma_start(out=cosb, in_=cos_in).then_inc(sem, 16)
            e.dma_start(out=sinb, in_=sin_in).then_inc(sem, 16)
        depc = ["p6_gpsimd", f"v6ev_{nv6-1}", "v6_done"] if qk == 0 else [f"rkev_{nrk-1}"]
        P.emit("gpsimd", ldcs, deps=tuple(depc), dma=2, label=f"cs6_{qk}")
        for grp in range(NC):
            def ldwg(e, sem, qk=qk, grp=grp):
                col0 = qk * D + grp * 128
                e.dma_start(out=wg6, in_=wqkv1.rearrange("(c p) o -> p c o", p=128)[:, :, col0:col0 + 128]).then_inc(sem, 16)
            depw = [f"cs6_{qk}"]
            if nrk >= 1:
                deps_guard = f"rkmm_{nrk-1}"
                depw.append(deps_guard)
            P.emit("gpsimd", ldwg, deps=tuple(depw), dma=1, label=f"wg6_{qk}_{grp}")
            def frot(e, sem):
                w3 = wg6.rearrange("p c (a two) -> p c a two", two=2)
                r3 = wr6.rearrange("p c (a two) -> p c a two", two=2)
                e.tensor_scalar_mul(r3[:, :, :, 0:1], w3[:, :, :, 1:2], -1.0)
                e.drain()
                e.tensor_copy(r3[:, :, :, 1:2], w3[:, :, :, 0:1])
                return e.drain()
            P.emit("vector", frot, deps=(f"wg6_{qk}_{grp}",), label=f"rot_{qk}_{grp}")
            for tc in range(4):
                def fmm(e, sem, grp=grp, tc=tc):
                    for kc in range(NC):
                        e.matmul(pb[0][:], wg6[:, kc, :], xn1f[:, kc, tc * 512:(tc + 1) * 512],
                                 start=(kc == 0), stop=(kc == NC - 1))
                    for kc in range(NC):
                        last = e.matmul(pb[1][:], wr6[:, kc, :], xn1f[:, kc, tc * 512:(tc + 1) * 512],
                                        start=(kc == 0), stop=(kc == NC - 1))
                    return last
                depm = [f"rot_{qk}_{grp}", "v6_done", "xn1f_done"]
                if nrk >= 1:
                    depm.append(f"rkev_{nrk-1}")
                P.emit("tensor", fmm, deps=tuple(depm), label=f"rkmm_{nrk}")
                def fev(e, sem, grp=grp, tc=tc, dstf=dstf):
                    sl = slice(tc * 512, (tc + 1) * 512)
                    e.tensor_tensor(rp1, pb[0][:], cosb[:, sl], ALU.mult)
                    e.tensor_tensor(rp2, pb[1][:], sinb[:, sl], ALU.mult)
                    e.drain()
                    e.tensor_tensor(dstf[:, grp, sl], rp1, rp2, ALU.add)
                    return e.drain()
                P.emit("vector", fev, deps=(f"rkmm_{nrk}",), label=f"rkev_{nrk}")
                nrk += 1
    P.mark_here("vector", "qk6_done")

    Qaug = A.alloc_at(pocket, [128, T], BF16)
    Kaug = A.alloc_at(pocket + 1024, [128, T], BF16)
    PT = A.alloc_at(pocket + 2048, [128, T], BF16)
    PT2 = A.alloc_at(pocket + 3072, [128, T], BF16)
    qqrow = A.alloc_at(pocket + 3072, [1, T], F32, rows=1)
    kkrow = A.alloc_at(pocket + 5120, [1, T], F32, rows=1)
    mkk = A.alloc_at(pocket + 7168, [1, 1], F32, rows=1)
    crow = qqrow
    crow_bf = A.alloc_at(pocket + 5120, [1, T], BF16, rows=1)
    ones64 = A.alloc_at(pocket + 7170, [128, 1], BF16)
    trim6 = A.alloc_at(pocket + 7244, [128, 128], F32)
    rl7 = A.alloc_at(pocket + 7372, [128, 8], F32)
    attn_sc7 = A.alloc_at(pocket + 7380, [128, 7, 64], BF16)
    sq6 = PT.bitcast(F32) if False else None

    def ftr6(e, sem):
        e.dma_start(out=trim6, in_=trimask).then_inc(sem, 16)
    P.emit("gpsimd", ftr6, deps=("p6_gpsimd", "qk6_done"), dma=1, label="tri_loaded")
    def fo64(e, sem):
        e.memset(ones64, 1.0)
        return e.drain()
    P.emit("vector", fo64, deps=("qk6_done",), label="ones64")

    nst6 = 0
    npv6 = 0
    for h in range(16):
        grp, hh = h // 2, h % 2
        rb = hh * 64
        def fsq(e, sem, grp=grp, rb=rb):
            e.activation(PT[0:64, :], Qf[rb:rb + 64, grp, :], AF.Square)
            return e.activation(PT[64:128, :], Kf[rb:rb + 64, grp, :], AF.Square)
        dep0 = ["qk6_done"] if h == 0 else [f"aug_{h-1}", f"pv6c_{npv6-1}"]
        P.emit("scalar", fsq, deps=tuple(dep0), label=f"sq6_{h}")
        for tc in range(4):
            def fqq(e, sem, tc=tc):
                e.matmul(pb[6][0:1, 0:512], ones64[0:64, :], PT[0:64, tc * 512:(tc + 1) * 512],
                         start=True, stop=True)
                return e.matmul(pb[7][0:1, 0:512], ones64[64:128, :],
                                PT[64:128, tc * 512:(tc + 1) * 512], start=True, stop=True)
            deps = [f"sq6_{h}", "ones64"]
            if not (h == 0 and tc == 0):
                deps.append(f"qqe_{h}_{tc-1}" if tc > 0 else f"qqe_{h-1}_3")
            P.emit("tensor", fqq, deps=tuple(deps), label=f"qq_{h}_{tc}")
            def fqe(e, sem, tc=tc):
                e.tensor_copy(qqrow[:, tc * 512:(tc + 1) * 512], pb[6][0:1, 0:512])
                e.tensor_copy(kkrow[:, tc * 512:(tc + 1) * 512], pb[7][0:1, 0:512])
                return e.drain()
            P.emit("vector", fqe, deps=(f"qq_{h}_{tc}",), label=f"qqe_{h}_{tc}")
        def fcrow(e, sem):
            e.reduce_max(mkk, kkrow, axis=AX.X)
            return e.drain()
        P.emit("vector", fcrow, deps=(f"qqe_{h}_3",), label=f"mkk_{h}")
        def fcs(e, sem):
            e.activation(mkk, mkk, AF.Sqrt)
            return e.activation(crow, qqrow, AF.Sqrt)
        P.emit("scalar", fcs, deps=(f"mkk_{h}",), label=f"cs_{h}")
        def fcb(e, sem):
            e.tensor_scalar(crow, crow, mkk, None, ALU.mult)
            e.drain()
            e.tensor_scalar(crow, crow, -1.02, -1.0, ALU.mult, ALU.add)
            e.drain()
            e.tensor_copy(crow_bf, crow)
            return e.drain()
        P.emit("vector", fcb, deps=(f"cs_{h}",), label=f"cb_{h}")
        def faug(e, sem, grp=grp, rb=rb):
            e.tensor_copy(Qaug[0:64, :], Qf[rb:rb + 64, grp, :])
            e.tensor_copy(Kaug[0:64, :], Kf[rb:rb + 64, grp, :])
            e.drain()
            e.tensor_copy(Qaug[64:65, :], crow_bf)
            e.memset(Kaug[64:65, :], 1.0)
            return e.drain()
        P.emit("vector", faug, deps=(f"cb_{h}",), label=f"aug_{h}")
        def acc(qt):
            return pb[4 + qt // 7][:, (qt % 7) * 65:(qt % 7) * 65 + 65]
        def emit_evacs(qts):
            nonlocal npv6
            kt_dep = 6 if qts[-1] <= 6 else (13 if qts[-1] <= 13 else 15)
            def fpve(e, sem, qts=qts):
                for j, qt in enumerate(qts):
                    e.reciprocal(rl7[:, j:j + 1], acc(qt)[:, 64:65])
                e.drain()
                for j, qt in enumerate(qts):
                    e.tensor_scalar_mul(attn_sc7[:, j, :], acc(qt)[:, 0:64], rl7[:, j:j + 1])
                return e.drain()
            depe = [f"pvk_{h}_{min(kt_dep, NT-1)}"]
            if npv6 >= 1:
                depe.append(f"pv6t_{npv6-1}")
            P.emit("vector", fpve, deps=tuple(depe), label=f"pv6e_{npv6}")
            def fpvt(e, sem, qts=qts):
                for j in range(len(qts)):
                    last = e.transpose(pb[7][0:64, j * 64:(j + 1) * 64].bitcast(BF16),
                                       attn_sc7[:, j, :], c_identb)
                return last
            dept = [f"pv6e_{npv6}"]
            if npv6 >= 1:
                dept.append(f"pv6c_{npv6-1}")
            P.emit("tensor", fpvt, deps=tuple(dept), label=f"pv6t_{npv6}")
            def fpvc(e, sem, qts=qts, grp=grp, rb=rb):
                for j, qt in enumerate(qts):
                    e.tensor_copy(s1f[rb:rb + 64, grp, qt * 128:(qt + 1) * 128],
                                  pb[7][0:64, j * 64:(j + 1) * 64].bitcast(BF16))
                return e.drain()
            P.emit("vector", fpvc, deps=(f"pv6t_{npv6}",), label=f"pv6c_{npv6}")
            npv6 += 1
        for kt in range(NT):
            qc0 = (kt * 128) // 512
            pts = PT if kt % 2 == 0 else PT2
            for qc in range(qc0, 4):
                b = nst6 % 4
                def fst(e, sem, kt=kt, qc=qc, b=b):
                    return e.matmul(pb[b][:], Kaug[0:65, kt * 128:(kt + 1) * 128],
                                    Qaug[0:65, qc * 512:(qc + 1) * 512], start=True, stop=True)
                deps = [f"aug_{h}"]
                if nst6 >= 4:
                    deps.append(f"stex_{nst6-4}")
                if h == 0 and nst6 < 4:
                    deps.append(f"rkev_{nrk-1}")
                P.emit("tensor", fst, deps=tuple(deps), label=f"st6_{nst6}")
                def fex(e, sem, qc=qc, b=b, pts=pts):
                    return e.activation(pts[:, qc * 512:(qc + 1) * 512], pb[b][:], AF.Exp)
                dex = [f"st6_{nst6}"]
                if kt > 1 and qc == qc0:
                    dex.append(f"pvk_{h}_{kt-2}")
                P.emit("scalar", fex, deps=tuple(dex), label=f"stex_{nst6}")
                nst6 += 1
            def fmsk(e, sem, kt=kt, pts=pts):
                e.tensor_tensor(pts[:, kt * 128:(kt + 1) * 128], pts[:, kt * 128:(kt + 1) * 128],
                                trim6, ALU.mult)
                return e.drain()
            P.emit("vector", fmsk, deps=(f"stex_{nst6-1}", "tri_loaded"), label=f"msk_{h}_{kt}")
            def fpvk(e, sem, h=h, kt=kt, pts=pts):
                for qt in range(kt, NT):
                    last = e.matmul(acc(qt), pts[:, qt * 128:(qt + 1) * 128],
                                    v65r[:, kt, h, :], start=(kt == 0), stop=(kt == qt))
                return last
            deps = [f"msk_{h}_{kt}", "v6_done"]
            if h > 0 and kt == 0:
                deps.append(f"pv6c_{npv6-1}")
            P.emit("tensor", fpvk, deps=tuple(deps), label=f"pvk_{h}_{kt}")
            if kt == 6:
                emit_evacs(list(range(0, 7)))
            elif kt == 13:
                emit_evacs(list(range(7, 14)))
            elif kt == 15:
                emit_evacs([14, 15])
    P.mark_here("vector", "s1f_done")

    wpj = A.alloc_at(pocket, [128, NC, D], BF16)
    g1t = A.alloc_at(pocket + 4096, [128, D], F32)
    sil6 = A.alloc_at(pocket + 5120, [128, 512], F32)
    cmb6s = A.alloc_at(pocket + 5632, [128, 512], F32)
    def ldwp(e, sem):
        e.dma_start(out=wpj, in_=wproj1.rearrange("(c p) o -> p c o", p=128)).then_inc(sem, 16)
    P.emit("gpsimd", ldwp, deps=("s1f_done",), dma=1, label="wpj_loaded")

    ncm6 = 0
    for i in range(NT):
        def fldx2(e, sem, i=i):
            e.dma_start(out=x1_tl, in_=x1_d.rearrange("p (a b) -> p a b", a=NT)[:, i, :]).then_inc(sem, 16)
            e.dma_start(out=g1t, in_=g_d.rearrange("p (a b) -> p a b", a=NT)[:, i, :]).then_inc(sem, 16)
        depl = ["g1_stored", "s1f_done"]
        if i > 0:
            depl.append(f"cmb6_{2*(i-1)+1}")
        P.emit("sync", fldx2, deps=tuple(depl), dma=2, label=f"xg6_{i}")
        for hf in range(2):
            b = (ncm6 % 2)
            def f(e, sem, i=i, hf=hf, b=b):
                for kc in range(NC):
                    last = e.matmul(pb[b][:], s1f[:, kc, i * 128:(i + 1) * 128],
                                    wpj[:, kc, hf * 512:(hf + 1) * 512],
                                    start=(kc == 0), stop=(kc == NC - 1))
                return last
            deps = [f"xg6_{i}", "wpj_loaded"]
            if ncm6 == 0:
                deps.append(f"rkmm_{nrk-1}")
            if ncm6 >= 2:
                deps.append(f"cmb6_{ncm6-2}")
            P.emit("tensor", f, deps=tuple(deps), label=f"s6mm_{ncm6}")
            def fsil(e, sem, i=i, hf=hf):
                return e.activation(sil6, g1t[:, hf * 512:(hf + 1) * 512], AF.Silu)
            P.emit("scalar", fsil, deps=(f"s6mm_{ncm6}",) + (() if ncm6 == 0 else (f"cmb6_{ncm6-1}",)), label=f"sil6_{ncm6}")
            def fcmb(e, sem, i=i, hf=hf, b=b):
                sl = slice(hf * 512, (hf + 1) * 512)
                e.tensor_tensor(cmb6s, pb[b][:], sil6, ALU.mult)
                e.drain()
                e.tensor_tensor(x1_tl[:, sl], x1_tl[:, sl], cmb6s, ALU.add)
                return e.drain()
            P.emit("vector", fcmb, deps=(f"sil6_{ncm6}",), label=f"cmb6_{ncm6}")
            ncm6 += 1
        def fstout(e, sem, i=i):
            e.dma_start(out=out.rearrange("(t p) d -> p t d", p=128)[:, i, :], in_=x1_tl).then_inc(sem, 16)
        P.emit("sync", fstout, deps=(f"cmb6_{ncm6-1}",), dma=1, label=f"out6_{i}")
    P.mark_here("sync", "out_done")

    P.build()
    return nc, es


def _finish_stub(P, out, xn_t):
    def st_out(e, sem):
        e.dma_start(out=out.rearrange("(t p) d -> p t d", p=128), in_=xn_t).then_inc(sem, 16)
    P.emit("sync", st_out, deps=(), dma=1, label="out_stored")



def make_consts():
    ident = np.eye(128, dtype=np.float32)
    bm = np.zeros((128, 3, 128), dtype=np.float32)
    k_idx = np.arange(128)[:, None]
    q_idx = np.arange(128)[None, :]
    for ri, rel in enumerate((-1, 0, 1)):
        bm[:, ri, :] = (np.abs(128 * rel + q_idx - k_idx) <= 3).astype(np.float32)
    return {
        "ident_f": ident,
        "bandmask": np.ascontiguousarray(bm.reshape(128, 384)),
    }


def make_in_map(inputs, core):
    m = {
        "x": np.ascontiguousarray(inputs["L0_attn_x"][core]),
        "w_qkv": inputs["L0_attn_w_qkv"],
        "proj": inputs["L0_attn_proj"],
        "w_out": inputs["L0_attn_w_out"],
        "b_out": inputs["L0_attn_b_out"].reshape(1, -1),
        "a_range": inputs["L0_ffn_adaptive_range"].reshape(1, 1),
    }
    for k in ("cn_w1", "cn_w2", "cn_w3", "gc_w1", "gc_w2"):
        m[k] = inputs[f"L0_ffn_{k}"]
    m["cn_w4"] = np.ascontiguousarray(inputs["L0_ffn_cn_w4"].reshape(2, 128).T)
    m["gc_w3"] = np.ascontiguousarray(inputs["L0_ffn_gc_w3"].reshape(2, 128).T)
    for k in ("cn_b1", "cn_b2", "cn_b3", "gc_b1", "gc_b2"):
        m[k] = np.ascontiguousarray(inputs[f"L0_ffn_{k}"].reshape(-1, 128).T)
    for k in ("cn_b4", "gc_b3"):
        m[k] = inputs[f"L0_ffn_{k}"].reshape(1, 1)
    m["wqkv1"] = inputs["L1_attn_Wqkv"]
    m["wproj1"] = inputs["L1_attn_Wproj"]
    m["gamma"] = inputs["L1_ffn_gamma"].reshape(1, -1)
    m["wqT"] = np.ascontiguousarray(inputs["L1_ffn_wq"].T)
    k = inputs["L1_ffn_keys"].transpose(0, 2, 3, 1).reshape(8, 2, 4, 128, 256)
    m["keysT"] = np.ascontiguousarray(k.transpose(3, 0, 1, 2, 4).reshape(128, 64 * 256))
    m["tab_du"] = np.ascontiguousarray(
        np.concatenate([inputs["L1_ffn_table_down"], inputs["L1_ffn_table_up"]], axis=1))
    dh = 64
    inv = 1.0 / (10000.0 ** (np.arange(0, dh, 2, dtype=np.float64) / dh))
    fr = np.arange(T, dtype=np.float64)[:, None] * inv[None, :]
    cosf = np.repeat(np.cos(fr), 2, axis=-1).T
    sinf = np.repeat(np.sin(fr), 2, axis=-1).T
    m["cosq8"] = np.tile(cosf / 8.0, (2, 1)).astype(np.float32)
    m["sinq8"] = np.tile(sinf / 8.0, (2, 1)).astype(np.float32)
    m["cosk"] = np.tile(cosf, (2, 1)).astype(np.float32)
    m["sink"] = np.tile(sinf, (2, 1)).astype(np.float32)
    kq = np.arange(128)
    m["trimask"] = (kq[:, None] <= kq[None, :]).astype(np.float32)
    m.update(make_consts())
    return m


def kernel(**inputs):
    nc, _es = build_program(nphase=6)
    in_maps = [make_in_map(inputs, c) for c in range(8)]
    res = run_bass_kernel_spmd(nc, in_maps, core_ids=list(range(8)))
    return np.stack([np.asarray(res.results[c]["out"]) for c in range(8)], axis=0)


if __name__ == "__main__":
    nc, _ = build_program(nphase=0)
    print("built phase0 ok")


# version 18
# speedup vs baseline: 1.0149x; 1.0060x over previous
import os
import numpy as np
import concourse.bass as bass
import concourse.mybir as mybir
from concourse.bass_utils import run_bass_kernel_spmd
from concourse.bass import IndirectOffsetOnAxis
from contextlib import ExitStack
from math import prod

F32 = mybir.dt.float32
BF16 = mybir.dt.bfloat16
U16 = mybir.dt.uint16
I32 = mybir.dt.int32
AF = mybir.ActivationFunctionType
ALU = mybir.AluOpType
AX = mybir.AxisListType

T = 2048
D = 1024
NT = T // 128
NC = D // 128
DTSZ = {F32: 4, BF16: 2, U16: 2, I32: 4}


class Prog:
    ENGS = ("sync", "gpsimd", "vector", "scalar", "tensor")

    def __init__(self, nc, es):
        self.nc = nc
        self.plan = {e: [] for e in self.ENGS}
        self.sem = {e: es.enter_context(nc.semaphore(f"S{e}")) for e in self.ENGS}
        self.cnt = {e: 0 for e in self.ENGS}
        self.marks = {}

    def emit(self, eng, fn, deps=(), label=None, dma=0):
        waits = tuple(self.marks[d] for d in deps)
        if dma:
            self.cnt[eng] += 16 * dma
        elif label:
            self.cnt[eng] += 1
        if label:
            self.marks[label] = (eng, self.cnt[eng])
        self.plan[eng].append((fn, waits, (label is not None and not dma), dma))

    def mark_here(self, eng, label):
        self.marks[label] = (eng, self.cnt[eng])

    def barrier(self, name):
        cur = {e: (e, self.cnt[e]) for e in self.ENGS}
        for e in self.ENGS:
            self.marks[f"__b_{name}_{e}"] = cur[e]
        for e in self.ENGS:
            deps = tuple(f"__b_{name}_{x}" for x in self.ENGS if x != e)
            def f(en, sem):
                return en.nop()
            self.emit(e, f, deps=deps, label=f"{name}_{e}")

    def build(self):
        nc = self.nc
        with nc.Block() as block:
            for eng in self.ENGS:
                def make(eng):
                    def body(e):
                        for fn, waits, inc1, dma in self.plan[eng]:
                            for weng, val in waits:
                                e.wait_ge(self.sem[weng], val)
                            last = fn(e, self.sem[eng])
                            if inc1:
                                assert last is not None, f"label group on {eng} returned None"
                                last.then_inc(self.sem[eng], 1)
                        e.wait_ge(self.sem[eng], self.cnt[eng])
                    return body
                getattr(block, eng)(make(eng))


class Arena:

    def __init__(self, nc, es, words):
        self.t = es.enter_context(nc.sbuf_tensor("arena", [128, words], F32))
        self.words = words
        self.off = 0

    def alloc(self, shape, dt=F32, rows=128):
        assert shape[0] == rows
        elems = prod(shape[1:])
        w = (elems * DTSZ[dt] + 3) // 4
        assert self.off + w <= self.words, f"arena overflow: {self.off + w} > {self.words}"
        ap = self.t[:, self.off:self.off + w]
        self.off += w
        if dt != F32:
            ap = ap.bitcast(dt)
        n = elems * (4 * w == elems * DTSZ[dt] or True)
        ap = ap[:, 0:elems] if ap.shape[1] != elems else ap
        if len(shape) > 2:
            ap = ap.rearrange("p (" + " ".join(f"a{i}" for i in range(len(shape) - 1)) + ") -> p " +
                              " ".join(f"a{i}" for i in range(len(shape) - 1)),
                              **{f"a{i}": shape[i + 1] for i in range(len(shape) - 1)})
        if rows != 128:
            ap = ap[0:rows]
        return ap

    def alloc_at(self, off, shape, dt=F32, rows=128):
        save = self.off
        self.off = off
        ap = self.alloc(shape, dt, rows)
        self.off = max(save, self.off) if False else save
        return ap

    def mark(self):
        return self.off

    def reset(self, m):
        self.off = m


def build_program(nphase=99, debug=False):
    nc = bass.Bass("TRN2", target_bir_lowering=False)
    es = ExitStack()

    def din(name, shape, dt=F32):
        return nc.dram_tensor(name, shape, dt, kind="ExternalInput").ap()

    def dscratch(name, shape, dt=F32):
        kind = "ExternalOutput" if debug else "Internal"
        return nc.dram_tensor(name, shape, dt, kind=kind).ap()

    x_in = din("x", [T, D])
    w_qkv = din("w_qkv", [D, 3 * D])
    proj_in = din("proj", [128, 64])
    w_out = din("w_out", [D, D])
    b_out = din("b_out", [1, D])
    cn_w1 = din("cn_w1", [D, D]); cn_b1 = din("cn_b1", [128, 8])
    cn_w2 = din("cn_w2", [D, 512]); cn_b2 = din("cn_b2", [128, 4])
    cn_w3 = din("cn_w3", [512, 256]); cn_b3 = din("cn_b3", [128, 2])
    cn_w4 = din("cn_w4", [128, 2]); cn_b4 = din("cn_b4", [1, 1])
    gc_w1 = din("gc_w1", [D, 512]); gc_b1 = din("gc_b1", [128, 4])
    gc_w2 = din("gc_w2", [512, 256]); gc_b2 = din("gc_b2", [128, 2])
    gc_w3 = din("gc_w3", [128, 2]); gc_b3 = din("gc_b3", [1, 1])
    a_range = din("a_range", [1, 1])
    ident_f = din("ident_f", [128, 128])
    bandmask = din("bandmask", [128, 3 * 128])
    wqkv1 = din("wqkv1", [D, 3 * D])
    wproj1 = din("wproj1", [D, D])
    gamma = din("gamma", [1, D])
    wqT = din("wqT", [D, 8192])
    keysT = din("keysT", [128, 64 * 256])
    tab_du = din("tab_du", [65536, 2 * D])
    cosq8 = din("cosq8", [128, T]); sinq8 = din("sinq8", [128, T])
    cosk = din("cosk", [128, T]); sink = din("sink", [128, T])
    trimask = din("trimask", [128, 128])

    out = nc.dram_tensor("out", [T, D], F32, kind="ExternalOutput").ap()

    xn_d = dscratch("xn_d", [128, NT * D])
    xnbf_f_d = dscratch("xnbf_f_d", [128, NC * T], U16)
    v_d = dscratch("v_d", [128, NT * D], U16)
    attnf_d = dscratch("attnf_d", [128, NC * T], U16)
    g_d = dscratch("g_d", [128, NT * D])
    x1_d = dscratch("x1_d", [128, NT * D])
    gf_d = dscratch("gf_d", [1, 1])
    tabdu_bf = dscratch("tabdu_bf", [65536, 2 * D], U16)
    dbg1 = dscratch("dbg1", [128, 16 * 16 * 16])
    dbg2 = dscratch("dbg2", [128, 16 * 16 * 16], U16)
    dbg3 = dscratch("dbg3", [128, 16 * 8 * 16])
    dbg4 = dscratch("dbg4", [128, 16 * 8 * 16], I32)

    P = Prog(nc, es)
    A = Arena(nc, es, 51000)

    pb = [es.enter_context(nc.psum_tensor(f"pb{i}", [128, 512], F32)) for i in range(8)]
    def pbf(i):
        return pb[i][:, 0:64].bitcast(BF16)

    c_identb = A.alloc([128, 128], BF16)
    c_identf = A.alloc([128, 128], F32)
    gf_sb = A.alloc([1, 1], F32, rows=1)
    ln1_rstd = A.alloc([128, NT], F32)
    ln1_nmr = A.alloc([128, NT], F32)
    base_min = A.mark()
    xn_t = A.alloc([128, NT, D], F32)
    m_xn = A.mark()
    fmaj_bf = A.alloc([128, NC, T], BF16)
    cast_scr = A.alloc([128, D], BF16)
    x_tile = A.alloc([128, D], F32)

    def ld_consts(e, sem):
        e.dma_start(out=c_identf, in_=ident_f).then_inc(sem, 16)
    P.emit("sync", ld_consts, dma=1, label="consts0")
    def cast_ident(e, sem):
        e.tensor_copy(c_identb, c_identf)
        return e.drain()
    P.emit("vector", cast_ident, deps=("consts0",), label="consts")

    ph_base = A.mark()

    st_sum = A.alloc([128, NT]); st_sq = A.alloc([128, NT])
    st_mean = A.alloc([128, NT]); st_var = A.alloc([128, NT])
    st_rstd = A.alloc([128, NT]); st_nmr = A.alloc([128, NT])
    sq_scr = A.alloc([128, D])

    for i in range(NT):
        def fx(e, sem, i=i):
            e.dma_start(out=x_tile, in_=x_in.rearrange("(t p) d -> p t d", p=128)[:, i, :]).then_inc(sem, 16)
        P.emit("sync", fx, deps=() if i == 0 else (f"sq_{i-1}",), dma=1, label=f"xld_{i}")
        def f(e, sem, i=i):
            return e.activation(sq_scr, x_tile, AF.Square)
        P.emit("scalar", f, deps=(f"xld_{i}",) + (() if i == 0 else (f"sq_{i-1}",)), label=f"sqdone_{i}")
        def f2(e, sem, i=i):
            e.reduce_sum(st_sum[:, i:i + 1], x_tile, axis=AX.X)
            e.reduce_sum(st_sq[:, i:i + 1], sq_scr, axis=AX.X)
            return e.tensor_copy(xn_t[:, i, :], x_tile)
        P.emit("vector", f2, deps=(f"sqdone_{i}",), label=f"sq_{i}")

    def stats(e, sem):
        e.tensor_scalar_mul(st_mean, st_sum, 1.0 / D)
        e.drain()
        e.tensor_scalar_mul(st_var, st_sq, 1.0 / D)
        e.drain()
        e.tensor_tensor(st_nmr, st_mean, st_mean, ALU.mult)
        e.drain()
        e.tensor_sub(st_var, st_var, st_nmr)
        e.drain()
        return e.tensor_scalar_add(st_var, st_var, 1e-5)
    P.emit("vector", stats, deps=(f"sq_{NT-1}",), label="var_done")
    def rstd1(e, sem):
        return e.activation(st_rstd, st_var, AF.Sqrt)
    P.emit("scalar", rstd1, deps=("var_done",), label="sqrt_done")
    def rstd2(e, sem):
        e.reciprocal(st_rstd, st_rstd)
        e.drain()
        e.tensor_tensor(st_nmr, st_mean, st_rstd, ALU.mult)
        e.drain()
        return e.tensor_scalar_mul(st_nmr, st_nmr, -1.0)
    P.emit("vector", rstd2, deps=("sqrt_done",), label="rstd_done")

    ntp = 0
    for i in range(NT):
        def f(e, sem, i=i):
            e.tensor_scalar(xn_t[:, i, :], xn_t[:, i, :], st_rstd[:, i:i + 1],
                            st_nmr[:, i:i + 1], ALU.mult, ALU.add)
            e.drain()
            e.tensor_copy(cast_scr, xn_t[:, i, :])
            return e.drain()
        P.emit("vector", f, deps=("rstd_done",) if i == 0 else (f"tpev_{ntp-1}",), label=f"xn_{i}")
        TPB0 = (2, 3, 6, 7)
        for c in range(NC):
            b = TPB0[ntp % 4]
            def ftp(e, sem, c=c, b=b):
                return e.transpose(pbf(b), cast_scr[:, c * 128:(c + 1) * 128], c_identb)
            deps = ["consts", f"xn_{i}"]
            if ntp >= 4:
                deps.append(f"tpev_{ntp-4}")
            P.emit("tensor", ftp, deps=tuple(deps), label=f"tp_{ntp}")
            def fev(e, sem, i=i, c=c, b=b):
                return e.tensor_copy(fmaj_bf[:, c, i * 128:(i + 1) * 128], pbf(b))
            P.emit("vector", fev, deps=(f"tp_{ntp}",), label=f"tpev_{ntp}")
            ntp += 1
    P.mark_here("vector", "xnbf_f_done")
    P.mark_here("vector", "xn_all")

    def st_xn(e, sem):
        e.dma_start(out=xn_d, in_=xn_t.rearrange("p a b -> p (a b)")).then_inc(sem, 16)
        e.dma_start(out=xnbf_f_d, in_=fmaj_bf.rearrange("p a b -> p (a b)").bitcast(U16)).then_inc(sem, 16)
    P.emit("sync", st_xn, deps=("xn_all", "xnbf_f_done"), dma=2, label="xn_stored")

    gcw1_bf = A.alloc([128, NC, 512], BF16)
    gcw2_bf = A.alloc([128, 4, 256], BF16)
    gcw3_bf = A.alloc([128, 2, 1], BF16)
    gcb1_t = A.alloc([128, 4]); gcb2_t = A.alloc([128, 2]); gcb3_t = A.alloc([1, 1], rows=1)
    gc_bf = A.alloc([128, NC], BF16)
    gc_col = A.alloc([128, NC])
    g1_col = A.alloc([128, 4]); g1_bf = A.alloc([128, 4], BF16)
    g2_col = A.alloc([128, 2]); g2_bf = A.alloc([128, 2], BF16)

    def ld_gcw(e, sem):
        e.dma_start(out=gcw1_bf, in_=gc_w1.rearrange("(c p) o -> p c o", p=128)).then_inc(sem, 16)
        e.dma_start(out=gcw2_bf, in_=gc_w2.rearrange("(c p) o -> p c o", p=128)).then_inc(sem, 16)
        e.dma_start(out=gcw3_bf, in_=gc_w3.rearrange("p (c o) -> p c o", o=1)).then_inc(sem, 16)
        e.dma_start(out=gcb1_t, in_=gc_b1).then_inc(sem, 16)
        e.dma_start(out=gcb2_t, in_=gc_b2).then_inc(sem, 16)
        e.dma_start(out=gcb3_t, in_=gc_b3).then_inc(sem, 16)
    P.emit("gpsimd", ld_gcw, dma=6, label="gcw_loaded")

    def gc_mean(e, sem):
        for c in range(NC):
            e.reduce_sum(gc_col[:, c:c + 1], fmaj_bf[:, c, :], axis=AX.X)
        e.drain()
        e.tensor_scalar_mul(gc_col, gc_col, 1.0 / T)
        e.drain()
        e.tensor_copy(gc_bf, gc_col)
        return e.drain()
    P.emit("vector", gc_mean, deps=("xnbf_f_done",), label="gc_done")

    p_gc = pb[5][:, 0:4]
    p_gf = pb[5][0:1, 4:5]
    for oc in range(4):
        def f(e, sem, oc=oc):
            for kc in range(NC):
                last = e.matmul(p_gc[:, oc:oc + 1], gcw1_bf[:, kc, oc * 128:(oc + 1) * 128],
                                gc_bf[:, kc:kc + 1], start=(kc == 0), stop=(kc == NC - 1))
            return last
        P.emit("tensor", f, deps=("gcw_loaded", "gc_done") if oc == 0 else (f"gfh1_{oc-1}",), label=f"gfh1_{oc}")
        def fe(e, sem, oc=oc):
            return e.activation(g1_col[:, oc:oc + 1], p_gc[:, oc:oc + 1], AF.Gelu,
                                bias=gcb1_t[:, oc:oc + 1])
        P.emit("scalar", fe, deps=(f"gfh1_{oc}",), label=f"gfh1e_{oc}")
    def f2c(e, sem):
        e.tensor_copy(g1_bf, g1_col)
        return e.drain()
    P.emit("vector", f2c, deps=tuple(f"gfh1e_{oc}" for oc in range(4)), label="g1bf")
    for oc in range(2):
        def f(e, sem, oc=oc):
            for kc in range(4):
                last = e.matmul(p_gc[:, oc:oc + 1], gcw2_bf[:, kc, oc * 128:(oc + 1) * 128],
                                g1_bf[:, kc:kc + 1], start=(kc == 0), stop=(kc == 3))
            return last
        P.emit("tensor", f, deps=("g1bf",) if oc == 0 else (f"gfh2_{oc-1}",), label=f"gfh2_{oc}")
        def fe(e, sem, oc=oc):
            return e.activation(g2_col[:, oc:oc + 1], p_gc[:, oc:oc + 1], AF.Gelu,
                                bias=gcb2_t[:, oc:oc + 1])
        P.emit("scalar", fe, deps=(f"gfh2_{oc}",), label=f"gfh2e_{oc}")
    def f3c(e, sem):
        e.tensor_copy(g2_bf, g2_col)
        return e.drain()
    P.emit("vector", f3c, deps=("gfh2e_0", "gfh2e_1"), label="g2bf")
    def f4(e, sem):
        for kc in range(2):
            last = e.matmul(p_gf, gcw3_bf[0:128, kc, 0:1], g2_bf[:, kc:kc + 1],
                            start=(kc == 0), stop=(kc == 1))
        return last
    P.emit("tensor", f4, deps=("g2bf",), label="gf_mm")
    def f4e(e, sem):
        return e.activation(gf_sb, p_gf, AF.Sigmoid, bias=gcb3_t)
    P.emit("scalar", f4e, deps=("gf_mm",), label="gf_done")
    def st_gf(e, sem):
        e.dma_start(out=gf_d, in_=gf_sb).then_inc(sem, 16)
    P.emit("sync", st_gf, deps=("gf_done",), dma=1, label="gf_stored")

    if nphase < 1:
        _finish_stub(P, out, xn_t)
        P.build()
        return nc, es

    P.barrier("p1")
    A.reset(ph_base)

    c_band = A.alloc([128, 3 * 128], F32)
    proj_bf = A.alloc([128, 64], BF16)
    def ld_p1a(e, sem):
        e.dma_start(out=c_band, in_=bandmask).then_inc(sem, 16)
        e.dma_start(out=proj_bf, in_=proj_in).then_inc(sem, 16)
        e.dma_start(out=wv_bf, in_=w_qkv.rearrange("(c p) o -> p c o", p=128)[:, :, 2 * D:3 * D]).then_inc(sem, 16)
    P.emit("gpsimd", ld_p1a, deps=("p1_gpsimd",), dma=3, label="p1w_loaded")

    v_base = A.mark()
    wv_bf = A.alloc([128, NC, D], BF16)
    v_tile = A.alloc([128, D], BF16)
    nv = 0
    for i in range(NT):
        for hf in range(2):
            b = nv % 2
            def f(e, sem, i=i, hf=hf, b=b):
                for kc in range(NC):
                    last = e.matmul(pb[b][:], fmaj_bf[:, kc, i * 128:(i + 1) * 128],
                                    wv_bf[:, kc, hf * 512:(hf + 1) * 512],
                                    start=(kc == 0), stop=(kc == NC - 1))
                return last
            deps = ["p1w_loaded", "p1_tensor"]
            if nv >= 2:
                deps.append(f"vev_{nv-2}")
            P.emit("tensor", f, deps=tuple(deps), label=f"vmm_{nv}")
            def fe(e, sem, i=i, hf=hf, b=b):
                return e.tensor_copy(v_tile[:, hf * 512:(hf + 1) * 512], pb[b][:])
            depv = [f"vmm_{nv}"]
            if hf == 0 and i > 0:
                depv.append(f"vst_{i-1}")
            P.emit("vector", fe, deps=tuple(depv), label=f"vev_{nv}")
            nv += 1
        def fst(e, sem, i=i):
            e.dma_start(out=v_d.rearrange("p (t d) -> p t d", t=NT)[:, i, :], in_=v_tile.bitcast(U16)).then_inc(sem, 16)
        P.emit("sync", fst, deps=(f"vev_{nv-1}",), dma=1, label=f"vst_{i}")
        if i < NT - 1:
            P.marks[f"vwar_{i}"] = P.marks[f"vst_{i}"]
    for i in range(NT - 1):
        pass

    A.reset(v_base)
    wqk_bf = A.alloc([128, NC, 4 * 128], BF16)
    qk_bf = A.alloc([128, 2, T], BF16)
    qphi = A.alloc([128, T], F32)
    kphi = A.alloc([128, T], F32)
    qphi_bf = A.alloc([128, T], BF16)
    kphi_bf = A.alloc([128, T], BF16)
    ksum = A.alloc([128, T], F32)
    zrow = A.alloc([128, T], F32, rows=128)
    invzT = A.alloc([128, NT, 2], F32)
    phi_min = A.alloc([128, 512], F32)
    phi_max = A.alloc([128, 512], F32)
    phi_exp = A.alloc([128, 512], F32)
    stb = A.alloc([128, 3, 384], BF16)
    attn_sc = A.alloc([128, 3, 128], BF16)
    attn_head = A.alloc([128, 2, T], BF16)
    vh_bf = A.alloc([128, 2, NT, 128], BF16)
    blk1 = A.alloc([128, 2], F32)

    def fb1(e, sem):
        e.memset(blk1, 0.0)
        e.drain()
        e.memset(blk1[0:64, 0:1], 1.0)
        e.drain()
        e.memset(blk1[64:128, 1:2], 1.0)
        return e.drain()
    P.emit("vector", fb1, deps=("p1_vector",), label="blk1_done")

    nqk = 0
    nst = 0
    npv = 0
    nbv = 0
    ntz = 0
    for ph in range(4):
        def ldw(e, sem, ph=ph):
            w3 = w_qkv.rearrange("(c p) o -> p c o", p=128)
            e.dma_start(out=wqk_bf[:, :, 0:256], in_=w3[:, :, ph * 256:(ph + 1) * 256]).then_inc(sem, 16)
            e.dma_start(out=wqk_bf[:, :, 256:512], in_=w3[:, :, D + ph * 256:D + (ph + 1) * 256]).then_inc(sem, 16)
        deps = ("p1_gpsimd", f"vst_{NT-1}") if ph == 0 else (f"phibf_{ph-1}",)
        P.emit("gpsimd", ldw, deps=deps, dma=2, label=f"wqk_{ph}")

        for qk in range(2):
            for hh in range(2):
                for tc in range(4):
                    b = nqk % 2
                    def f(e, sem, qk=qk, hh=hh, tc=tc, b=b):
                        for kc in range(NC):
                            last = e.matmul(pb[b][:], wqk_bf[:, kc, qk * 256 + hh * 128: qk * 256 + (hh + 1) * 128],
                                            fmaj_bf[:, kc, tc * 512:(tc + 1) * 512],
                                            start=(kc == 0), stop=(kc == NC - 1))
                        return last
                    deps = [f"wqk_{ph}"]
                    if nqk == 0:
                        deps.append(f"vev_{nv-1}")
                    if nqk >= 2:
                        deps.append(f"qkpost_{nqk-2}")
                    if npv >= 1 and qk == 0 and hh == 0 and tc == 0:
                        deps.append(f"pvevB_{nbv-1}")
                    P.emit("tensor", f, deps=tuple(deps), label=f"qkmm_{nqk}")
                    def fe(e, sem, hh=hh, tc=tc, b=b):
                        return e.tensor_copy(qk_bf[:, hh, tc * 512:(tc + 1) * 512], pb[b][:])
                    P.emit("vector", fe, deps=(f"qkmm_{nqk}",), label=f"qkpost_{nqk}")
                    nqk += 1
            phit = qphi if qk == 0 else kphi
            for tc in range(4):
                b = nqk % 2
                def f(e, sem, tc=tc, b=b):
                    e.matmul(pb[b][0:64, :], proj_bf[:, 0:64], qk_bf[:, 0, tc * 512:(tc + 1) * 512],
                             start=True, stop=True)
                    return e.matmul(pb[b][64:128, :], proj_bf[:, 0:64], qk_bf[:, 1, tc * 512:(tc + 1) * 512],
                                    start=True, stop=True)
                deps = [f"qkpost_{nqk-1}"]
                if nqk >= 2:
                    deps.append(f"qkpost_{nqk-2}")
                P.emit("tensor", f, deps=tuple(deps), label=f"qkmm_{nqk}")
                def fel(e, sem, b=b):
                    e.tensor_scalar_min(phi_min, pb[b][:], 0.0)
                    e.tensor_scalar_max(phi_max, pb[b][:], 0.0)
                    return e.drain()
                P.emit("vector", fel, deps=(f"qkmm_{nqk}",), label=f"qkpost_{nqk}_a")
                def fes(e, sem):
                    return e.activation(phi_exp, phi_min, AF.Exp)
                P.emit("scalar", fes, deps=(f"qkpost_{nqk}_a",), label=f"qkpost_{nqk}_b")
                def fea(e, sem, phit=phit, tc=tc):
                    e.tensor_add(phit[:, tc * 512:(tc + 1) * 512], phi_exp, phi_max)
                    return e.drain()
                P.emit("vector", fea, deps=(f"qkpost_{nqk}_b",), label=f"qkpost_{nqk}")
                nqk += 1
        def fcast(e, sem):
            e.tensor_copy(qphi_bf, qphi)
            e.tensor_copy(kphi_bf, kphi)
            return e.drain()
        P.emit("vector", fcast, deps=(f"qkpost_{nqk-1}",), label=f"phibf_{ph}")

        def fks(e, sem):
            e.tensor_copy(ksum, kphi)
            e.drain()
            for o in (-3, -2, -1, 1, 2, 3):
                if o > 0:
                    e.tensor_add(ksum[:, 0:T - o], ksum[:, 0:T - o], kphi[:, o:T])
                else:
                    e.tensor_add(ksum[:, -o:T], ksum[:, -o:T], kphi[:, 0:T + o])
                e.drain()
            e.tensor_tensor(ksum, ksum, qphi, ALU.mult)
            return e.drain()
        P.emit("vector", fks, deps=(f"phibf_{ph}",), label=f"ks_{ph}")

        for tc in range(4):
            def fz(e, sem, tc=tc):
                return e.matmul(pb[4][0:2, :], blk1, ksum[:, tc * 512:(tc + 1) * 512],
                                start=True, stop=True)
            deps = ["blk1_done", f"ks_{ph}"]
            if not (ph == 0 and tc == 0):
                deps.append(f"zev_{ph}_{tc-1}" if tc > 0 else f"zev_{ph-1}_3")
            if ph == 0 and tc == 0:
                deps.append("gf_done")
            P.emit("tensor", fz, deps=tuple(deps), label=f"zmm_{ph}_{tc}")
            def fzev(e, sem, tc=tc):
                return e.tensor_copy(zrow[0:2, tc * 512:(tc + 1) * 512], pb[4][0:2, :])
            P.emit("vector", fzev, deps=(f"zmm_{ph}_{tc}",), label=f"zev_{ph}_{tc}")

        def finvz(e, sem):
            e.tensor_scalar_add(zrow[0:2, :], zrow[0:2, :], 1e-8)
            e.drain()
            e.reciprocal(zrow[0:2, :], zrow[0:2, :])
            return e.drain()
        P.emit("vector", finvz, deps=(f"zev_{ph}_3",), label=f"invz_{ph}")

        for qt in range(NT):
            def ftz(e, sem, qt=qt):
                return e.transpose(pb[4][0:128, 256:258], zrow[0:2, qt * 128:(qt + 1) * 128], c_identf[0:2, 0:2])
            deps = [f"invz_{ph}"]
            if ntz >= 1:
                deps.append(f"tzev_{ntz-1}")
            P.emit("tensor", ftz, deps=tuple(deps), label=f"tz_{ntz}")
            def ftzev(e, sem, qt=qt):
                return e.tensor_copy(invzT[:, qt, :], pb[4][0:128, 256:258])
            P.emit("vector", ftzev, deps=(f"tz_{ntz}",), label=f"tzev_{ntz}")
            ntz += 1
        P.mark_here("vector", f"invzT_{ph}")

        def ldvh(e, sem, ph=ph):
            v3 = v_d.rearrange("p (t d) -> p t d", t=NT)
            e.dma_start(out=vh_bf[:, 0, :, :].bitcast(U16), in_=v3[:, :, (2 * ph) * 128:(2 * ph + 1) * 128]).then_inc(sem, 16)
            e.dma_start(out=vh_bf[:, 1, :, :].bitcast(U16), in_=v3[:, :, (2 * ph + 1) * 128:(2 * ph + 2) * 128]).then_inc(sem, 16)
        deps = [f"vst_{NT-1}"]
        if ph > 0:
            deps.append(f"pvdone_{ph-1}")
        P.emit("gpsimd", ldvh, deps=tuple(deps), dma=2, label=f"vh_{ph}")

        for hh in range(2):
            rb = hh * 64
            def emit_st(kt):
                nonlocal nst
                qts = [q for q in (kt - 1, kt, kt + 1) if 0 <= q < NT]
                b = nst % 2
                def fst(e, sem, rb=rb, kt=kt, qts=qts, b=b):
                    first = True
                    for q in qts:
                        reg = q - kt + 1
                        last = e.matmul(pb[2 + b][:, 0:384][:, reg * 128:(reg + 1) * 128],
                                        kphi_bf[rb:rb + 64, kt * 128:(kt + 1) * 128],
                                        qphi_bf[rb:rb + 64, q * 128:(q + 1) * 128],
                                        start=first, stop=(q == qts[-1]))
                        first = False
                    return last
                deps = [f"phibf_{ph}", f"invzT_{ph}"]
                if nst >= 2:
                    deps.append(f"stev_{nst-2}")
                if npv >= 1 and kt >= 3:
                    deps.append(f"pvmm_{npv-1}")
                P.emit("tensor", fst, deps=tuple(deps), label=f"stmm_{nst}")
                def fstev(e, sem, kt=kt, qts=qts, b=b):
                    r0 = qts[0] - kt + 1
                    r1 = qts[-1] - kt + 2
                    return e.tensor_tensor(stb[:, kt % 3, r0 * 128:r1 * 128],
                                           pb[2 + b][:, 0:384][:, r0 * 128:r1 * 128],
                                           c_band[:, r0 * 128:r1 * 128], ALU.mult)
                P.emit("vector", fstev, deps=(f"stmm_{nst}",), label=f"stev_{nst}")
                nst += 1

            def emit_pvmm(qt):
                nonlocal npv
                kts = [k for k in (qt - 1, qt, qt + 1) if 0 <= k < NT]
                b = npv % 2
                def fpv(e, sem, hh=hh, qt=qt, kts=kts, b=b):
                    first = True
                    for k in kts:
                        reg = qt - k + 1
                        last = e.matmul(pb[b][:, 0:128], stb[:, k % 3, reg * 128:(reg + 1) * 128],
                                        vh_bf[:, hh, k, :],
                                        start=first, stop=(k == kts[-1]))
                        first = False
                    return last
                deps = [f"stev_{nst-1}", f"vh_{ph}", f"qkpost_{nqk-1}"]
                if npv >= 2:
                    deps.append(f"pvevB_{(npv-2)//2}")
                P.emit("tensor", fpv, deps=tuple(deps), label=f"pvmm_{npv}")
                npv += 1

            def emit_evac2(q0, q1):
                nonlocal nbv
                def fev(e, sem, hh=hh, q0=q0, q1=q1):
                    e.tensor_scalar_mul(attn_sc[:, 0, :], pb[0][:, 0:128], invzT[:, q0, hh:hh + 1])
                    e.tensor_scalar_mul(attn_sc[:, 1, :], pb[1][:, 0:128], invzT[:, q1, hh:hh + 1])
                    return e.drain()
                depv = [f"pvmm_{npv-1}"]
                if nbv >= 1:
                    depv.append(f"pvtpB_{nbv-1}")
                P.emit("vector", fev, deps=tuple(depv), label=f"pvevB_{nbv}")
                def ftp(e, sem):
                    e.transpose(pbf(5), attn_sc[:, 0, :], c_identb)
                    return e.transpose(pbf(6), attn_sc[:, 1, :], c_identb)
                dept = [f"pvevB_{nbv}"]
                if nbv == 0:
                    dept.append("gf_stored")
                P.emit("tensor", ftp, deps=tuple(dept), label=f"pvtpB_{nbv}")
                def fcp(e, sem, hh=hh, q0=q0, q1=q1):
                    e.tensor_copy(attn_head[:, hh, q0 * 128:(q0 + 1) * 128], pbf(5))
                    e.tensor_copy(attn_head[:, hh, q1 * 128:(q1 + 1) * 128], pbf(6))
                    return e.drain()
                P.emit("vector", fcp, deps=(f"pvtpB_{nbv}",), label=f"pvtevB_{nbv}")
                nbv += 1

            for kt in range(NT):
                emit_st(kt)
                if kt >= 1:
                    emit_pvmm(kt - 1)
                    if (kt - 1) % 2 == 1:
                        emit_evac2(kt - 2, kt - 1)
            emit_pvmm(NT - 1)
            emit_evac2(NT - 2, NT - 1)
        P.mark_here("vector", f"pvdone_{ph}")
        def sthead(e, sem, ph=ph):
            a3 = attnf_d.rearrange("p (c t) -> p c t", c=NC)
            e.dma_start(out=a3[:, 2 * ph, :], in_=attn_head[:, 0, :].bitcast(U16)).then_inc(sem, 16)
            e.dma_start(out=a3[:, 2 * ph + 1, :], in_=attn_head[:, 1, :].bitcast(U16)).then_inc(sem, 16)
        P.emit("sync", sthead, deps=(f"pvdone_{ph}",), dma=2, label=f"ah_{ph}")
    P.mark_here("sync", "attn_stored")

    if nphase < 2:
        _finish_stub(P, out, xn_t)
        P.build()
        return nc, es

    P.barrier("p2")
    A.reset(ph_base)

    cnw1_bf = A.alloc([128, NC, D], BF16)
    cnw2_bf = A.alloc([128, NC, 512], BF16)
    cnw3_bf = A.alloc([128, 4, 256], BF16)
    cnw4_bf = A.alloc([128, 2, 1], BF16)
    cnb1_t = A.alloc([128, NC]); cnb2_t = A.alloc([128, 4])
    cnb3_t = A.alloc([128, 2]); cnb4_t = A.alloc([1, 1], rows=1)
    ar_sb = A.alloc([1, 1], rows=1)
    c0_sb = A.alloc([1, 1], rows=1); c1_sb = A.alloc([1, 1], rows=1)
    tt1 = A.alloc([1, 1], rows=1)
    h1_bf = A.alloc([128, NC, T], BF16)
    h2_bf = A.alloc([128, 4, T], BF16)
    h3_bf = A.alloc([128, 2, T], BF16)
    cp_row = A.alloc([1, T], rows=1)
    cpT = A.alloc([128, NT]); cpTn = A.alloc([128, NT])
    tw_tmp = A.alloc([128, 2, 256])

    def ld_cnw(e, sem):
        e.dma_start(out=cnw1_bf, in_=cn_w1.rearrange("(c p) o -> p c o", p=128)).then_inc(sem, 16)
        e.dma_start(out=cnw2_bf, in_=cn_w2.rearrange("(c p) o -> p c o", p=128)).then_inc(sem, 16)
        e.dma_start(out=cnw3_bf, in_=cn_w3.rearrange("(c p) o -> p c o", p=128)).then_inc(sem, 16)
        e.dma_start(out=cnw4_bf, in_=cn_w4.rearrange("p (c o) -> p c o", o=1)).then_inc(sem, 16)
        e.dma_start(out=cnb1_t, in_=cn_b1).then_inc(sem, 16)
        e.dma_start(out=cnb2_t, in_=cn_b2).then_inc(sem, 16)
        e.dma_start(out=cnb3_t, in_=cn_b3).then_inc(sem, 16)
        e.dma_start(out=cnb4_t, in_=cn_b4).then_inc(sem, 16)
        e.dma_start(out=ar_sb, in_=a_range).then_inc(sem, 16)
    P.emit("gpsimd", ld_cnw, deps=("p2_gpsimd",), dma=9, label="cnw_loaded")

    def fcc(e, sem):
        e.tensor_scalar_mul(tt1, gf_sb, 0.7)
        e.drain()
        e.tensor_scalar_add(tt1, tt1, -0.5)
        e.drain()
        e.tensor_tensor(tt1, tt1, ar_sb, ALU.mult)
        e.drain()
        e.tensor_scalar_mul(tt1, tt1, 2.0)
        e.drain()
        e.tensor_scalar_add(c0_sb, tt1, 0.1)
        e.drain()
        e.tensor_scalar_mul(c1_sb, ar_sb, 0.6)
        return e.drain()
    P.emit("vector", fcc, deps=("gf_done", "cnw_loaded", "p2_vector"), label="cc_done")

    nmm = 0
    nftp = 0
    for cyc in range(3):
        for i in range(NT):
            def fc(e, sem, i=i, cyc=cyc):
                e.tensor_copy(cast_scr, xn_t[:, i, :])
                return e.drain()
            dep0 = ["cc_done"] if (cyc == 0 and i == 0) else [f"ftpev_{nftp-1}"]
            if cyc > 0 and i == 0:
                dep0 = [f"tw_done_{cyc-1}"]
            P.emit("vector", fc, deps=tuple(dep0), label=f"stc_{cyc}_{i}")
            for c in range(NC):
                b = nftp % 4
                def ftp(e, sem, c=c, b=b):
                    return e.transpose(pbf(b), cast_scr[:, c * 128:(c + 1) * 128], c_identb)
                deps = [f"stc_{cyc}_{i}"]
                if nftp >= 4:
                    deps.append(f"ftpev_{nftp-4}")
                if nftp < 4 and cyc == 0:
                    deps += [f"tzev_{ntz-1}", f"pvtevB_{nbv-1}", f"qkpost_{nqk-1}", f"stev_{nst-1}"]
                P.emit("tensor", ftp, deps=tuple(deps), label=f"ftp_{nftp}")
                def fev(e, sem, i=i, c=c, b=b):
                    return e.tensor_copy(fmaj_bf[:, c, i * 128:(i + 1) * 128], pbf(b))
                P.emit("vector", fev, deps=(f"ftp_{nftp}",), label=f"ftpev_{nftp}")
                nftp += 1
        last_tp = f"ftpev_{nftp-1}"

        def chain(win, kcs, ocs, src, dst, bias_t, prev):
            nonlocal nmm
            for oc in range(ocs):
                for tc in range(4):
                    b = nmm % 4
                    def f(e, sem, oc=oc, tc=tc, b=b, win=win, kcs=kcs, src=src):
                        for kc in range(kcs):
                            last = e.matmul(pb[b][:], win[:, kc, oc * 128:(oc + 1) * 128],
                                            src[:, kc, tc * 512:(tc + 1) * 512],
                                            start=(kc == 0), stop=(kc == kcs - 1))
                        return last
                    deps = list(prev) if (oc == 0 and tc == 0) else []
                    if nmm < 4:
                        deps.append(f"ftpev_{nftp-1}" if False else f"mme_{nmm-2}") if nmm >= 2 else None
                    if nmm >= 4:
                        deps.append(f"mme_{nmm-4}")
                    P.emit("tensor", f, deps=tuple(deps), label=f"mm_{nmm}")
                    def fe(e, sem, oc=oc, tc=tc, b=b, dst=dst, bias_t=bias_t):
                        return e.activation(dst[:, oc, tc * 512:(tc + 1) * 512], pb[b][:],
                                            AF.Gelu, bias=bias_t[:, oc:oc + 1])
                    P.emit("scalar", fe, deps=(f"mm_{nmm}",), label=f"mme_{nmm}")
                    nmm += 1
            return f"mme_{nmm-1}"

        prev0 = ["cnw_loaded", last_tp]
        if cyc == 0:
            prev0 += [f"pvtevB_{nbv-1}", f"qkpost_{nqk-1}"]
        l1 = chain(cnw1_bf, NC, NC, fmaj_bf, h1_bf, cnb1_t, tuple(prev0))
        l2 = chain(cnw2_bf, NC, 4, h1_bf, h2_bf, cnb2_t, (l1,))
        l3 = chain(cnw3_bf, 4, 2, h2_bf, h3_bf, cnb3_t, (l2,))
        for tc in range(4):
            def f(e, sem, tc=tc):
                for kc in range(2):
                    last = e.matmul(pb[4][0:1, :], cnw4_bf[0:128, kc, 0:1],
                                    h3_bf[:, kc, tc * 512:(tc + 1) * 512],
                                    start=(kc == 0), stop=(kc == 1))
                return last
            deps = [l3] if tc == 0 else [f"t4e_{cyc}_{tc-1}"]
            if cyc == 0 and tc == 0:
                deps.append(f"tzev_{ntz-1}")
            P.emit("tensor", f, deps=tuple(deps), label=f"t4_{cyc}_{tc}")
            def fe(e, sem, tc=tc):
                return e.activation(cp_row[:, tc * 512:(tc + 1) * 512], pb[4][0:1, :], AF.Sigmoid,
                                    bias=cnb4_t)
            P.emit("scalar", fe, deps=(f"t4_{cyc}_{tc}",), label=f"t4e_{cyc}_{tc}")
        def fcp(e, sem):
            e.tensor_scalar(cp_row, cp_row, c1_sb, c0_sb, ALU.mult, ALU.add)
            return e.drain()
        P.emit("vector", fcp, deps=(f"t4e_{cyc}_3",), label=f"cprow_{cyc}")
        for qt in range(NT):
            def ftc(e, sem, qt=qt):
                return e.transpose(pb[4][0:128, 256:257], cp_row[:, qt * 128:(qt + 1) * 128], c_identf[0:1, 0:1])
            deps = [f"cprow_{cyc}"]
            if qt > 0:
                deps.append(f"cpev_{cyc}_{qt-1}")
            P.emit("tensor", ftc, deps=tuple(deps), label=f"cptp_{cyc}_{qt}")
            def ftcev(e, sem, qt=qt):
                return e.tensor_copy(cpT[:, qt:qt + 1], pb[4][0:128, 256:257])
            P.emit("vector", ftcev, deps=(f"cptp_{cyc}_{qt}",), label=f"cpev_{cyc}_{qt}")
        def fneg(e, sem):
            e.tensor_scalar_mul(cpTn, cpT, -1.0)
            return e.drain()
        P.emit("vector", fneg, deps=(f"cpev_{cyc}_{NT-1}",), label=f"cpneg_{cyc}")
        for i in range(NT):
            def ftw(e, sem, i=i):
                q = lambda k: xn_t[:, i, k * 256:(k + 1) * 256]
                e.tensor_copy(tw_tmp[:, 0, :], q(0))
                e.tensor_copy(tw_tmp[:, 1, :], q(1))
                e.drain()
                e.scalar_tensor_tensor(q(0), q(2), cpT[:, i:i + 1], q(0), ALU.mult, ALU.add)
                e.scalar_tensor_tensor(q(1), q(3), cpTn[:, i:i + 1], q(1), ALU.mult, ALU.add)
                e.drain()
                e.scalar_tensor_tensor(q(2), tw_tmp[:, 0, :], cpTn[:, i:i + 1], q(2), ALU.mult, ALU.add)
                e.scalar_tensor_tensor(q(3), tw_tmp[:, 1, :], cpT[:, i:i + 1], q(3), ALU.mult, ALU.add)
                return e.drain()
            P.emit("vector", ftw, deps=(f"cpneg_{cyc}",), label=f"tw_{cyc}_{i}")
        P.mark_here("vector", f"tw_done_{cyc}")

    def st_g(e, sem):
        e.dma_start(out=g_d, in_=xn_t.rearrange("p a b -> p (a b)")).then_inc(sem, 16)
    P.emit("sync", st_g, deps=("tw_done_2",), dma=1, label="g_stored")

    if nphase < 3:
        _finish_stub(P, out, xn_t)
        P.build()
        return nc, es

    P.barrier("p3")
    A.reset(ph_base)

    wout_bf = A.alloc([128, NC, D], BF16)
    bout_row = A.alloc([1, D], rows=1)
    bout_bc = A.alloc([128, D])
    ones1 = A.alloc([1, 128], rows=1)
    sil = A.alloc([128, 512])
    cmb = A.alloc([128, 512])
    af_sl = A.alloc([128, 2, NC, 128], BF16)

    def ld_wout(e, sem):
        e.dma_start(out=wout_bf, in_=w_out.rearrange("(c p) o -> p c o", p=128)).then_inc(sem, 16)
        e.dma_start(out=bout_row, in_=b_out).then_inc(sem, 16)
    P.emit("gpsimd", ld_wout, deps=("p3_gpsimd",), dma=2, label="wout_loaded")
    def fones(e, sem):
        e.memset(ones1, 1.0)
        return e.drain()
    P.emit("vector", fones, deps=("p3_vector",), label="ones_done")
    for hf in range(2):
        def fbb(e, sem, hf=hf):
            return e.matmul(pb[hf][:], ones1, bout_row[:, hf * 512:(hf + 1) * 512],
                            start=True, stop=True)
        P.emit("tensor", fbb, deps=("ones_done", "wout_loaded", "p3_tensor"), label=f"bb_{hf}")
        def fbbe(e, sem, hf=hf):
            return e.tensor_copy(bout_bc[:, hf * 512:(hf + 1) * 512], pb[hf][:])
        P.emit("vector", fbbe, deps=(f"bb_{hf}",), label=f"bbe_{hf}")

    ncm = 0
    for i in range(NT):
        def ldaf(e, sem, i=i):
            a3 = attnf_d.rearrange("p (c t) -> p c t", c=NC)
            e.dma_start(out=af_sl[:, i % 2, :, :].bitcast(U16), in_=a3[:, :, i * 128:(i + 1) * 128]).then_inc(sem, 16)
        depl = ["attn_stored", "p3_gpsimd"]
        if i >= 2:
            depl.append(f"cmb_{2*(i-2)+1}")
        P.emit("gpsimd", ldaf, deps=tuple(depl), dma=1, label=f"afld_{i}")
        def ldx(e, sem, i=i):
            e.dma_start(out=x_tile, in_=x_in.rearrange("(t p) d -> p t d", p=128)[:, i, :]).then_inc(sem, 16)
        depx = ["p3_sync"]
        if i >= 1:
            depx.append(f"cmb_{2*(i-1)+1}")
        P.emit("sync", ldx, deps=tuple(depx), dma=1, label=f"xld2_{i}")
        for hf in range(2):
            b = 2 + (ncm % 2)
            def f(e, sem, i=i, hf=hf, b=b):
                for kc in range(NC):
                    last = e.matmul(pb[b][:], af_sl[:, i % 2, kc, :],
                                    wout_bf[:, kc, hf * 512:(hf + 1) * 512],
                                    start=(kc == 0), stop=(kc == NC - 1))
                return last
            deps = [f"afld_{i}"]
            if ncm == 0:
                deps += ["bbe_0", "bbe_1", "g_stored"]
            if ncm >= 2:
                deps.append(f"cmb_{ncm-2}")
            P.emit("tensor", f, deps=tuple(deps), label=f"smm_{ncm}")
            def fsil(e, sem, i=i, hf=hf):
                return e.activation(sil, xn_t[:, i, hf * 512:(hf + 1) * 512], AF.Silu)
            P.emit("scalar", fsil, deps=(f"smm_{ncm}",) + (() if ncm == 0 else (f"cmb_{ncm-1}",)), label=f"sil_{ncm}")
            def fcmb(e, sem, i=i, hf=hf, b=b):
                sl = slice(hf * 512, (hf + 1) * 512)
                e.tensor_tensor(cmb, pb[b][:], bout_bc[:, sl], ALU.add)
                e.drain()
                e.tensor_tensor(cmb, cmb, sil, ALU.mult)
                e.drain()
                e.tensor_tensor(xn_t[:, i, sl], x_tile[:, sl], cmb, ALU.add)
                return e.drain()
            P.emit("vector", fcmb, deps=(f"sil_{ncm}", f"xld2_{i}"), label=f"cmb_{ncm}")
            ncm += 1
    P.mark_here("vector", "x1_done")

    def st_x1(e, sem):
        e.dma_start(out=x1_d, in_=xn_t.rearrange("p a b -> p (a b)")).then_inc(sem, 16)
    P.emit("sync", st_x1, deps=("x1_done",), dma=1, label="x1_stored")

    if nphase < 4:
        def st_out(e, sem):
            e.dma_start(out=out.rearrange("(t p) d -> p t d", p=128), in_=xn_t).then_inc(sem, 16)
        P.emit("sync", st_out, deps=("x1_done",), dma=1, label="out_stored")
        P.build()
        return nc, es

    P.barrier("p4")
    A.reset(m_xn)
    xn2_t = A.alloc([128, NT, D], BF16)
    p5_base = A.mark()

    st_sum4 = A.alloc([128, NT]); st_sq4 = A.alloc([128, NT])
    st_mean4 = A.alloc([128, NT]); st_var4 = A.alloc([128, NT])
    st_a2 = A.alloc([128, NT]); st_b2 = A.alloc([128, NT])
    st_sv = A.alloc([128, NT])
    sq_scr4 = A.alloc([128, D])
    gam_bc = A.alloc([128, D])
    gam_row = A.alloc([1, D], rows=1)
    ones4 = A.alloc([1, 128], rows=1)
    xn1_scr = A.alloc([128, D])
    conv_buf = A.alloc([128, 2, 2, 2 * D], BF16)

    NCHUNK = 0 if os.environ.get("SKIP_CONV") else 256
    ncv = 0
    for tab_in, tab_out in (((tab_du, tabdu_bf),) if NCHUNK else ()):
        for ch in range(NCHUNK):
            bcv = ncv % 2
            def fldc(e, sem, tab_in=tab_in, ch=ch, bcv=bcv):
                src_ap = tab_in[ch * 256:(ch + 1) * 256, :].rearrange("(r p) d -> p r d", p=128)
                e.dma_start(out=conv_buf[:, bcv, :, :], in_=src_ap).then_inc(sem, 16)
            depc = ["p4_gpsimd"]
            if ncv >= 2:
                depc.append(f"cvst_{ncv-2}")
            P.emit("gpsimd", fldc, deps=tuple(depc), dma=1, label=f"cvld_{ncv}")
            def fstc(e, sem, tab_out=tab_out, ch=ch, bcv=bcv):
                dst_ap = tab_out[ch * 256:(ch + 1) * 256, :].rearrange("(r p) d -> p r d", p=128)
                e.dma_start(out=dst_ap, in_=conv_buf[:, bcv, :, :].bitcast(U16)).then_inc(sem, 16)
            P.emit("sync", fstc, deps=(f"cvld_{ncv}",), dma=1, label=f"cvst_{ncv}")
            ncv += 1
    P.mark_here("sync", "tables_done")
    NPH16 = 1 if os.environ.get("SMALL_A") else 16
    NTI5 = 1 if os.environ.get("SMALL_A") else NT

    for i in range(NT):
        def f(e, sem, i=i):
            return e.activation(sq_scr4, xn_t[:, i, :], AF.Square)
        P.emit("scalar", f, deps=("p4_scalar", "x1_done") if i == 0 else (f"sq4_{i-1}",), label=f"sqd4_{i}")
        def f2(e, sem, i=i):
            e.reduce_sum(st_sum4[:, i:i + 1], xn_t[:, i, :], axis=AX.X)
            return e.reduce_sum(st_sq4[:, i:i + 1], sq_scr4, axis=AX.X)
        P.emit("vector", f2, deps=(f"sqd4_{i}", "p4_vector"), label=f"sq4_{i}")
    def stats4(e, sem):
        e.tensor_scalar_mul(st_mean4, st_sum4, 1.0 / D)
        e.drain()
        e.tensor_scalar_mul(st_var4, st_sq4, 1.0 / D)
        e.drain()
        e.tensor_tensor(st_b2, st_mean4, st_mean4, ALU.mult)
        e.drain()
        e.tensor_sub(st_var4, st_var4, st_b2)
        e.drain()
        return e.tensor_scalar_add(st_a2, st_var4, 1e-5)
    P.emit("vector", stats4, deps=(f"sq4_{NT-1}",), label="var4_done")
    def rstd4a(e, sem):
        e.activation(ln1_rstd, st_a2, AF.Sqrt)
        return e.activation(st_sv, st_var4, AF.Sqrt)
    P.emit("scalar", rstd4a, deps=("var4_done",), label="sqrt4_done")
    def rstd4b(e, sem):
        e.reciprocal(ln1_rstd, ln1_rstd)
        e.drain()
        e.tensor_tensor(ln1_nmr, st_mean4, ln1_rstd, ALU.mult)
        e.drain()
        e.tensor_scalar_mul(ln1_nmr, ln1_nmr, -1.0)
        e.drain()
        e.tensor_tensor(st_a2, ln1_rstd, st_sv, ALU.mult)
        e.drain()
        e.reciprocal(st_a2, st_a2)
        e.drain()
        e.tensor_tensor(st_b2, ln1_nmr, st_a2, ALU.mult)
        e.drain()
        e.tensor_tensor(st_a2, ln1_rstd, st_a2, ALU.mult)
        return e.drain()
    P.emit("vector", rstd4b, deps=("sqrt4_done",), label="ln1_done")

    def fon4(e, sem):
        e.memset(ones4, 1.0)
        return e.drain()
    P.emit("vector", fon4, deps=("p4_vector",), label="ones4")
    def ldgam(e, sem):
        e.dma_start(out=gam_row, in_=gamma).then_inc(sem, 16)
    P.emit("sync", ldgam, deps=("p4_sync",), dma=1, label="gam_loaded")
    for hf in range(2):
        def fgb(e, sem, hf=hf):
            return e.matmul(pb[6][:], ones4, gam_row[:, hf * 512:(hf + 1) * 512], start=True, stop=True)
        P.emit("tensor", fgb, deps=("ones4", "gam_loaded", "p4_tensor") if hf == 0 else (f"gbe_{hf-1}",), label=f"gb_{hf}")
        def fgbe(e, sem, hf=hf):
            return e.tensor_copy(gam_bc[:, hf * 512:(hf + 1) * 512], pb[6][:])
        P.emit("vector", fgbe, deps=(f"gb_{hf}",), label=f"gbe_{hf}")

    for i in range(NT):
        def fxn2(e, sem, i=i):
            e.tensor_scalar(xn1_scr, xn_t[:, i, :], st_a2[:, i:i + 1], st_b2[:, i:i + 1],
                            ALU.mult, ALU.add)
            e.drain()
            e.tensor_tensor(xn2_t[:, i, :], xn1_scr, gam_bc, ALU.mult)
            return e.drain()
        P.emit("vector", fxn2, deps=("ln1_done", "gbe_1") if i == 0 else (f"xn2_{i-1}",), label=f"xn2_{i}")
    P.mark_here("vector", "xn2t_done")

    if nphase < 5:
        _finish_stub(P, out, xn_t)
        P.build()
        return nc, es

    P.barrier("p5")
    A.reset(p5_base)
    xn2_f = A.alloc_at(base_min, [128, NC, T], BF16)
    keysT_sb = A.alloc_at(base_min + 8192, [128, 64, 256], BF16)
    wdu = A.alloc_at(base_min, [128, 2, 8, 2 * D], BF16)
    wqsl_off = A.mark()
    wq_sl = A.alloc([128, 2, NC, 512], BF16)
    qf_sb = A.alloc([128, 4, 128], BF16)
    sim_sb = A.alloc([128, 256], F32)
    sim2_sb = A.alloc([128, 256], F32)
    sx_all = A.alloc([128, NT, 16, 16], F32)
    ix_all = A.alloc([128, NT, 16, 16], U16)
    w16_all = A.alloc([128, NT, 8, 16], F32)
    ixs_all = A.alloc([128, NT, 8, 16], U16)
    iys_all = A.alloc([128, NT, 8, 16], U16)
    A_sb = A.alloc([128, 256], F32)
    sc16 = A.alloc([128, 8, 16], F32)
    pk16 = A.alloc([128, 16], U16)
    iota16 = A.alloc([128, 16], U16)
    isel = A.alloc([128, 16], U16)
    jsel = A.alloc([128, 16], U16)
    oh16 = A.alloc([128, 256], U16)
    pr16 = A.alloc([128, 256], U16)
    m0n8 = A.alloc([128, 8], F32)
    e16 = A.alloc([128, 8, 16], F32)
    l8 = A.alloc([128, 8], F32)
    eidx = A.alloc([128, 128], I32)
    ei_a = A.alloc([128, 128], I32)
    ei_b = A.alloc([128, 128], I32)
    prod5 = A.alloc([128, 8, D], BF16)
    h16 = A.alloc([128, 16], F32)
    hh16 = A.alloc([128, 16], F32)
    g1_tile = A.alloc([128, D], F32)
    g1_acc = A.alloc_at(wqsl_off, [128, 4, D], F32)

    def ldk5(e, sem):
        e.dma_start(out=keysT_sb, in_=keysT.rearrange("p (a b) -> p a b", a=64)).then_inc(sem, 16)
    P.emit("gpsimd", ldk5, deps=("p5_gpsimd",), dma=1, label="keys_loaded")
    ntp5 = 0
    for i in range(NT):
        for c4 in range(2):
            def ftp(e, sem, i=i, c4=c4):
                for j in range(4):
                    last = e.transpose(pbf(j), xn2_t[:, i, (c4 * 4 + j) * 128:(c4 * 4 + j + 1) * 128], c_identb)
                return last
            deps = ["xn2t_done", "p5_tensor"]
            if ntp5 >= 1:
                deps.append(f"tp5ev_{ntp5-1}")
            P.emit("tensor", ftp, deps=tuple(deps), label=f"tp5_{ntp5}")
            def fev(e, sem, i=i, c4=c4):
                for j in range(4):
                    e.tensor_copy(xn2_f[:, c4 * 4 + j, i * 128:(i + 1) * 128], pbf(j))
                return e.drain()
            P.emit("vector", fev, deps=(f"tp5_{ntp5}",), label=f"tp5ev_{ntp5}")
            ntp5 += 1
    P.mark_here("vector", "xn2_done")
    def fio5(e, sem):
        return e.iota(iota16, pattern=[[1, 16]], base=0, channel_multiplier=0)
    P.emit("gpsimd", fio5, deps=("p5_gpsimd",), label="iota5")

    nq5 = 0
    for ph16 in range(NPH16):
        p_, h_ = ph16 // 8, ph16 % 8
        wb = ph16 % 2
        def ldw5(e, sem, p_=p_, h_=h_, wb=wb):
            col0 = p_ * 4096 + h_ * 512
            e.dma_start(out=wq_sl[:, wb, :, :],
                        in_=wqT.rearrange("(c p) o -> p c o", p=128)[:, :, col0:col0 + 512]).then_inc(sem, 16)
        depw = ["p5_gpsimd"]
        if ph16 >= 2:
            depw.append(f"s1_{ph16-2}_{NT-1}")
        P.emit("gpsimd", ldw5, deps=tuple(depw), dma=1, label=f"wq5_{ph16}")
        for i in range(NTI5):
            bq = nq5 % 2
            def fq(e, sem, i=i, wb=wb, bq=bq):
                for oc4 in range(4):
                    for kc in range(NC):
                        last = e.matmul(pb[bq][:, oc4 * 128:(oc4 + 1) * 128],
                                        wq_sl[:, wb, kc, oc4 * 128:(oc4 + 1) * 128],
                                        xn2_f[:, kc, i * 128:(i + 1) * 128],
                                        start=(oc4 == 0 and kc == 0), stop=(kc == NC - 1))
                return last
            depq = [f"wq5_{ph16}", "xn2_done"]

            if nq5 >= 2:
                depq.append(f"qev5_{nq5-2}")
            P.emit("tensor", fq, deps=tuple(depq), label=f"qmm5_{nq5}")
            def fqe(e, sem, bq=bq):
                return e.tensor_copy(qf_sb.rearrange("p a b -> p (a b)"), pb[bq][:])
            dqe = [f"qmm5_{nq5}"]
            if nq5 >= 1:
                dqe.append(f"sim5_{nq5-1}")
            P.emit("vector", fqe, deps=tuple(dqe), label=f"qev5_{nq5}")
            def fsim(e, sem, ph16=ph16, bq=bq):
                for j in range(4):
                    last = e.matmul(pb[4 + (bq % 2)][:, 0:256], qf_sb[:, j, :],
                                    keysT_sb[:, (ph16 % 8) * 8 + (ph16 // 8) * 4 + j, :],
                                    start=(j == 0), stop=(j == 3))
                return last
            dsim = [f"qev5_{nq5}", "keys_loaded"]
            if nq5 >= 2:
                dsim.append(f"simev5_{nq5-2}")
            P.emit("tensor", fsim, deps=tuple(dsim), label=f"sim5_{nq5}")
            def fsime(e, sem, bq=bq):
                return e.tensor_copy(sim_sb, pb[4 + (bq % 2)][:, 0:256])
            P.emit("vector", fsime, deps=(f"sim5_{nq5}",), label=f"simev5_{nq5}")
            def ftop1(e, sem, i=i, ph16=ph16):
                e.max(out=sx_all[:, i, ph16, 0:8], in_=sim_sb)
                e.drain()
                e.max_index(out=ix_all[:, i, ph16, 0:8], in_max=sx_all[:, i, ph16, 0:8], in_values=sim_sb)
                e.drain()
                e.match_replace(out=sim2_sb, in_to_replace=sx_all[:, i, ph16, 0:8], in_values=sim_sb,
                                imm_value=-3.0e38)
                e.drain()
                e.max(out=sx_all[:, i, ph16, 8:16], in_=sim2_sb)
                e.drain()
                e.max_index(out=ix_all[:, i, ph16, 8:16], in_max=sx_all[:, i, ph16, 8:16], in_values=sim2_sb)
                return e.drain()
            P.emit("vector", ftop1, deps=(f"simev5_{nq5}", "iota5"), label=f"s1_{ph16}_{i}")
            nq5 += 1
    if "s1_15_15" not in P.marks and NPH16 == 16:
        pass
    P.mark_here("vector", "stage1_done")

    NB5 = 0 if os.environ.get("SKIP_B") else NT
    for i in range(NB5):
        def fs2(e, sem, i=i):
            for h_ in range(8):
                sx = sx_all[:, i, h_, :]
                sy = sx_all[:, i, 8 + h_, :]
                e.tensor_tensor(A_sb.rearrange("p (a b) -> p a b", a=16),
                                sx.unsqueeze(2).to_broadcast([128, 16, 16]),
                                sy.unsqueeze(1).to_broadcast([128, 16, 16]), ALU.add)
                e.drain()
                e.max(out=sc16[:, h_, 0:8], in_=A_sb)
                e.drain()
                e.max_index(out=pk16[:, 0:8], in_max=sc16[:, h_, 0:8], in_values=A_sb)
                e.drain()
                e.match_replace(out=A_sb, in_to_replace=sc16[:, h_, 0:8], in_values=A_sb, imm_value=-3.0e38)
                e.drain()
                e.max(out=sc16[:, h_, 8:16], in_=A_sb)
                e.drain()
                e.max_index(out=pk16[:, 8:16], in_max=sc16[:, h_, 8:16], in_values=A_sb)
                e.drain()
                e.tensor_scalar(isel, pk16, 4, None, ALU.logical_shift_right)
                e.tensor_scalar(jsel, pk16, 15, None, ALU.bitwise_and)
                e.drain()
                e.tensor_tensor(oh16.rearrange("p (a b) -> p a b", a=16),
                                isel.unsqueeze(2).to_broadcast([128, 16, 16]),
                                iota16.unsqueeze(1).to_broadcast([128, 16, 16]), ALU.is_equal)
                e.drain()
                e.tensor_tensor(pr16.rearrange("p (a b) -> p a b", a=16),
                                oh16.rearrange("p (a b) -> p a b", a=16),
                                ix_all[:, i, h_, :].unsqueeze(1).to_broadcast([128, 16, 16]), ALU.mult)
                e.drain()
                with nc.allow_low_precision(reason="u16 index select"):
                    e.reduce_sum(ixs_all[:, i, h_, :], pr16.rearrange("p (a b) -> p a b", a=16), axis=AX.X)
                e.drain()
                e.tensor_tensor(oh16.rearrange("p (a b) -> p a b", a=16),
                                jsel.unsqueeze(2).to_broadcast([128, 16, 16]),
                                iota16.unsqueeze(1).to_broadcast([128, 16, 16]), ALU.is_equal)
                e.drain()
                e.tensor_tensor(pr16.rearrange("p (a b) -> p a b", a=16),
                                oh16.rearrange("p (a b) -> p a b", a=16),
                                ix_all[:, i, 8 + h_, :].unsqueeze(1).to_broadcast([128, 16, 16]), ALU.mult)
                e.drain()
                with nc.allow_low_precision(reason="u16 index select"):
                    e.reduce_sum(iys_all[:, i, h_, :], pr16.rearrange("p (a b) -> p a b", a=16), axis=AX.X)
                e.drain()
                e.tensor_scalar_mul(m0n8[:, h_:h_ + 1], sc16[:, h_, 0:1], -1.0)
                e.drain()
            return e.drain()
        dep_s2 = [f"s1_15_{i}"]
        if i > 0:
            dep_s2.append(f"sexp_{i-1}")
        P.emit("vector", fs2, deps=tuple(dep_s2), label=f"s2_{i}")
        def fexp(e, sem):
            for h_ in range(8):
                last = e.activation(e16[:, h_, :], sc16[:, h_, :], AF.Exp, bias=m0n8[:, h_:h_ + 1])
            return last
        dep_e = [f"s2_{i}"]
        if i > 0:
            dep_e.append(f"sm2_{i-1}")
        P.emit("scalar", fexp, deps=tuple(dep_e), label=f"sexp_{i}")
        def fsm(e, sem, i=i):
            for h_ in range(8):
                e.reduce_sum(l8[:, h_:h_ + 1], e16[:, h_, :], axis=AX.X)
            e.drain()
            e.reciprocal(l8, l8)
            e.drain()
            for h_ in range(8):
                e.tensor_scalar_mul(w16_all[:, i, h_, :], e16[:, h_, :], l8[:, h_:h_ + 1])
            return e.drain()
        P.emit("vector", fsm, deps=(f"sexp_{i}",), label=f"sm2_{i}")
    P.mark_here("vector", "stage2_done")

    ng5 = 0
    NCI = 1 if os.environ.get("SMALL_C") else NT
    for i in range(NCI):
        def feid(e, sem, i=i):
            e.tensor_copy(ei_a, ixs_all[:, i, :, :].rearrange("p a b -> p (a b)"))
            e.tensor_copy(ei_b, iys_all[:, i, :, :].rearrange("p a b -> p (a b)"))
            e.tensor_scalar(eidx, ei_a, 256, None, ALU.mult)
            return e.tensor_tensor(eidx, eidx, ei_b, ALU.add)
        depe = [f"sm2_{i}"]
        if i > 0:
            depe.append(f"gdone_{i-1}")
        if "stage2_done" not in P.marks:
            P.mark_here("vector", "stage2_done")
        P.emit("gpsimd", feid, deps=tuple(depe), label=f"eid_{i}")
        if debug:
            def fdbg(e, sem, i=i):
                e.dma_start(out=dbg4.rearrange("p (a b) -> p a b", a=16)[:, i, :], in_=eidx).then_inc(sem, 16)
            P.emit("sync", fdbg, deps=(f"eid_{i}",), dma=1, label=f"dbge_{i}")
        def fgz(e, sem, i=i):
            e.memset(g1_acc.rearrange("p a b -> p (a b)"), 0.0)
            return e.drain()
        P.emit("vector", fgz, deps=(f"eid_{i}",) + (() if i == 0 else (f"g1st_{i-1}",)), label=f"g1z_{i}")
        for h_ in range(8):
            for hb in range(2):
                def fgth(e, sem, i=i, h_=h_, hb=hb):
                    last = None
                    for kk in range(8):
                        kg = h_ * 16 + hb * 8 + kk
                        last = e.indirect_dma_start(
                            out=wdu[:, hb, kk, :].bitcast(U16), out_offset=None,
                            in_=tabdu_bf[0:128, :],
                            in_offset=IndirectOffsetOnAxis(ap=eidx[:, kg:kg + 1], axis=0))
                        last.then_inc(sem, 16)
                depg = [f"eid_{i}", "tables_done"]
                if ng5 >= 2:
                    depg.append(f"updot_{ng5-2}")
                P.emit("gpsimd", fgth, deps=tuple(depg), dma=8, label=f"gth_{ng5}")
                def fdown(e, sem, i=i, hb=hb):
                    return e.tensor_tensor(prod5, wdu[:, hb, :, 0:D],
                                           xn2_t[:, i, :].unsqueeze(1).to_broadcast([128, 8, D]),
                                           ALU.mult)
                P.emit("vector", fdown, deps=(f"gth_{ng5}",), label=f"down_{ng5}")
                def fgel(e, sem, hb=hb):
                    for kk in range(8):
                        e.activation(prod5[:, kk, :], prod5[:, kk, :], AF.Identity,
                                     accum_out=h16[:, hb * 8 + kk:hb * 8 + kk + 1])
                    return e.activation(hh16[:, hb * 8:(hb + 1) * 8], h16[:, hb * 8:(hb + 1) * 8],
                                        AF.Gelu)
                P.emit("scalar", fgel, deps=(f"down_{ng5}",), label=f"gel_{ng5}")
                def fup(e, sem, i=i, h_=h_, hb=hb):
                    sl = slice(hb * 8, (hb + 1) * 8)
                    e.tensor_tensor(h16[:, sl], hh16[:, sl], w16_all[:, i, h_, sl], ALU.mult)
                    e.drain()
                    for kk in range(8):
                        e.scalar_tensor_tensor(g1_acc[:, kk % 4, :], wdu[:, hb, kk, D:2 * D],
                                               h16[:, hb * 8 + kk:hb * 8 + kk + 1],
                                               g1_acc[:, kk % 4, :], ALU.mult, ALU.add)
                    return e.drain()
                P.emit("vector", fup, deps=(f"gel_{ng5}", f"g1z_{i}"), label=f"updot_{ng5}")
                ng5 += 1
        def fgsum(e, sem, i=i):
            e.tensor_add(g1_acc[:, 0, :], g1_acc[:, 0, :], g1_acc[:, 1, :])
            e.tensor_add(g1_acc[:, 2, :], g1_acc[:, 2, :], g1_acc[:, 3, :])
            e.drain()
            e.tensor_add(g1_tile, g1_acc[:, 0, :], g1_acc[:, 2, :])
            return e.drain()
        P.emit("vector", fgsum, deps=(f"updot_{ng5-1}",), label=f"gdone_{i}")
        def fg1st(e, sem, i=i):
            e.dma_start(out=g_d.rearrange("p (a b) -> p a b", a=NT)[:, i, :], in_=g1_tile).then_inc(sem, 16)
        P.emit("sync", fg1st, deps=(f"gdone_{i}",), dma=1, label=f"g1st_{i}")
    P.mark_here("sync", "g1_stored")

    if nphase < 6:
        _finish_stub(P, out, xn_t)
        P.build()
        return nc, es

    P.barrier("p6")
    A.reset(base_min)
    s1f = A.alloc([128, NC, T], BF16)
    v65 = A.alloc([128, NT, 16 * 65], BF16)
    xn1f = A.alloc([128, NC, T], BF16)
    Qf = A.alloc([128, NC, T], BF16)
    Kf = A.alloc([128, NC, T], BF16)
    m6 = A.mark()

    xn1sc = A.alloc([128, D], BF16)
    x1_tl = A.alloc([128, D], F32)
    pocket = A.mark()
    ntp6 = 0
    for i in range(NT):
        def fldx(e, sem, i=i):
            e.dma_start(out=x1_tl, in_=x1_d.rearrange("p (a b) -> p a b", a=NT)[:, i, :]).then_inc(sem, 16)
        depl = ["p6_sync", "x1_stored"]
        if i > 0:
            depl.append(f"tp6_{ntp6-1}")
        P.emit("sync", fldx, deps=tuple(depl), dma=1, label=f"x1ld_{i}")
        def fxn1(e, sem, i=i):
            e.tensor_scalar(xn1sc, x1_tl, ln1_rstd[:, i:i + 1], ln1_nmr[:, i:i + 1],
                            ALU.mult, ALU.add)
            return e.drain()
        P.emit("vector", fxn1, deps=(f"x1ld_{i}", "p6_vector"), label=f"xn1_{i}")
        for c4 in range(2):
            def ftp(e, sem, i=i, c4=c4):
                for j in range(4):
                    last = e.transpose(pbf(j), xn1sc[:, (c4 * 4 + j) * 128:(c4 * 4 + j + 1) * 128], c_identb)
                return last
            deps = [f"xn1_{i}", "p6_tensor"]
            if ntp6 >= 1:
                deps.append(f"tp6ev_{ntp6-1}")
            P.emit("tensor", ftp, deps=tuple(deps), label=f"tp6_{ntp6}")
            def fev(e, sem, i=i, c4=c4):
                for j in range(4):
                    e.tensor_copy(xn1f[:, c4 * 4 + j, i * 128:(i + 1) * 128], pbf(j))
                return e.drain()
            P.emit("vector", fev, deps=(f"tp6_{ntp6}",), label=f"tp6ev_{ntp6}")
            ntp6 += 1
    P.mark_here("vector", "xn1f_done")

    wv6 = A.alloc_at(pocket, [128, NC, D], BF16)
    def ldwv(e, sem):
        e.dma_start(out=wv6, in_=wqkv1.rearrange("(c p) o -> p c o", p=128)[:, :, 2 * D:3 * D]).then_inc(sem, 16)
    P.emit("gpsimd", ldwv, deps=("p6_gpsimd",), dma=1, label="wv6_loaded")
    v65r = v65.rearrange("p a (h w) -> p a h w", h=16)
    nv6 = 0
    for i in range(NT):
        for hf in range(2):
            b = nv6 % 2
            def f(e, sem, i=i, hf=hf, b=b):
                for kc in range(NC):
                    last = e.matmul(pb[b][:], xn1f[:, kc, i * 128:(i + 1) * 128],
                                    wv6[:, kc, hf * 512:(hf + 1) * 512],
                                    start=(kc == 0), stop=(kc == NC - 1))
                return last
            deps = ["wv6_loaded", "xn1f_done"]
            if nv6 >= 2:
                deps.append(f"v6ev_{nv6-2}")
            P.emit("tensor", f, deps=tuple(deps), label=f"v6mm_{nv6}")
            def fe(e, sem, i=i, hf=hf, b=b):
                e.tensor_copy(v65r[:, i, hf * 8:(hf + 1) * 8, 0:64],
                              pb[b][:].rearrange("p (a b) -> p a b", a=8))
                return e.drain()
            P.emit("vector", fe, deps=(f"v6mm_{nv6}",), label=f"v6ev_{nv6}")
            nv6 += 1
        def fo(e, sem, i=i):
            e.memset(v65r[:, i, :, 64:65], 1.0)
            return e.drain()
        P.emit("vector", fo, deps=(f"v6ev_{nv6-1}",), label=f"v6o_{i}")
    P.mark_here("vector", "v6_done")

    wg6 = A.alloc_at(pocket, [128, NC, 128], BF16)
    wr6 = A.alloc_at(pocket + 512, [128, NC, 128], BF16)
    cosb = A.alloc_at(pocket + 1024, [128, T], F32)
    sinb = A.alloc_at(pocket + 3072, [128, T], F32)
    rp1 = A.alloc_at(pocket + 5120, [128, 512], F32)
    rp2 = A.alloc_at(pocket + 5632, [128, 512], F32)
    nrk = 0
    for qk in range(2):
        dstf = Qf if qk == 0 else Kf
        cos_in = cosq8 if qk == 0 else cosk
        sin_in = sinq8 if qk == 0 else sink
        def ldcs(e, sem, cos_in=cos_in, sin_in=sin_in):
            e.dma_start(out=cosb, in_=cos_in).then_inc(sem, 16)
            e.dma_start(out=sinb, in_=sin_in).then_inc(sem, 16)
        depc = ["p6_gpsimd", f"v6ev_{nv6-1}", "v6_done"] if qk == 0 else [f"rkev_{nrk-1}"]
        P.emit("gpsimd", ldcs, deps=tuple(depc), dma=2, label=f"cs6_{qk}")
        for grp in range(NC):
            def ldwg(e, sem, qk=qk, grp=grp):
                col0 = qk * D + grp * 128
                e.dma_start(out=wg6, in_=wqkv1.rearrange("(c p) o -> p c o", p=128)[:, :, col0:col0 + 128]).then_inc(sem, 16)
            depw = [f"cs6_{qk}"]
            if nrk >= 1:
                deps_guard = f"rkmm_{nrk-1}"
                depw.append(deps_guard)
            P.emit("gpsimd", ldwg, deps=tuple(depw), dma=1, label=f"wg6_{qk}_{grp}")
            def frot(e, sem):
                w3 = wg6.rearrange("p c (a two) -> p c a two", two=2)
                r3 = wr6.rearrange("p c (a two) -> p c a two", two=2)
                e.tensor_scalar_mul(r3[:, :, :, 0:1], w3[:, :, :, 1:2], -1.0)
                e.drain()
                e.tensor_copy(r3[:, :, :, 1:2], w3[:, :, :, 0:1])
                return e.drain()
            P.emit("vector", frot, deps=(f"wg6_{qk}_{grp}",), label=f"rot_{qk}_{grp}")
            for tc in range(4):
                def fmm(e, sem, grp=grp, tc=tc):
                    for kc in range(NC):
                        e.matmul(pb[0][:], wg6[:, kc, :], xn1f[:, kc, tc * 512:(tc + 1) * 512],
                                 start=(kc == 0), stop=(kc == NC - 1))
                    for kc in range(NC):
                        last = e.matmul(pb[1][:], wr6[:, kc, :], xn1f[:, kc, tc * 512:(tc + 1) * 512],
                                        start=(kc == 0), stop=(kc == NC - 1))
                    return last
                depm = [f"rot_{qk}_{grp}", "v6_done", "xn1f_done"]
                if nrk >= 1:
                    depm.append(f"rkev_{nrk-1}")
                P.emit("tensor", fmm, deps=tuple(depm), label=f"rkmm_{nrk}")
                def fev(e, sem, grp=grp, tc=tc, dstf=dstf):
                    sl = slice(tc * 512, (tc + 1) * 512)
                    e.tensor_tensor(rp1, pb[0][:], cosb[:, sl], ALU.mult)
                    e.tensor_tensor(rp2, pb[1][:], sinb[:, sl], ALU.mult)
                    e.drain()
                    e.tensor_tensor(dstf[:, grp, sl], rp1, rp2, ALU.add)
                    return e.drain()
                P.emit("vector", fev, deps=(f"rkmm_{nrk}",), label=f"rkev_{nrk}")
                nrk += 1
    P.mark_here("vector", "qk6_done")

    Qaug = A.alloc_at(pocket, [128, T], BF16)
    Kaug = A.alloc_at(pocket + 1024, [128, T], BF16)
    PT = A.alloc_at(pocket + 2048, [128, T], BF16)
    PT2 = A.alloc_at(pocket + 3072, [128, T], BF16)
    qqrow = A.alloc_at(pocket + 3072, [1, T], F32, rows=1)
    kkrow = A.alloc_at(pocket + 5120, [1, T], F32, rows=1)
    mkk = A.alloc_at(pocket + 7168, [1, 1], F32, rows=1)
    crow = qqrow
    crow_bf = A.alloc_at(pocket + 5120, [1, T], BF16, rows=1)
    ones64 = A.alloc_at(pocket + 7170, [128, 1], BF16)
    trim6 = A.alloc_at(pocket + 7244, [128, 128], F32)
    rl7 = A.alloc_at(pocket + 7372, [128, 8], F32)
    attn_sc7 = A.alloc_at(pocket + 7380, [128, 7, 64], BF16)
    sq6 = PT.bitcast(F32) if False else None

    def ftr6(e, sem):
        e.dma_start(out=trim6, in_=trimask).then_inc(sem, 16)
    P.emit("gpsimd", ftr6, deps=("p6_gpsimd", "qk6_done"), dma=1, label="tri_loaded")
    def fo64(e, sem):
        e.memset(ones64, 1.0)
        return e.drain()
    P.emit("vector", fo64, deps=("qk6_done",), label="ones64")

    nst6 = 0
    npv6 = 0
    for h in range(16):
        grp, hh = h // 2, h % 2
        rb = hh * 64
        def fsq(e, sem, grp=grp, rb=rb):
            e.activation(PT[0:64, :], Qf[rb:rb + 64, grp, :], AF.Square)
            return e.activation(PT[64:128, :], Kf[rb:rb + 64, grp, :], AF.Square)
        dep0 = ["qk6_done"] if h == 0 else [f"aug_{h-1}", f"pv6c_{npv6-1}"]
        P.emit("scalar", fsq, deps=tuple(dep0), label=f"sq6_{h}")
        for tc in range(4):
            def fqq(e, sem, tc=tc):
                e.matmul(pb[6][0:1, 0:512], ones64[0:64, :], PT[0:64, tc * 512:(tc + 1) * 512],
                         start=True, stop=True)
                return e.matmul(pb[7][0:1, 0:512], ones64[64:128, :],
                                PT[64:128, tc * 512:(tc + 1) * 512], start=True, stop=True)
            deps = [f"sq6_{h}", "ones64"]
            if not (h == 0 and tc == 0):
                deps.append(f"qqe_{h}_{tc-1}" if tc > 0 else f"qqe_{h-1}_3")
            P.emit("tensor", fqq, deps=tuple(deps), label=f"qq_{h}_{tc}")
            def fqe(e, sem, tc=tc):
                e.tensor_copy(qqrow[:, tc * 512:(tc + 1) * 512], pb[6][0:1, 0:512])
                e.tensor_copy(kkrow[:, tc * 512:(tc + 1) * 512], pb[7][0:1, 0:512])
                return e.drain()
            P.emit("vector", fqe, deps=(f"qq_{h}_{tc}",), label=f"qqe_{h}_{tc}")
        def fcrow(e, sem):
            e.reduce_max(mkk, kkrow, axis=AX.X)
            return e.drain()
        P.emit("vector", fcrow, deps=(f"qqe_{h}_3",), label=f"mkk_{h}")
        def fcs(e, sem):
            e.activation(mkk, mkk, AF.Sqrt)
            return e.activation(crow, qqrow, AF.Sqrt)
        P.emit("scalar", fcs, deps=(f"mkk_{h}",), label=f"cs_{h}")
        def fcb(e, sem):
            e.tensor_scalar(crow, crow, mkk, None, ALU.mult)
            e.drain()
            e.tensor_scalar(crow, crow, -1.02, -1.0, ALU.mult, ALU.add)
            e.drain()
            e.tensor_copy(crow_bf, crow)
            return e.drain()
        P.emit("vector", fcb, deps=(f"cs_{h}",), label=f"cb_{h}")
        def faug(e, sem, grp=grp, rb=rb):
            e.tensor_copy(Qaug[0:64, :], Qf[rb:rb + 64, grp, :])
            e.tensor_copy(Kaug[0:64, :], Kf[rb:rb + 64, grp, :])
            e.drain()
            e.tensor_copy(Qaug[64:65, :], crow_bf)
            e.memset(Kaug[64:65, :], 1.0)
            return e.drain()
        P.emit("vector", faug, deps=(f"cb_{h}",), label=f"aug_{h}")
        def acc(qt):
            return pb[4 + qt // 7][:, (qt % 7) * 65:(qt % 7) * 65 + 65]
        def emit_evacs(qts):
            nonlocal npv6
            kt_dep = 6 if qts[-1] <= 6 else (13 if qts[-1] <= 13 else 15)
            def fpve(e, sem, qts=qts):
                for j, qt in enumerate(qts):
                    e.reciprocal(rl7[:, j:j + 1], acc(qt)[:, 64:65])
                e.drain()
                for j, qt in enumerate(qts):
                    e.tensor_scalar_mul(attn_sc7[:, j, :], acc(qt)[:, 0:64], rl7[:, j:j + 1])
                return e.drain()
            depe = [f"pvk_{h}_{min(kt_dep, NT-1)}"]
            if npv6 >= 1:
                depe.append(f"pv6t_{npv6-1}")
            P.emit("vector", fpve, deps=tuple(depe), label=f"pv6e_{npv6}")
            def fpvt(e, sem, qts=qts):
                for j in range(len(qts)):
                    last = e.transpose(pb[7][0:64, j * 64:(j + 1) * 64].bitcast(BF16),
                                       attn_sc7[:, j, :], c_identb)
                return last
            dept = [f"pv6e_{npv6}"]
            if npv6 >= 1:
                dept.append(f"pv6c_{npv6-1}")
            P.emit("tensor", fpvt, deps=tuple(dept), label=f"pv6t_{npv6}")
            def fpvc(e, sem, qts=qts, grp=grp, rb=rb):
                for j, qt in enumerate(qts):
                    e.tensor_copy(s1f[rb:rb + 64, grp, qt * 128:(qt + 1) * 128],
                                  pb[7][0:64, j * 64:(j + 1) * 64].bitcast(BF16))
                return e.drain()
            P.emit("vector", fpvc, deps=(f"pv6t_{npv6}",), label=f"pv6c_{npv6}")
            npv6 += 1
        for kt in range(NT):
            qc0 = (kt * 128) // 512
            pts = PT if kt % 2 == 0 else PT2
            for qc in range(qc0, 4):
                b = nst6 % 4
                def fst(e, sem, kt=kt, qc=qc, b=b):
                    return e.matmul(pb[b][:], Kaug[0:65, kt * 128:(kt + 1) * 128],
                                    Qaug[0:65, qc * 512:(qc + 1) * 512], start=True, stop=True)
                deps = [f"aug_{h}"]
                if nst6 >= 4:
                    deps.append(f"stex_{nst6-4}")
                if h == 0 and nst6 < 4:
                    deps.append(f"rkev_{nrk-1}")
                P.emit("tensor", fst, deps=tuple(deps), label=f"st6_{nst6}")
                def fex(e, sem, qc=qc, b=b, pts=pts):
                    return e.activation(pts[:, qc * 512:(qc + 1) * 512], pb[b][:], AF.Exp)
                dex = [f"st6_{nst6}"]
                if kt > 1 and qc == qc0:
                    dex.append(f"pvk_{h}_{kt-2}")
                P.emit("scalar", fex, deps=tuple(dex), label=f"stex_{nst6}")
                nst6 += 1
            def fmsk(e, sem, kt=kt, pts=pts):
                e.tensor_tensor(pts[:, kt * 128:(kt + 1) * 128], pts[:, kt * 128:(kt + 1) * 128],
                                trim6, ALU.mult)
                return e.drain()
            P.emit("vector", fmsk, deps=(f"stex_{nst6-1}", "tri_loaded"), label=f"msk_{h}_{kt}")
            def fpvk(e, sem, h=h, kt=kt, pts=pts):
                for qt in range(kt, NT):
                    last = e.matmul(acc(qt), pts[:, qt * 128:(qt + 1) * 128],
                                    v65r[:, kt, h, :], start=(kt == 0), stop=(kt == qt))
                return last
            deps = [f"msk_{h}_{kt}", "v6_done"]
            if h > 0 and kt == 0:
                deps.append(f"pv6c_{npv6-1}")
            P.emit("tensor", fpvk, deps=tuple(deps), label=f"pvk_{h}_{kt}")
            if kt == 6:
                emit_evacs(list(range(0, 7)))
            elif kt == 13:
                emit_evacs(list(range(7, 14)))
            elif kt == 15:
                emit_evacs([14, 15])
    P.mark_here("vector", "s1f_done")

    wpj = A.alloc_at(pocket, [128, NC, D], BF16)
    g1t = A.alloc_at(pocket + 4096, [128, D], F32)
    sil6 = A.alloc_at(pocket + 5120, [128, 512], F32)
    cmb6s = A.alloc_at(pocket + 5632, [128, 512], F32)
    def ldwp(e, sem):
        e.dma_start(out=wpj, in_=wproj1.rearrange("(c p) o -> p c o", p=128)).then_inc(sem, 16)
    P.emit("gpsimd", ldwp, deps=("s1f_done",), dma=1, label="wpj_loaded")

    ncm6 = 0
    for i in range(NT):
        def fldx2(e, sem, i=i):
            e.dma_start(out=x1_tl, in_=x1_d.rearrange("p (a b) -> p a b", a=NT)[:, i, :]).then_inc(sem, 16)
            e.dma_start(out=g1t, in_=g_d.rearrange("p (a b) -> p a b", a=NT)[:, i, :]).then_inc(sem, 16)
        depl = ["g1_stored", "s1f_done"]
        if i > 0:
            depl.append(f"cmb6_{2*(i-1)+1}")
        P.emit("sync", fldx2, deps=tuple(depl), dma=2, label=f"xg6_{i}")
        for hf in range(2):
            b = (ncm6 % 2)
            def f(e, sem, i=i, hf=hf, b=b):
                for kc in range(NC):
                    last = e.matmul(pb[b][:], s1f[:, kc, i * 128:(i + 1) * 128],
                                    wpj[:, kc, hf * 512:(hf + 1) * 512],
                                    start=(kc == 0), stop=(kc == NC - 1))
                return last
            deps = [f"xg6_{i}", "wpj_loaded"]
            if ncm6 == 0:
                deps.append(f"rkmm_{nrk-1}")
            if ncm6 >= 2:
                deps.append(f"cmb6_{ncm6-2}")
            P.emit("tensor", f, deps=tuple(deps), label=f"s6mm_{ncm6}")
            def fsil(e, sem, i=i, hf=hf):
                return e.activation(sil6, g1t[:, hf * 512:(hf + 1) * 512], AF.Silu)
            P.emit("scalar", fsil, deps=(f"s6mm_{ncm6}",) + (() if ncm6 == 0 else (f"cmb6_{ncm6-1}",)), label=f"sil6_{ncm6}")
            def fcmb(e, sem, i=i, hf=hf, b=b):
                sl = slice(hf * 512, (hf + 1) * 512)
                e.tensor_tensor(cmb6s, pb[b][:], sil6, ALU.mult)
                e.drain()
                e.tensor_tensor(x1_tl[:, sl], x1_tl[:, sl], cmb6s, ALU.add)
                return e.drain()
            P.emit("vector", fcmb, deps=(f"sil6_{ncm6}",), label=f"cmb6_{ncm6}")
            ncm6 += 1
        def fstout(e, sem, i=i):
            e.dma_start(out=out.rearrange("(t p) d -> p t d", p=128)[:, i, :], in_=x1_tl).then_inc(sem, 16)
        P.emit("sync", fstout, deps=(f"cmb6_{ncm6-1}",), dma=1, label=f"out6_{i}")
    P.mark_here("sync", "out_done")

    P.build()
    return nc, es


def _finish_stub(P, out, xn_t):
    def st_out(e, sem):
        e.dma_start(out=out.rearrange("(t p) d -> p t d", p=128), in_=xn_t).then_inc(sem, 16)
    P.emit("sync", st_out, deps=(), dma=1, label="out_stored")



def make_consts():
    ident = np.eye(128, dtype=np.float32)
    bm = np.zeros((128, 3, 128), dtype=np.float32)
    k_idx = np.arange(128)[:, None]
    q_idx = np.arange(128)[None, :]
    for ri, rel in enumerate((-1, 0, 1)):
        bm[:, ri, :] = (np.abs(128 * rel + q_idx - k_idx) <= 3).astype(np.float32)
    return {
        "ident_f": ident,
        "bandmask": np.ascontiguousarray(bm.reshape(128, 384)),
    }


def make_in_map(inputs, core):
    m = {
        "x": np.ascontiguousarray(inputs["L0_attn_x"][core]),
        "w_qkv": inputs["L0_attn_w_qkv"],
        "proj": inputs["L0_attn_proj"],
        "w_out": inputs["L0_attn_w_out"],
        "b_out": inputs["L0_attn_b_out"].reshape(1, -1),
        "a_range": inputs["L0_ffn_adaptive_range"].reshape(1, 1),
    }
    for k in ("cn_w1", "cn_w2", "cn_w3", "gc_w1", "gc_w2"):
        m[k] = inputs[f"L0_ffn_{k}"]
    m["cn_w4"] = np.ascontiguousarray(inputs["L0_ffn_cn_w4"].reshape(2, 128).T)
    m["gc_w3"] = np.ascontiguousarray(inputs["L0_ffn_gc_w3"].reshape(2, 128).T)
    for k in ("cn_b1", "cn_b2", "cn_b3", "gc_b1", "gc_b2"):
        m[k] = np.ascontiguousarray(inputs[f"L0_ffn_{k}"].reshape(-1, 128).T)
    for k in ("cn_b4", "gc_b3"):
        m[k] = inputs[f"L0_ffn_{k}"].reshape(1, 1)
    m["wqkv1"] = inputs["L1_attn_Wqkv"]
    m["wproj1"] = inputs["L1_attn_Wproj"]
    m["gamma"] = inputs["L1_ffn_gamma"].reshape(1, -1)
    m["wqT"] = np.ascontiguousarray(inputs["L1_ffn_wq"].T)
    k = inputs["L1_ffn_keys"].transpose(0, 2, 3, 1).reshape(8, 2, 4, 128, 256)
    m["keysT"] = np.ascontiguousarray(k.transpose(3, 0, 1, 2, 4).reshape(128, 64 * 256))
    m["tab_du"] = np.ascontiguousarray(
        np.concatenate([inputs["L1_ffn_table_down"], inputs["L1_ffn_table_up"]], axis=1))
    dh = 64
    inv = 1.0 / (10000.0 ** (np.arange(0, dh, 2, dtype=np.float64) / dh))
    fr = np.arange(T, dtype=np.float64)[:, None] * inv[None, :]
    cosf = np.repeat(np.cos(fr), 2, axis=-1).T
    sinf = np.repeat(np.sin(fr), 2, axis=-1).T
    m["cosq8"] = np.tile(cosf / 8.0, (2, 1)).astype(np.float32)
    m["sinq8"] = np.tile(sinf / 8.0, (2, 1)).astype(np.float32)
    m["cosk"] = np.tile(cosf, (2, 1)).astype(np.float32)
    m["sink"] = np.tile(sinf, (2, 1)).astype(np.float32)
    kq = np.arange(128)
    m["trimask"] = (kq[:, None] <= kq[None, :]).astype(np.float32)
    m.update(make_consts())
    return m


def kernel(**inputs):
    nc, _es = build_program(nphase=6)
    in_maps = [make_in_map(inputs, c) for c in range(8)]
    res = run_bass_kernel_spmd(nc, in_maps, core_ids=list(range(8)))
    return np.stack([np.asarray(res.results[c]["out"]) for c in range(8)], axis=0)


if __name__ == "__main__":
    nc, _ = build_program(nphase=0)
    print("built phase0 ok")


# version 20
# speedup vs baseline: 1.0202x; 1.0053x over previous
import os
import numpy as np
import concourse.bass as bass
import concourse.mybir as mybir
from concourse.bass_utils import run_bass_kernel_spmd
from concourse.bass import IndirectOffsetOnAxis
from contextlib import ExitStack
from math import prod

F32 = mybir.dt.float32
BF16 = mybir.dt.bfloat16
U16 = mybir.dt.uint16
I32 = mybir.dt.int32
AF = mybir.ActivationFunctionType
ALU = mybir.AluOpType
AX = mybir.AxisListType

T = 2048
D = 1024
NT = T // 128
NC = D // 128
DTSZ = {F32: 4, BF16: 2, U16: 2, I32: 4}


class Prog:
    ENGS = ("sync", "gpsimd", "vector", "scalar", "tensor")

    def __init__(self, nc, es):
        self.nc = nc
        self.plan = {e: [] for e in self.ENGS}
        self.sem = {e: es.enter_context(nc.semaphore(f"S{e}")) for e in self.ENGS}
        self.cnt = {e: 0 for e in self.ENGS}
        self.marks = {}

    def emit(self, eng, fn, deps=(), label=None, dma=0):
        waits = tuple(self.marks[d] for d in deps)
        if dma:
            self.cnt[eng] += 16 * dma
        elif label:
            self.cnt[eng] += 1
        if label:
            self.marks[label] = (eng, self.cnt[eng])
        self.plan[eng].append((fn, waits, (label is not None and not dma), dma))

    def mark_here(self, eng, label):
        self.marks[label] = (eng, self.cnt[eng])

    def barrier(self, name):
        cur = {e: (e, self.cnt[e]) for e in self.ENGS}
        for e in self.ENGS:
            self.marks[f"__b_{name}_{e}"] = cur[e]
        for e in self.ENGS:
            deps = tuple(f"__b_{name}_{x}" for x in self.ENGS if x != e)
            def f(en, sem):
                return en.nop()
            self.emit(e, f, deps=deps, label=f"{name}_{e}")

    def build(self):
        nc = self.nc
        with nc.Block() as block:
            for eng in self.ENGS:
                def make(eng):
                    def body(e):
                        for fn, waits, inc1, dma in self.plan[eng]:
                            for weng, val in waits:
                                e.wait_ge(self.sem[weng], val)
                            last = fn(e, self.sem[eng])
                            if inc1:
                                assert last is not None, f"label group on {eng} returned None"
                                last.then_inc(self.sem[eng], 1)
                        e.wait_ge(self.sem[eng], self.cnt[eng])
                    return body
                getattr(block, eng)(make(eng))


class Arena:

    def __init__(self, nc, es, words):
        self.t = es.enter_context(nc.sbuf_tensor("arena", [128, words], F32))
        self.words = words
        self.off = 0

    def alloc(self, shape, dt=F32, rows=128):
        assert shape[0] == rows
        elems = prod(shape[1:])
        w = (elems * DTSZ[dt] + 3) // 4
        assert self.off + w <= self.words, f"arena overflow: {self.off + w} > {self.words}"
        ap = self.t[:, self.off:self.off + w]
        self.off += w
        if dt != F32:
            ap = ap.bitcast(dt)
        n = elems * (4 * w == elems * DTSZ[dt] or True)
        ap = ap[:, 0:elems] if ap.shape[1] != elems else ap
        if len(shape) > 2:
            ap = ap.rearrange("p (" + " ".join(f"a{i}" for i in range(len(shape) - 1)) + ") -> p " +
                              " ".join(f"a{i}" for i in range(len(shape) - 1)),
                              **{f"a{i}": shape[i + 1] for i in range(len(shape) - 1)})
        if rows != 128:
            ap = ap[0:rows]
        return ap

    def alloc_at(self, off, shape, dt=F32, rows=128):
        save = self.off
        self.off = off
        ap = self.alloc(shape, dt, rows)
        self.off = max(save, self.off) if False else save
        return ap

    def mark(self):
        return self.off

    def reset(self, m):
        self.off = m


def build_program(nphase=99, debug=False):
    nc = bass.Bass("TRN2", target_bir_lowering=False)
    es = ExitStack()

    def din(name, shape, dt=F32):
        return nc.dram_tensor(name, shape, dt, kind="ExternalInput").ap()

    def dscratch(name, shape, dt=F32):
        kind = "ExternalOutput" if debug else "Internal"
        return nc.dram_tensor(name, shape, dt, kind=kind).ap()

    x_in = din("x", [T, D])
    w_qkv = din("w_qkv", [D, 3 * D])
    proj_in = din("proj", [128, 64])
    w_out = din("w_out", [D, D])
    b_out = din("b_out", [1, D])
    cn_w1 = din("cn_w1", [D, D]); cn_b1 = din("cn_b1", [128, 8])
    cn_w2 = din("cn_w2", [D, 512]); cn_b2 = din("cn_b2", [128, 4])
    cn_w3 = din("cn_w3", [512, 256]); cn_b3 = din("cn_b3", [128, 2])
    cn_w4 = din("cn_w4", [128, 2]); cn_b4 = din("cn_b4", [1, 1])
    gc_w1 = din("gc_w1", [D, 512]); gc_b1 = din("gc_b1", [128, 4])
    gc_w2 = din("gc_w2", [512, 256]); gc_b2 = din("gc_b2", [128, 2])
    gc_w3 = din("gc_w3", [128, 2]); gc_b3 = din("gc_b3", [1, 1])
    a_range = din("a_range", [1, 1])
    ident_f = din("ident_f", [128, 128])
    bandmask = din("bandmask", [128, 3 * 128])
    wqkv1 = din("wqkv1", [D, 3 * D])
    wproj1 = din("wproj1", [D, D])
    gamma = din("gamma", [1, D])
    wqT = din("wqT", [D, 8192])
    keysT = din("keysT", [128, 64 * 256])
    tab_du = din("tab_du", [65536, 2 * D])
    cosq8 = din("cosq8", [128, T]); sinq8 = din("sinq8", [128, T])
    cosk = din("cosk", [128, T]); sink = din("sink", [128, T])
    trimask = din("trimask", [128, 128])

    out = nc.dram_tensor("out", [T, D], F32, kind="ExternalOutput").ap()

    xn_d = dscratch("xn_d", [128, NT * D])
    xnbf_f_d = dscratch("xnbf_f_d", [128, NC * T], U16)
    v_d = dscratch("v_d", [128, NT * D], U16)
    attnf_d = dscratch("attnf_d", [128, NC * T], U16)
    g_d = dscratch("g_d", [128, NT * D])
    x1_d = dscratch("x1_d", [128, NT * D])
    gf_d = dscratch("gf_d", [1, 1])
    tabdu_bf = dscratch("tabdu_bf", [65536, 2 * D], U16)
    dbg1 = dscratch("dbg1", [128, 16 * 16 * 16])
    dbg2 = dscratch("dbg2", [128, 16 * 16 * 16], U16)
    dbg3 = dscratch("dbg3", [128, 16 * 8 * 16])
    dbg4 = dscratch("dbg4", [128, 16 * 8 * 16], I32)

    P = Prog(nc, es)
    A = Arena(nc, es, 51000)

    pb = [es.enter_context(nc.psum_tensor(f"pb{i}", [128, 512], F32)) for i in range(8)]
    def pbf(i):
        return pb[i][:, 0:64].bitcast(BF16)

    c_identb = A.alloc([128, 128], BF16)
    c_identf = A.alloc([128, 128], F32)
    gf_sb = A.alloc([1, 1], F32, rows=1)
    ln1_rstd = A.alloc([128, NT], F32)
    ln1_nmr = A.alloc([128, NT], F32)
    base_min = A.mark()
    xn_t = A.alloc([128, NT, D], F32)
    m_xn = A.mark()
    fmaj_bf = A.alloc([128, NC, T], BF16)
    cast_scr = A.alloc([128, D], BF16)
    x_tile = A.alloc([128, D], F32)

    def ld_consts(e, sem):
        e.dma_start(out=c_identf, in_=ident_f).then_inc(sem, 16)
    P.emit("sync", ld_consts, dma=1, label="consts0")
    def cast_ident(e, sem):
        e.tensor_copy(c_identb, c_identf)
        return e.drain()
    P.emit("vector", cast_ident, deps=("consts0",), label="consts")

    ph_base = A.mark()

    st_sum = A.alloc([128, NT]); st_sq = A.alloc([128, NT])
    st_mean = A.alloc([128, NT]); st_var = A.alloc([128, NT])
    st_rstd = A.alloc([128, NT]); st_nmr = A.alloc([128, NT])
    sq_scr = A.alloc([128, D])

    for i in range(NT):
        def fx(e, sem, i=i):
            e.dma_start(out=x_tile, in_=x_in.rearrange("(t p) d -> p t d", p=128)[:, i, :]).then_inc(sem, 16)
        P.emit("sync", fx, deps=() if i == 0 else (f"sq_{i-1}",), dma=1, label=f"xld_{i}")
        def f(e, sem, i=i):
            return e.activation(sq_scr, x_tile, AF.Square)
        P.emit("scalar", f, deps=(f"xld_{i}",) + (() if i == 0 else (f"sq_{i-1}",)), label=f"sqdone_{i}")
        def f2(e, sem, i=i):
            e.reduce_sum(st_sum[:, i:i + 1], x_tile, axis=AX.X)
            e.reduce_sum(st_sq[:, i:i + 1], sq_scr, axis=AX.X)
            return e.tensor_copy(xn_t[:, i, :], x_tile)
        P.emit("vector", f2, deps=(f"sqdone_{i}",), label=f"sq_{i}")

    def stats(e, sem):
        e.tensor_scalar_mul(st_mean, st_sum, 1.0 / D)
        e.drain()
        e.tensor_scalar_mul(st_var, st_sq, 1.0 / D)
        e.drain()
        e.tensor_tensor(st_nmr, st_mean, st_mean, ALU.mult)
        e.drain()
        e.tensor_sub(st_var, st_var, st_nmr)
        e.drain()
        return e.tensor_scalar_add(st_var, st_var, 1e-5)
    P.emit("vector", stats, deps=(f"sq_{NT-1}",), label="var_done")
    def rstd1(e, sem):
        return e.activation(st_rstd, st_var, AF.Sqrt)
    P.emit("scalar", rstd1, deps=("var_done",), label="sqrt_done")
    def rstd2(e, sem):
        e.reciprocal(st_rstd, st_rstd)
        e.drain()
        e.tensor_tensor(st_nmr, st_mean, st_rstd, ALU.mult)
        e.drain()
        return e.tensor_scalar_mul(st_nmr, st_nmr, -1.0)
    P.emit("vector", rstd2, deps=("sqrt_done",), label="rstd_done")

    ntp = 0
    for i in range(NT):
        def f(e, sem, i=i):
            e.tensor_scalar(xn_t[:, i, :], xn_t[:, i, :], st_rstd[:, i:i + 1],
                            st_nmr[:, i:i + 1], ALU.mult, ALU.add)
            e.drain()
            e.tensor_copy(cast_scr, xn_t[:, i, :])
            return e.drain()
        P.emit("vector", f, deps=("rstd_done",) if i == 0 else (f"tpev_{ntp-1}",), label=f"xn_{i}")
        TPB0 = (2, 3, 6, 7)
        for c in range(NC):
            b = TPB0[ntp % 4]
            def ftp(e, sem, c=c, b=b):
                return e.transpose(pbf(b), cast_scr[:, c * 128:(c + 1) * 128], c_identb)
            deps = ["consts", f"xn_{i}"]
            if ntp >= 4:
                deps.append(f"tpev_{ntp-4}")
            P.emit("tensor", ftp, deps=tuple(deps), label=f"tp_{ntp}")
            def fev(e, sem, i=i, c=c, b=b):
                return e.tensor_copy(fmaj_bf[:, c, i * 128:(i + 1) * 128], pbf(b))
            P.emit("vector", fev, deps=(f"tp_{ntp}",), label=f"tpev_{ntp}")
            ntp += 1
    P.mark_here("vector", "xnbf_f_done")
    P.mark_here("vector", "xn_all")

    def st_xn(e, sem):
        e.dma_start(out=xn_d, in_=xn_t.rearrange("p a b -> p (a b)")).then_inc(sem, 16)
        e.dma_start(out=xnbf_f_d, in_=fmaj_bf.rearrange("p a b -> p (a b)").bitcast(U16)).then_inc(sem, 16)
    P.emit("sync", st_xn, deps=("xn_all", "xnbf_f_done"), dma=2, label="xn_stored")

    gcw1_bf = A.alloc([128, NC, 512], BF16)
    gcw2_bf = A.alloc([128, 4, 256], BF16)
    gcw3_bf = A.alloc([128, 2, 1], BF16)
    gcb1_t = A.alloc([128, 4]); gcb2_t = A.alloc([128, 2]); gcb3_t = A.alloc([1, 1], rows=1)
    gc_bf = A.alloc([128, NC], BF16)
    gc_col = A.alloc([128, NC])
    g1_col = A.alloc([128, 4]); g1_bf = A.alloc([128, 4], BF16)
    g2_col = A.alloc([128, 2]); g2_bf = A.alloc([128, 2], BF16)

    def ld_gcw(e, sem):
        e.dma_start(out=gcw1_bf, in_=gc_w1.rearrange("(c p) o -> p c o", p=128)).then_inc(sem, 16)
        e.dma_start(out=gcw2_bf, in_=gc_w2.rearrange("(c p) o -> p c o", p=128)).then_inc(sem, 16)
        e.dma_start(out=gcw3_bf, in_=gc_w3.rearrange("p (c o) -> p c o", o=1)).then_inc(sem, 16)
        e.dma_start(out=gcb1_t, in_=gc_b1).then_inc(sem, 16)
        e.dma_start(out=gcb2_t, in_=gc_b2).then_inc(sem, 16)
        e.dma_start(out=gcb3_t, in_=gc_b3).then_inc(sem, 16)
    P.emit("gpsimd", ld_gcw, dma=6, label="gcw_loaded")

    def gc_mean(e, sem):
        for c in range(NC):
            e.reduce_sum(gc_col[:, c:c + 1], fmaj_bf[:, c, :], axis=AX.X)
        e.drain()
        e.tensor_scalar_mul(gc_col, gc_col, 1.0 / T)
        e.drain()
        e.tensor_copy(gc_bf, gc_col)
        return e.drain()
    P.emit("vector", gc_mean, deps=("xnbf_f_done",), label="gc_done")

    p_gc = pb[5][:, 0:4]
    p_gf = pb[5][0:1, 4:5]
    for oc in range(4):
        def f(e, sem, oc=oc):
            for kc in range(NC):
                last = e.matmul(p_gc[:, oc:oc + 1], gcw1_bf[:, kc, oc * 128:(oc + 1) * 128],
                                gc_bf[:, kc:kc + 1], start=(kc == 0), stop=(kc == NC - 1))
            return last
        P.emit("tensor", f, deps=("gcw_loaded", "gc_done") if oc == 0 else (f"gfh1_{oc-1}",), label=f"gfh1_{oc}")
        def fe(e, sem, oc=oc):
            return e.activation(g1_col[:, oc:oc + 1], p_gc[:, oc:oc + 1], AF.Gelu,
                                bias=gcb1_t[:, oc:oc + 1])
        P.emit("scalar", fe, deps=(f"gfh1_{oc}",), label=f"gfh1e_{oc}")
    def f2c(e, sem):
        e.tensor_copy(g1_bf, g1_col)
        return e.drain()
    P.emit("vector", f2c, deps=tuple(f"gfh1e_{oc}" for oc in range(4)), label="g1bf")
    for oc in range(2):
        def f(e, sem, oc=oc):
            for kc in range(4):
                last = e.matmul(p_gc[:, oc:oc + 1], gcw2_bf[:, kc, oc * 128:(oc + 1) * 128],
                                g1_bf[:, kc:kc + 1], start=(kc == 0), stop=(kc == 3))
            return last
        P.emit("tensor", f, deps=("g1bf",) if oc == 0 else (f"gfh2_{oc-1}",), label=f"gfh2_{oc}")
        def fe(e, sem, oc=oc):
            return e.activation(g2_col[:, oc:oc + 1], p_gc[:, oc:oc + 1], AF.Gelu,
                                bias=gcb2_t[:, oc:oc + 1])
        P.emit("scalar", fe, deps=(f"gfh2_{oc}",), label=f"gfh2e_{oc}")
    def f3c(e, sem):
        e.tensor_copy(g2_bf, g2_col)
        return e.drain()
    P.emit("vector", f3c, deps=("gfh2e_0", "gfh2e_1"), label="g2bf")
    def f4(e, sem):
        for kc in range(2):
            last = e.matmul(p_gf, gcw3_bf[0:128, kc, 0:1], g2_bf[:, kc:kc + 1],
                            start=(kc == 0), stop=(kc == 1))
        return last
    P.emit("tensor", f4, deps=("g2bf",), label="gf_mm")
    def f4e(e, sem):
        return e.activation(gf_sb, p_gf, AF.Sigmoid, bias=gcb3_t)
    P.emit("scalar", f4e, deps=("gf_mm",), label="gf_done")
    def st_gf(e, sem):
        e.dma_start(out=gf_d, in_=gf_sb).then_inc(sem, 16)
    P.emit("sync", st_gf, deps=("gf_done",), dma=1, label="gf_stored")

    if nphase < 1:
        _finish_stub(P, out, xn_t)
        P.build()
        return nc, es

    P.barrier("p1")
    A.reset(ph_base)

    c_band = A.alloc([128, 3 * 128], F32)
    proj_bf = A.alloc([128, 64], BF16)
    def ld_p1a(e, sem):
        e.dma_start(out=c_band, in_=bandmask).then_inc(sem, 16)
        e.dma_start(out=proj_bf, in_=proj_in).then_inc(sem, 16)
        e.dma_start(out=wv_bf, in_=w_qkv.rearrange("(c p) o -> p c o", p=128)[:, :, 2 * D:3 * D]).then_inc(sem, 16)
    P.emit("gpsimd", ld_p1a, deps=("p1_gpsimd",), dma=3, label="p1w_loaded")

    v_base = A.mark()
    wv_bf = A.alloc([128, NC, D], BF16)
    v_tile = A.alloc([128, D], BF16)
    nv = 0
    for i in range(NT):
        for hf in range(2):
            b = nv % 2
            def f(e, sem, i=i, hf=hf, b=b):
                for kc in range(NC):
                    last = e.matmul(pb[b][:], fmaj_bf[:, kc, i * 128:(i + 1) * 128],
                                    wv_bf[:, kc, hf * 512:(hf + 1) * 512],
                                    start=(kc == 0), stop=(kc == NC - 1))
                return last
            deps = ["p1w_loaded", "p1_tensor"]
            if nv >= 2:
                deps.append(f"vev_{nv-2}")
            P.emit("tensor", f, deps=tuple(deps), label=f"vmm_{nv}")
            def fe(e, sem, i=i, hf=hf, b=b):
                return e.tensor_copy(v_tile[:, hf * 512:(hf + 1) * 512], pb[b][:])
            depv = [f"vmm_{nv}"]
            if hf == 0 and i > 0:
                depv.append(f"vst_{i-1}")
            P.emit("vector", fe, deps=tuple(depv), label=f"vev_{nv}")
            nv += 1
        def fst(e, sem, i=i):
            e.dma_start(out=v_d.rearrange("p (t d) -> p t d", t=NT)[:, i, :], in_=v_tile.bitcast(U16)).then_inc(sem, 16)
        P.emit("sync", fst, deps=(f"vev_{nv-1}",), dma=1, label=f"vst_{i}")
        if i < NT - 1:
            P.marks[f"vwar_{i}"] = P.marks[f"vst_{i}"]
    for i in range(NT - 1):
        pass

    A.reset(v_base)
    wqk_bf = A.alloc([128, NC, 4 * 128], BF16)
    qk_bf = A.alloc([128, 2, T], BF16)
    qphi = A.alloc([128, T], F32)
    kphi = A.alloc([128, T], F32)
    qphi_bf = A.alloc([128, T], BF16)
    kphi_bf = A.alloc([128, T], BF16)
    ksum = A.alloc([128, T], F32)
    zrow = A.alloc([128, T], F32, rows=128)
    invzT = A.alloc([128, NT, 2], F32)
    phi_min = A.alloc([128, 512], F32)
    phi_max = A.alloc([128, 512], F32)
    phi_exp = A.alloc([128, 512], F32)
    stb = A.alloc([128, 3, 384], BF16)
    attn_sc = A.alloc([128, 3, 128], BF16)
    attn_head = A.alloc([128, 2, T], BF16)
    vh_bf = A.alloc([128, 2, NT, 128], BF16)
    blk1 = A.alloc([128, 2], F32)

    def fb1(e, sem):
        e.memset(blk1, 0.0)
        e.drain()
        e.memset(blk1[0:64, 0:1], 1.0)
        e.drain()
        e.memset(blk1[64:128, 1:2], 1.0)
        return e.drain()
    P.emit("vector", fb1, deps=("p1_vector",), label="blk1_done")

    nqk = 0
    nst = 0
    npv = 0
    ntz = 0
    for ph in range(4):
        def ldw(e, sem, ph=ph):
            w3 = w_qkv.rearrange("(c p) o -> p c o", p=128)
            e.dma_start(out=wqk_bf[:, :, 0:256], in_=w3[:, :, ph * 256:(ph + 1) * 256]).then_inc(sem, 16)
            e.dma_start(out=wqk_bf[:, :, 256:512], in_=w3[:, :, D + ph * 256:D + (ph + 1) * 256]).then_inc(sem, 16)
        deps = ("p1_gpsimd", f"vst_{NT-1}") if ph == 0 else (f"phibf_{ph-1}",)
        P.emit("gpsimd", ldw, deps=deps, dma=2, label=f"wqk_{ph}")

        for qk in range(2):
            for hh in range(2):
                for tc in range(4):
                    b = nqk % 2
                    def f(e, sem, qk=qk, hh=hh, tc=tc, b=b):
                        for kc in range(NC):
                            last = e.matmul(pb[b][:], wqk_bf[:, kc, qk * 256 + hh * 128: qk * 256 + (hh + 1) * 128],
                                            fmaj_bf[:, kc, tc * 512:(tc + 1) * 512],
                                            start=(kc == 0), stop=(kc == NC - 1))
                        return last
                    deps = [f"wqk_{ph}"]
                    if nqk == 0:
                        deps.append(f"vev_{nv-1}")
                    if nqk >= 2:
                        deps.append(f"qkpost_{nqk-2}")
                    if npv >= 1 and qk == 0 and hh == 0 and tc == 0:
                        deps.append(f"pvev_{npv-1}")
                    P.emit("tensor", f, deps=tuple(deps), label=f"qkmm_{nqk}")
                    def fe(e, sem, hh=hh, tc=tc, b=b):
                        return e.tensor_copy(qk_bf[:, hh, tc * 512:(tc + 1) * 512], pb[b][:])
                    P.emit("vector", fe, deps=(f"qkmm_{nqk}",), label=f"qkpost_{nqk}")
                    nqk += 1
            phit = qphi if qk == 0 else kphi
            for tc in range(4):
                b = nqk % 2
                def f(e, sem, tc=tc, b=b):
                    e.matmul(pb[b][0:64, :], proj_bf[:, 0:64], qk_bf[:, 0, tc * 512:(tc + 1) * 512],
                             start=True, stop=True)
                    return e.matmul(pb[b][64:128, :], proj_bf[:, 0:64], qk_bf[:, 1, tc * 512:(tc + 1) * 512],
                                    start=True, stop=True)
                deps = [f"qkpost_{nqk-1}"]
                if nqk >= 2:
                    deps.append(f"qkpost_{nqk-2}")
                P.emit("tensor", f, deps=tuple(deps), label=f"qkmm_{nqk}")
                def fel(e, sem, b=b):
                    e.tensor_scalar_min(phi_min, pb[b][:], 0.0)
                    e.tensor_scalar_max(phi_max, pb[b][:], 0.0)
                    return e.drain()
                P.emit("vector", fel, deps=(f"qkmm_{nqk}",), label=f"qkpost_{nqk}_a")
                def fes(e, sem):
                    return e.activation(phi_exp, phi_min, AF.Exp)
                P.emit("scalar", fes, deps=(f"qkpost_{nqk}_a",), label=f"qkpost_{nqk}_b")
                def fea(e, sem, phit=phit, tc=tc):
                    e.tensor_add(phit[:, tc * 512:(tc + 1) * 512], phi_exp, phi_max)
                    return e.drain()
                P.emit("vector", fea, deps=(f"qkpost_{nqk}_b",), label=f"qkpost_{nqk}")
                nqk += 1
        def fcast(e, sem):
            e.tensor_copy(qphi_bf, qphi)
            e.tensor_copy(kphi_bf, kphi)
            return e.drain()
        P.emit("vector", fcast, deps=(f"qkpost_{nqk-1}",), label=f"phibf_{ph}")

        def fks(e, sem):
            e.tensor_copy(ksum, kphi)
            e.drain()
            for o in (-3, -2, -1, 1, 2, 3):
                if o > 0:
                    e.tensor_add(ksum[:, 0:T - o], ksum[:, 0:T - o], kphi[:, o:T])
                else:
                    e.tensor_add(ksum[:, -o:T], ksum[:, -o:T], kphi[:, 0:T + o])
                e.drain()
            e.tensor_tensor(ksum, ksum, qphi, ALU.mult)
            return e.drain()
        P.emit("vector", fks, deps=(f"phibf_{ph}",), label=f"ks_{ph}")

        for tc in range(4):
            def fz(e, sem, tc=tc):
                return e.matmul(pb[4][0:2, :], blk1, ksum[:, tc * 512:(tc + 1) * 512],
                                start=True, stop=True)
            deps = ["blk1_done", f"ks_{ph}"]
            if not (ph == 0 and tc == 0):
                deps.append(f"zev_{ph}_{tc-1}" if tc > 0 else f"zev_{ph-1}_3")
            if ph == 0 and tc == 0:
                deps.append("gf_done")
            P.emit("tensor", fz, deps=tuple(deps), label=f"zmm_{ph}_{tc}")
            def fzev(e, sem, tc=tc):
                return e.tensor_copy(zrow[0:2, tc * 512:(tc + 1) * 512], pb[4][0:2, :])
            P.emit("vector", fzev, deps=(f"zmm_{ph}_{tc}",), label=f"zev_{ph}_{tc}")

        def finvz(e, sem):
            e.tensor_scalar_add(zrow[0:2, :], zrow[0:2, :], 1e-8)
            e.drain()
            e.reciprocal(zrow[0:2, :], zrow[0:2, :])
            return e.drain()
        P.emit("vector", finvz, deps=(f"zev_{ph}_3",), label=f"invz_{ph}")

        for qt in range(NT):
            def ftz(e, sem, qt=qt):
                return e.transpose(pb[4][0:128, 256:258], zrow[0:2, qt * 128:(qt + 1) * 128], c_identf[0:2, 0:2])
            deps = [f"invz_{ph}"]
            if ntz >= 1:
                deps.append(f"tzev_{ntz-1}")
            P.emit("tensor", ftz, deps=tuple(deps), label=f"tz_{ntz}")
            def ftzev(e, sem, qt=qt):
                return e.tensor_copy(invzT[:, qt, :], pb[4][0:128, 256:258])
            P.emit("vector", ftzev, deps=(f"tz_{ntz}",), label=f"tzev_{ntz}")
            ntz += 1
        P.mark_here("vector", f"invzT_{ph}")

        def ldvh(e, sem, ph=ph):
            v3 = v_d.rearrange("p (t d) -> p t d", t=NT)
            e.dma_start(out=vh_bf[:, 0, :, :].bitcast(U16), in_=v3[:, :, (2 * ph) * 128:(2 * ph + 1) * 128]).then_inc(sem, 16)
            e.dma_start(out=vh_bf[:, 1, :, :].bitcast(U16), in_=v3[:, :, (2 * ph + 1) * 128:(2 * ph + 2) * 128]).then_inc(sem, 16)
        deps = [f"vst_{NT-1}"]
        if ph > 0:
            deps.append(f"pvdone_{ph-1}")
        P.emit("gpsimd", ldvh, deps=tuple(deps), dma=2, label=f"vh_{ph}")

        for hh in range(2):
            rb = hh * 64
            def emit_st(kt):
                nonlocal nst
                qts = [q for q in (kt - 1, kt, kt + 1) if 0 <= q < NT]
                b = nst % 2
                def fst(e, sem, rb=rb, kt=kt, qts=qts, b=b):
                    first = True
                    for q in qts:
                        reg = q - kt + 1
                        last = e.matmul(pb[2 + b][:, 0:384][:, reg * 128:(reg + 1) * 128],
                                        kphi_bf[rb:rb + 64, kt * 128:(kt + 1) * 128],
                                        qphi_bf[rb:rb + 64, q * 128:(q + 1) * 128],
                                        start=first, stop=(q == qts[-1]))
                        first = False
                    return last
                deps = [f"phibf_{ph}", f"invzT_{ph}"]
                if nst >= 2:
                    deps.append(f"stev_{nst-2}")
                if npv >= 1 and kt >= 3:
                    deps.append(f"pvmm_{npv-1}")
                P.emit("tensor", fst, deps=tuple(deps), label=f"stmm_{nst}")
                def fstev(e, sem, kt=kt, qts=qts, b=b):
                    r0 = qts[0] - kt + 1
                    r1 = qts[-1] - kt + 2
                    return e.tensor_tensor(stb[:, kt % 3, r0 * 128:r1 * 128],
                                           pb[2 + b][:, 0:384][:, r0 * 128:r1 * 128],
                                           c_band[:, r0 * 128:r1 * 128], ALU.mult)
                P.emit("vector", fstev, deps=(f"stmm_{nst}",), label=f"stev_{nst}")
                nst += 1

            TPB1 = (5, 6, 7)
            def emit_pv(qt):
                nonlocal npv
                kts = [k for k in (qt - 1, qt, qt + 1) if 0 <= k < NT]
                b = npv % 2
                def fpv(e, sem, hh=hh, qt=qt, kts=kts, b=b):
                    first = True
                    for k in kts:
                        reg = qt - k + 1
                        last = e.matmul(pb[b][:, 0:128], stb[:, k % 3, reg * 128:(reg + 1) * 128],
                                        vh_bf[:, hh, k, :],
                                        start=first, stop=(k == kts[-1]))
                        first = False
                    return last
                deps = [f"stev_{nst-1}", f"vh_{ph}", f"qkpost_{nqk-1}"]
                if npv >= 2:
                    deps.append(f"pvev_{npv-2}")
                P.emit("tensor", fpv, deps=tuple(deps), label=f"pvmm_{npv}")
                bs = npv % 3
                def fpvev(e, sem, hh=hh, qt=qt, b=b, bs=bs):
                    e.tensor_scalar_mul(attn_sc[:, bs, :], pb[b][:, 0:128], invzT[:, qt, hh:hh + 1])
                    return e.drain()
                depv = [f"pvmm_{npv}"]
                if npv >= 3:
                    depv.append(f"pvtp_{npv-3}")
                P.emit("vector", fpvev, deps=tuple(depv), label=f"pvev_{npv}")
                def fpvtp(e, sem, bs=bs):
                    return e.transpose(pbf(TPB1[bs]), attn_sc[:, bs, :], c_identb)
                dept = [f"pvev_{npv}"]
                if npv >= 3:
                    dept.append(f"pvtev_{npv-3}")
                if npv <= 2:
                    dept.append("gf_stored")
                P.emit("tensor", fpvtp, deps=tuple(dept), label=f"pvtp_{npv}")
                def fpvtev(e, sem, hh=hh, qt=qt, bs=bs):
                    return e.tensor_copy(attn_head[:, hh, qt * 128:(qt + 1) * 128], pbf(TPB1[bs]))
                P.emit("vector", fpvtev, deps=(f"pvtp_{npv}",), label=f"pvtev_{npv}")
                npv += 1

            for kt in range(NT):
                emit_st(kt)
                if kt >= 1:
                    emit_pv(kt - 1)
            emit_pv(NT - 1)
        P.mark_here("vector", f"pvdone_{ph}")
        def sthead(e, sem, ph=ph):
            a3 = attnf_d.rearrange("p (c t) -> p c t", c=NC)
            e.dma_start(out=a3[:, 2 * ph, :], in_=attn_head[:, 0, :].bitcast(U16)).then_inc(sem, 16)
            e.dma_start(out=a3[:, 2 * ph + 1, :], in_=attn_head[:, 1, :].bitcast(U16)).then_inc(sem, 16)
        P.emit("sync", sthead, deps=(f"pvdone_{ph}",), dma=2, label=f"ah_{ph}")
    P.mark_here("sync", "attn_stored")

    if nphase < 2:
        _finish_stub(P, out, xn_t)
        P.build()
        return nc, es

    P.barrier("p2")
    A.reset(ph_base)

    cnw1_bf = A.alloc([128, NC, D], BF16)
    cnw2_bf = A.alloc([128, NC, 512], BF16)
    cnw3_bf = A.alloc([128, 4, 256], BF16)
    cnw4_bf = A.alloc([128, 2, 1], BF16)
    cnb1_t = A.alloc([128, NC]); cnb2_t = A.alloc([128, 4])
    cnb3_t = A.alloc([128, 2]); cnb4_t = A.alloc([1, 1], rows=1)
    ar_sb = A.alloc([1, 1], rows=1)
    c0_sb = A.alloc([1, 1], rows=1); c1_sb = A.alloc([1, 1], rows=1)
    tt1 = A.alloc([1, 1], rows=1)
    h1_bf = A.alloc([128, NC, T], BF16)
    h2_bf = A.alloc([128, 4, T], BF16)
    h3_bf = A.alloc([128, 2, T], BF16)
    cp_row = A.alloc([1, T], rows=1)
    cpT = A.alloc([128, NT]); cpTn = A.alloc([128, NT])
    tw_tmp = A.alloc([128, 2, 256])

    def ld_cnw(e, sem):
        e.dma_start(out=cnw1_bf, in_=cn_w1.rearrange("(c p) o -> p c o", p=128)).then_inc(sem, 16)
        e.dma_start(out=cnw2_bf, in_=cn_w2.rearrange("(c p) o -> p c o", p=128)).then_inc(sem, 16)
        e.dma_start(out=cnw3_bf, in_=cn_w3.rearrange("(c p) o -> p c o", p=128)).then_inc(sem, 16)
        e.dma_start(out=cnw4_bf, in_=cn_w4.rearrange("p (c o) -> p c o", o=1)).then_inc(sem, 16)
        e.dma_start(out=cnb1_t, in_=cn_b1).then_inc(sem, 16)
        e.dma_start(out=cnb2_t, in_=cn_b2).then_inc(sem, 16)
        e.dma_start(out=cnb3_t, in_=cn_b3).then_inc(sem, 16)
        e.dma_start(out=cnb4_t, in_=cn_b4).then_inc(sem, 16)
        e.dma_start(out=ar_sb, in_=a_range).then_inc(sem, 16)
    P.emit("gpsimd", ld_cnw, deps=("p2_gpsimd",), dma=9, label="cnw_loaded")

    def fcc(e, sem):
        e.tensor_scalar_mul(tt1, gf_sb, 0.7)
        e.drain()
        e.tensor_scalar_add(tt1, tt1, -0.5)
        e.drain()
        e.tensor_tensor(tt1, tt1, ar_sb, ALU.mult)
        e.drain()
        e.tensor_scalar_mul(tt1, tt1, 2.0)
        e.drain()
        e.tensor_scalar_add(c0_sb, tt1, 0.1)
        e.drain()
        e.tensor_scalar_mul(c1_sb, ar_sb, 0.6)
        return e.drain()
    P.emit("vector", fcc, deps=("gf_done", "cnw_loaded", "p2_vector"), label="cc_done")

    nmm = 0
    nftp = 0
    for cyc in range(3):
        for i in range(NT):
            def fc(e, sem, i=i, cyc=cyc):
                e.tensor_copy(cast_scr, xn_t[:, i, :])
                return e.drain()
            dep0 = ["cc_done"] if (cyc == 0 and i == 0) else [f"ftpev_{nftp-1}"]
            if cyc > 0 and i == 0:
                dep0 = [f"tw_done_{cyc-1}"]
            P.emit("vector", fc, deps=tuple(dep0), label=f"stc_{cyc}_{i}")
            for c in range(NC):
                b = nftp % 4
                def ftp(e, sem, c=c, b=b):
                    return e.transpose(pbf(b), cast_scr[:, c * 128:(c + 1) * 128], c_identb)
                deps = [f"stc_{cyc}_{i}"]
                if nftp >= 4:
                    deps.append(f"ftpev_{nftp-4}")
                if nftp < 4 and cyc == 0:
                    deps += [f"tzev_{ntz-1}", f"pvtev_{npv-1}", f"qkpost_{nqk-1}", f"stev_{nst-1}"]
                P.emit("tensor", ftp, deps=tuple(deps), label=f"ftp_{nftp}")
                def fev(e, sem, i=i, c=c, b=b):
                    return e.tensor_copy(fmaj_bf[:, c, i * 128:(i + 1) * 128], pbf(b))
                P.emit("vector", fev, deps=(f"ftp_{nftp}",), label=f"ftpev_{nftp}")
                nftp += 1
        last_tp = f"ftpev_{nftp-1}"

        def chain(win, kcs, ocs, src, dst, bias_t, prev):
            nonlocal nmm
            for oc in range(ocs):
                for tc in range(4):
                    b = nmm % 4
                    def f(e, sem, oc=oc, tc=tc, b=b, win=win, kcs=kcs, src=src):
                        for kc in range(kcs):
                            last = e.matmul(pb[b][:], win[:, kc, oc * 128:(oc + 1) * 128],
                                            src[:, kc, tc * 512:(tc + 1) * 512],
                                            start=(kc == 0), stop=(kc == kcs - 1))
                        return last
                    deps = list(prev) if (oc == 0 and tc == 0) else []
                    if nmm < 4:
                        deps.append(f"ftpev_{nftp-1}" if False else f"mme_{nmm-2}") if nmm >= 2 else None
                    if nmm >= 4:
                        deps.append(f"mme_{nmm-4}")
                    P.emit("tensor", f, deps=tuple(deps), label=f"mm_{nmm}")
                    def fe(e, sem, oc=oc, tc=tc, b=b, dst=dst, bias_t=bias_t):
                        return e.activation(dst[:, oc, tc * 512:(tc + 1) * 512], pb[b][:],
                                            AF.Gelu, bias=bias_t[:, oc:oc + 1])
                    P.emit("scalar", fe, deps=(f"mm_{nmm}",), label=f"mme_{nmm}")
                    nmm += 1
            return f"mme_{nmm-1}"

        prev0 = ["cnw_loaded", last_tp]
        if cyc == 0:
            prev0 += [f"pvtev_{npv-1}", f"qkpost_{nqk-1}"]
        l1 = chain(cnw1_bf, NC, NC, fmaj_bf, h1_bf, cnb1_t, tuple(prev0))
        l2 = chain(cnw2_bf, NC, 4, h1_bf, h2_bf, cnb2_t, (l1,))
        l3 = chain(cnw3_bf, 4, 2, h2_bf, h3_bf, cnb3_t, (l2,))
        for tc in range(4):
            def f(e, sem, tc=tc):
                for kc in range(2):
                    last = e.matmul(pb[4][0:1, :], cnw4_bf[0:128, kc, 0:1],
                                    h3_bf[:, kc, tc * 512:(tc + 1) * 512],
                                    start=(kc == 0), stop=(kc == 1))
                return last
            deps = [l3] if tc == 0 else [f"t4e_{cyc}_{tc-1}"]
            if cyc == 0 and tc == 0:
                deps.append(f"tzev_{ntz-1}")
            P.emit("tensor", f, deps=tuple(deps), label=f"t4_{cyc}_{tc}")
            def fe(e, sem, tc=tc):
                return e.activation(cp_row[:, tc * 512:(tc + 1) * 512], pb[4][0:1, :], AF.Sigmoid,
                                    bias=cnb4_t)
            P.emit("scalar", fe, deps=(f"t4_{cyc}_{tc}",), label=f"t4e_{cyc}_{tc}")
        def fcp(e, sem):
            e.tensor_scalar(cp_row, cp_row, c1_sb, c0_sb, ALU.mult, ALU.add)
            return e.drain()
        P.emit("vector", fcp, deps=(f"t4e_{cyc}_3",), label=f"cprow_{cyc}")
        for qt in range(NT):
            def ftc(e, sem, qt=qt):
                return e.transpose(pb[4][0:128, 256:257], cp_row[:, qt * 128:(qt + 1) * 128], c_identf[0:1, 0:1])
            deps = [f"cprow_{cyc}"]
            if qt > 0:
                deps.append(f"cpev_{cyc}_{qt-1}")
            P.emit("tensor", ftc, deps=tuple(deps), label=f"cptp_{cyc}_{qt}")
            def ftcev(e, sem, qt=qt):
                return e.tensor_copy(cpT[:, qt:qt + 1], pb[4][0:128, 256:257])
            P.emit("vector", ftcev, deps=(f"cptp_{cyc}_{qt}",), label=f"cpev_{cyc}_{qt}")
        def fneg(e, sem):
            e.tensor_scalar_mul(cpTn, cpT, -1.0)
            return e.drain()
        P.emit("vector", fneg, deps=(f"cpev_{cyc}_{NT-1}",), label=f"cpneg_{cyc}")
        for i in range(NT):
            def ftw(e, sem, i=i):
                q = lambda k: xn_t[:, i, k * 256:(k + 1) * 256]
                e.tensor_copy(tw_tmp[:, 0, :], q(0))
                e.tensor_copy(tw_tmp[:, 1, :], q(1))
                e.drain()
                e.scalar_tensor_tensor(q(0), q(2), cpT[:, i:i + 1], q(0), ALU.mult, ALU.add)
                e.scalar_tensor_tensor(q(1), q(3), cpTn[:, i:i + 1], q(1), ALU.mult, ALU.add)
                e.drain()
                e.scalar_tensor_tensor(q(2), tw_tmp[:, 0, :], cpTn[:, i:i + 1], q(2), ALU.mult, ALU.add)
                e.scalar_tensor_tensor(q(3), tw_tmp[:, 1, :], cpT[:, i:i + 1], q(3), ALU.mult, ALU.add)
                return e.drain()
            P.emit("vector", ftw, deps=(f"cpneg_{cyc}",), label=f"tw_{cyc}_{i}")
        P.mark_here("vector", f"tw_done_{cyc}")

    def st_g(e, sem):
        e.dma_start(out=g_d, in_=xn_t.rearrange("p a b -> p (a b)")).then_inc(sem, 16)
    P.emit("sync", st_g, deps=("tw_done_2",), dma=1, label="g_stored")

    if nphase < 3:
        _finish_stub(P, out, xn_t)
        P.build()
        return nc, es

    P.barrier("p3")
    A.reset(ph_base)

    wout_bf = A.alloc([128, NC, D], BF16)
    bout_row = A.alloc([1, D], rows=1)
    bout_bc = A.alloc([128, D])
    ones1 = A.alloc([1, 128], rows=1)
    sil = A.alloc([128, 512])
    cmb = A.alloc([128, 512])
    af_sl = A.alloc([128, 2, NC, 128], BF16)

    def ld_wout(e, sem):
        e.dma_start(out=wout_bf, in_=w_out.rearrange("(c p) o -> p c o", p=128)).then_inc(sem, 16)
        e.dma_start(out=bout_row, in_=b_out).then_inc(sem, 16)
    P.emit("gpsimd", ld_wout, deps=("p3_gpsimd",), dma=2, label="wout_loaded")
    def fones(e, sem):
        e.memset(ones1, 1.0)
        return e.drain()
    P.emit("vector", fones, deps=("p3_vector",), label="ones_done")
    for hf in range(2):
        def fbb(e, sem, hf=hf):
            return e.matmul(pb[hf][:], ones1, bout_row[:, hf * 512:(hf + 1) * 512],
                            start=True, stop=True)
        P.emit("tensor", fbb, deps=("ones_done", "wout_loaded", "p3_tensor"), label=f"bb_{hf}")
        def fbbe(e, sem, hf=hf):
            return e.tensor_copy(bout_bc[:, hf * 512:(hf + 1) * 512], pb[hf][:])
        P.emit("vector", fbbe, deps=(f"bb_{hf}",), label=f"bbe_{hf}")

    ncm = 0
    for i in range(NT):
        def ldaf(e, sem, i=i):
            a3 = attnf_d.rearrange("p (c t) -> p c t", c=NC)
            e.dma_start(out=af_sl[:, i % 2, :, :].bitcast(U16), in_=a3[:, :, i * 128:(i + 1) * 128]).then_inc(sem, 16)
        depl = ["attn_stored", "p3_gpsimd"]
        if i >= 2:
            depl.append(f"cmb_{2*(i-2)+1}")
        P.emit("gpsimd", ldaf, deps=tuple(depl), dma=1, label=f"afld_{i}")
        def ldx(e, sem, i=i):
            e.dma_start(out=x_tile, in_=x_in.rearrange("(t p) d -> p t d", p=128)[:, i, :]).then_inc(sem, 16)
        depx = ["p3_sync"]
        if i >= 1:
            depx.append(f"cmb_{2*(i-1)+1}")
        P.emit("sync", ldx, deps=tuple(depx), dma=1, label=f"xld2_{i}")
        for hf in range(2):
            b = 2 + (ncm % 2)
            def f(e, sem, i=i, hf=hf, b=b):
                for kc in range(NC):
                    last = e.matmul(pb[b][:], af_sl[:, i % 2, kc, :],
                                    wout_bf[:, kc, hf * 512:(hf + 1) * 512],
                                    start=(kc == 0), stop=(kc == NC - 1))
                return last
            deps = [f"afld_{i}"]
            if ncm == 0:
                deps += ["bbe_0", "bbe_1", "g_stored"]
            if ncm >= 2:
                deps.append(f"cmb_{ncm-2}")
            P.emit("tensor", f, deps=tuple(deps), label=f"smm_{ncm}")
            def fsil(e, sem, i=i, hf=hf):
                return e.activation(sil, xn_t[:, i, hf * 512:(hf + 1) * 512], AF.Silu)
            P.emit("scalar", fsil, deps=(f"smm_{ncm}",) + (() if ncm == 0 else (f"cmb_{ncm-1}",)), label=f"sil_{ncm}")
            def fcmb(e, sem, i=i, hf=hf, b=b):
                sl = slice(hf * 512, (hf + 1) * 512)
                e.tensor_tensor(cmb, pb[b][:], bout_bc[:, sl], ALU.add)
                e.drain()
                e.tensor_tensor(cmb, cmb, sil, ALU.mult)
                e.drain()
                e.tensor_tensor(xn_t[:, i, sl], x_tile[:, sl], cmb, ALU.add)
                return e.drain()
            P.emit("vector", fcmb, deps=(f"sil_{ncm}", f"xld2_{i}"), label=f"cmb_{ncm}")
            ncm += 1
    P.mark_here("vector", "x1_done")

    def st_x1(e, sem):
        e.dma_start(out=x1_d, in_=xn_t.rearrange("p a b -> p (a b)")).then_inc(sem, 16)
    P.emit("sync", st_x1, deps=("x1_done",), dma=1, label="x1_stored")

    if nphase < 4:
        def st_out(e, sem):
            e.dma_start(out=out.rearrange("(t p) d -> p t d", p=128), in_=xn_t).then_inc(sem, 16)
        P.emit("sync", st_out, deps=("x1_done",), dma=1, label="out_stored")
        P.build()
        return nc, es

    P.barrier("p4")
    A.reset(m_xn)
    xn2_t = A.alloc([128, NT, D], BF16)
    p5_base = A.mark()

    st_sum4 = A.alloc([128, NT]); st_sq4 = A.alloc([128, NT])
    st_mean4 = A.alloc([128, NT]); st_var4 = A.alloc([128, NT])
    st_a2 = A.alloc([128, NT]); st_b2 = A.alloc([128, NT])
    st_sv = A.alloc([128, NT])
    sq_scr4 = A.alloc([128, D])
    gam_bc = A.alloc([128, D])
    gam_row = A.alloc([1, D], rows=1)
    ones4 = A.alloc([1, 128], rows=1)
    xn1_scr = A.alloc([128, D])
    conv_buf = A.alloc([128, 2, 2, 2 * D], BF16)

    NCHUNK = 0 if os.environ.get("SKIP_CONV") else 256
    ncv = 0
    for tab_in, tab_out in (((tab_du, tabdu_bf),) if NCHUNK else ()):
        for ch in range(NCHUNK):
            bcv = ncv % 2
            def fldc(e, sem, tab_in=tab_in, ch=ch, bcv=bcv):
                src_ap = tab_in[ch * 256:(ch + 1) * 256, :].rearrange("(r p) d -> p r d", p=128)
                e.dma_start(out=conv_buf[:, bcv, :, :], in_=src_ap).then_inc(sem, 16)
            depc = ["p4_gpsimd"]
            if ncv >= 2:
                depc.append(f"cvst_{ncv-2}")
            P.emit("gpsimd", fldc, deps=tuple(depc), dma=1, label=f"cvld_{ncv}")
            def fstc(e, sem, tab_out=tab_out, ch=ch, bcv=bcv):
                dst_ap = tab_out[ch * 256:(ch + 1) * 256, :].rearrange("(r p) d -> p r d", p=128)
                e.dma_start(out=dst_ap, in_=conv_buf[:, bcv, :, :].bitcast(U16)).then_inc(sem, 16)
            P.emit("sync", fstc, deps=(f"cvld_{ncv}",), dma=1, label=f"cvst_{ncv}")
            ncv += 1
    P.mark_here("sync", "tables_done")
    NPH16 = 1 if os.environ.get("SMALL_A") else 16
    NTI5 = 1 if os.environ.get("SMALL_A") else NT

    for i in range(NT):
        def f(e, sem, i=i):
            return e.activation(sq_scr4, xn_t[:, i, :], AF.Square)
        P.emit("scalar", f, deps=("p4_scalar", "x1_done") if i == 0 else (f"sq4_{i-1}",), label=f"sqd4_{i}")
        def f2(e, sem, i=i):
            e.reduce_sum(st_sum4[:, i:i + 1], xn_t[:, i, :], axis=AX.X)
            return e.reduce_sum(st_sq4[:, i:i + 1], sq_scr4, axis=AX.X)
        P.emit("vector", f2, deps=(f"sqd4_{i}", "p4_vector"), label=f"sq4_{i}")
    def stats4(e, sem):
        e.tensor_scalar_mul(st_mean4, st_sum4, 1.0 / D)
        e.drain()
        e.tensor_scalar_mul(st_var4, st_sq4, 1.0 / D)
        e.drain()
        e.tensor_tensor(st_b2, st_mean4, st_mean4, ALU.mult)
        e.drain()
        e.tensor_sub(st_var4, st_var4, st_b2)
        e.drain()
        return e.tensor_scalar_add(st_a2, st_var4, 1e-5)
    P.emit("vector", stats4, deps=(f"sq4_{NT-1}",), label="var4_done")
    def rstd4a(e, sem):
        e.activation(ln1_rstd, st_a2, AF.Sqrt)
        return e.activation(st_sv, st_var4, AF.Sqrt)
    P.emit("scalar", rstd4a, deps=("var4_done",), label="sqrt4_done")
    def rstd4b(e, sem):
        e.reciprocal(ln1_rstd, ln1_rstd)
        e.drain()
        e.tensor_tensor(ln1_nmr, st_mean4, ln1_rstd, ALU.mult)
        e.drain()
        e.tensor_scalar_mul(ln1_nmr, ln1_nmr, -1.0)
        e.drain()
        e.tensor_tensor(st_a2, ln1_rstd, st_sv, ALU.mult)
        e.drain()
        e.reciprocal(st_a2, st_a2)
        e.drain()
        e.tensor_tensor(st_b2, ln1_nmr, st_a2, ALU.mult)
        e.drain()
        e.tensor_tensor(st_a2, ln1_rstd, st_a2, ALU.mult)
        return e.drain()
    P.emit("vector", rstd4b, deps=("sqrt4_done",), label="ln1_done")

    def fon4(e, sem):
        e.memset(ones4, 1.0)
        return e.drain()
    P.emit("vector", fon4, deps=("p4_vector",), label="ones4")
    def ldgam(e, sem):
        e.dma_start(out=gam_row, in_=gamma).then_inc(sem, 16)
    P.emit("sync", ldgam, deps=("p4_sync",), dma=1, label="gam_loaded")
    for hf in range(2):
        def fgb(e, sem, hf=hf):
            return e.matmul(pb[6][:], ones4, gam_row[:, hf * 512:(hf + 1) * 512], start=True, stop=True)
        P.emit("tensor", fgb, deps=("ones4", "gam_loaded", "p4_tensor") if hf == 0 else (f"gbe_{hf-1}",), label=f"gb_{hf}")
        def fgbe(e, sem, hf=hf):
            return e.tensor_copy(gam_bc[:, hf * 512:(hf + 1) * 512], pb[6][:])
        P.emit("vector", fgbe, deps=(f"gb_{hf}",), label=f"gbe_{hf}")

    for i in range(NT):
        def fxn2(e, sem, i=i):
            e.tensor_scalar(xn1_scr, xn_t[:, i, :], st_a2[:, i:i + 1], st_b2[:, i:i + 1],
                            ALU.mult, ALU.add)
            e.drain()
            e.tensor_tensor(xn2_t[:, i, :], xn1_scr, gam_bc, ALU.mult)
            return e.drain()
        P.emit("vector", fxn2, deps=("ln1_done", "gbe_1") if i == 0 else (f"xn2_{i-1}",), label=f"xn2_{i}")
    P.mark_here("vector", "xn2t_done")

    if nphase < 5:
        _finish_stub(P, out, xn_t)
        P.build()
        return nc, es

    P.barrier("p5")
    A.reset(p5_base)
    xn2_f = A.alloc_at(base_min, [128, NC, T], BF16)
    keysT_sb = A.alloc_at(base_min + 8192, [128, 64, 256], BF16)
    wdu = A.alloc_at(base_min, [128, 2, 8, 2 * D], BF16)
    wqsl_off = A.mark()
    wq_sl = A.alloc([128, 2, NC, 512], BF16)
    qf_sb = A.alloc([128, 4, 128], BF16)
    sim_sb = A.alloc([128, 256], F32)
    sim2_sb = A.alloc([128, 256], F32)
    sx_all = A.alloc([128, NT, 16, 16], F32)
    ix_all = A.alloc([128, NT, 16, 16], U16)
    w16_all = A.alloc([128, NT, 8, 16], F32)
    ixs_all = A.alloc([128, NT, 8, 16], U16)
    iys_all = A.alloc([128, NT, 8, 16], U16)
    A_sb = A.alloc([128, 256], F32)
    sc16 = A.alloc([128, 8, 16], F32)
    pk16 = A.alloc([128, 16], U16)
    iota16 = A.alloc([128, 16], U16)
    isel = A.alloc([128, 16], U16)
    jsel = A.alloc([128, 16], U16)
    oh16 = A.alloc([128, 256], U16)
    pr16 = A.alloc([128, 256], U16)
    m0n8 = A.alloc([128, 8], F32)
    e16 = A.alloc([128, 8, 16], F32)
    l8 = A.alloc([128, 8], F32)
    eidx = A.alloc([128, 128], I32)
    ei_a = A.alloc([128, 128], I32)
    ei_b = A.alloc([128, 128], I32)
    prod5 = A.alloc([128, 8, D], BF16)
    h16 = A.alloc([128, 16], F32)
    hh16 = A.alloc([128, 16], F32)
    g1_tile = A.alloc([128, D], F32)
    g1_acc = A.alloc_at(wqsl_off, [128, 4, D], F32)

    def ldk5(e, sem):
        e.dma_start(out=keysT_sb, in_=keysT.rearrange("p (a b) -> p a b", a=64)).then_inc(sem, 16)
    P.emit("gpsimd", ldk5, deps=("p5_gpsimd",), dma=1, label="keys_loaded")
    ntp5 = 0
    for i in range(NT):
        for c4 in range(2):
            def ftp(e, sem, i=i, c4=c4):
                for j in range(4):
                    last = e.transpose(pbf(j), xn2_t[:, i, (c4 * 4 + j) * 128:(c4 * 4 + j + 1) * 128], c_identb)
                return last
            deps = ["xn2t_done", "p5_tensor"]
            if ntp5 >= 1:
                deps.append(f"tp5ev_{ntp5-1}")
            P.emit("tensor", ftp, deps=tuple(deps), label=f"tp5_{ntp5}")
            def fev(e, sem, i=i, c4=c4):
                for j in range(4):
                    e.tensor_copy(xn2_f[:, c4 * 4 + j, i * 128:(i + 1) * 128], pbf(j))
                return e.drain()
            P.emit("vector", fev, deps=(f"tp5_{ntp5}",), label=f"tp5ev_{ntp5}")
            ntp5 += 1
    P.mark_here("vector", "xn2_done")
    def fio5(e, sem):
        return e.iota(iota16, pattern=[[1, 16]], base=0, channel_multiplier=0)
    P.emit("gpsimd", fio5, deps=("p5_gpsimd",), label="iota5")

    nq5 = 0
    for ph16 in range(NPH16):
        p_, h_ = ph16 // 8, ph16 % 8
        wb = ph16 % 2
        def ldw5(e, sem, p_=p_, h_=h_, wb=wb):
            col0 = p_ * 4096 + h_ * 512
            e.dma_start(out=wq_sl[:, wb, :, :],
                        in_=wqT.rearrange("(c p) o -> p c o", p=128)[:, :, col0:col0 + 512]).then_inc(sem, 16)
        depw = ["p5_gpsimd"]
        if ph16 >= 2:
            depw.append(f"s1_{ph16-2}_{NT-1}")
        P.emit("gpsimd", ldw5, deps=tuple(depw), dma=1, label=f"wq5_{ph16}")
        for i in range(NTI5):
            bq = nq5 % 2
            def fq(e, sem, i=i, wb=wb, bq=bq):
                for oc4 in range(4):
                    for kc in range(NC):
                        last = e.matmul(pb[bq][:, oc4 * 128:(oc4 + 1) * 128],
                                        wq_sl[:, wb, kc, oc4 * 128:(oc4 + 1) * 128],
                                        xn2_f[:, kc, i * 128:(i + 1) * 128],
                                        start=(oc4 == 0 and kc == 0), stop=(kc == NC - 1))
                return last
            depq = [f"wq5_{ph16}", "xn2_done"]

            if nq5 >= 2:
                depq.append(f"qev5_{nq5-2}")
            P.emit("tensor", fq, deps=tuple(depq), label=f"qmm5_{nq5}")
            def fqe(e, sem, bq=bq):
                return e.tensor_copy(qf_sb.rearrange("p a b -> p (a b)"), pb[bq][:])
            dqe = [f"qmm5_{nq5}"]
            if nq5 >= 1:
                dqe.append(f"sim5_{nq5-1}")
            P.emit("vector", fqe, deps=tuple(dqe), label=f"qev5_{nq5}")
            def fsim(e, sem, ph16=ph16, bq=bq):
                for j in range(4):
                    last = e.matmul(pb[4 + (bq % 2)][:, 0:256], qf_sb[:, j, :],
                                    keysT_sb[:, (ph16 % 8) * 8 + (ph16 // 8) * 4 + j, :],
                                    start=(j == 0), stop=(j == 3))
                return last
            dsim = [f"qev5_{nq5}", "keys_loaded"]
            if nq5 >= 2:
                dsim.append(f"simev5_{nq5-2}")
            P.emit("tensor", fsim, deps=tuple(dsim), label=f"sim5_{nq5}")
            def fsime(e, sem, bq=bq):
                return e.tensor_copy(sim_sb, pb[4 + (bq % 2)][:, 0:256])
            P.emit("vector", fsime, deps=(f"sim5_{nq5}",), label=f"simev5_{nq5}")
            def ftop1(e, sem, i=i, ph16=ph16):
                e.max(out=sx_all[:, i, ph16, 0:8], in_=sim_sb)
                e.drain()
                e.max_index(out=ix_all[:, i, ph16, 0:8], in_max=sx_all[:, i, ph16, 0:8], in_values=sim_sb)
                e.drain()
                e.match_replace(out=sim2_sb, in_to_replace=sx_all[:, i, ph16, 0:8], in_values=sim_sb,
                                imm_value=-3.0e38)
                e.drain()
                e.max(out=sx_all[:, i, ph16, 8:16], in_=sim2_sb)
                e.drain()
                e.max_index(out=ix_all[:, i, ph16, 8:16], in_max=sx_all[:, i, ph16, 8:16], in_values=sim2_sb)
                return e.drain()
            P.emit("vector", ftop1, deps=(f"simev5_{nq5}", "iota5"), label=f"s1_{ph16}_{i}")
            nq5 += 1
    if "s1_15_15" not in P.marks and NPH16 == 16:
        pass
    P.mark_here("vector", "stage1_done")

    NB5 = 0 if os.environ.get("SKIP_B") else NT
    for i in range(NB5):
        def fs2(e, sem, i=i):
            for h_ in range(8):
                sx = sx_all[:, i, h_, :]
                sy = sx_all[:, i, 8 + h_, :]
                e.tensor_tensor(A_sb.rearrange("p (a b) -> p a b", a=16),
                                sx.unsqueeze(2).to_broadcast([128, 16, 16]),
                                sy.unsqueeze(1).to_broadcast([128, 16, 16]), ALU.add)
                e.drain()
                e.max(out=sc16[:, h_, 0:8], in_=A_sb)
                e.drain()
                e.max_index(out=pk16[:, 0:8], in_max=sc16[:, h_, 0:8], in_values=A_sb)
                e.drain()
                e.match_replace(out=A_sb, in_to_replace=sc16[:, h_, 0:8], in_values=A_sb, imm_value=-3.0e38)
                e.drain()
                e.max(out=sc16[:, h_, 8:16], in_=A_sb)
                e.drain()
                e.max_index(out=pk16[:, 8:16], in_max=sc16[:, h_, 8:16], in_values=A_sb)
                e.drain()
                e.tensor_scalar(isel, pk16, 4, None, ALU.logical_shift_right)
                e.tensor_scalar(jsel, pk16, 15, None, ALU.bitwise_and)
                e.drain()
                e.tensor_tensor(oh16.rearrange("p (a b) -> p a b", a=16),
                                isel.unsqueeze(2).to_broadcast([128, 16, 16]),
                                iota16.unsqueeze(1).to_broadcast([128, 16, 16]), ALU.is_equal)
                e.drain()
                e.tensor_tensor(pr16.rearrange("p (a b) -> p a b", a=16),
                                oh16.rearrange("p (a b) -> p a b", a=16),
                                ix_all[:, i, h_, :].unsqueeze(1).to_broadcast([128, 16, 16]), ALU.mult)
                e.drain()
                with nc.allow_low_precision(reason="u16 index select"):
                    e.reduce_sum(ixs_all[:, i, h_, :], pr16.rearrange("p (a b) -> p a b", a=16), axis=AX.X)
                e.drain()
                e.tensor_tensor(oh16.rearrange("p (a b) -> p a b", a=16),
                                jsel.unsqueeze(2).to_broadcast([128, 16, 16]),
                                iota16.unsqueeze(1).to_broadcast([128, 16, 16]), ALU.is_equal)
                e.drain()
                e.tensor_tensor(pr16.rearrange("p (a b) -> p a b", a=16),
                                oh16.rearrange("p (a b) -> p a b", a=16),
                                ix_all[:, i, 8 + h_, :].unsqueeze(1).to_broadcast([128, 16, 16]), ALU.mult)
                e.drain()
                with nc.allow_low_precision(reason="u16 index select"):
                    e.reduce_sum(iys_all[:, i, h_, :], pr16.rearrange("p (a b) -> p a b", a=16), axis=AX.X)
                e.drain()
                e.tensor_scalar_mul(m0n8[:, h_:h_ + 1], sc16[:, h_, 0:1], -1.0)
                e.drain()
            return e.drain()
        dep_s2 = [f"s1_15_{i}"]
        if i > 0:
            dep_s2.append(f"sexp_{i-1}")
        P.emit("vector", fs2, deps=tuple(dep_s2), label=f"s2_{i}")
        def fexp(e, sem):
            for h_ in range(8):
                last = e.activation(e16[:, h_, :], sc16[:, h_, :], AF.Exp, bias=m0n8[:, h_:h_ + 1])
            return last
        dep_e = [f"s2_{i}"]
        if i > 0:
            dep_e.append(f"sm2_{i-1}")
        P.emit("scalar", fexp, deps=tuple(dep_e), label=f"sexp_{i}")
        def fsm(e, sem, i=i):
            for h_ in range(8):
                e.reduce_sum(l8[:, h_:h_ + 1], e16[:, h_, :], axis=AX.X)
            e.drain()
            e.reciprocal(l8, l8)
            e.drain()
            for h_ in range(8):
                e.tensor_scalar_mul(w16_all[:, i, h_, :], e16[:, h_, :], l8[:, h_:h_ + 1])
            return e.drain()
        P.emit("vector", fsm, deps=(f"sexp_{i}",), label=f"sm2_{i}")
    P.mark_here("vector", "stage2_done")

    ng5 = 0
    NCI = 1 if os.environ.get("SMALL_C") else NT
    for i in range(NCI):
        def feid(e, sem, i=i):
            e.tensor_copy(ei_a, ixs_all[:, i, :, :].rearrange("p a b -> p (a b)"))
            e.tensor_copy(ei_b, iys_all[:, i, :, :].rearrange("p a b -> p (a b)"))
            e.tensor_scalar(eidx, ei_a, 256, None, ALU.mult)
            return e.tensor_tensor(eidx, eidx, ei_b, ALU.add)
        depe = [f"sm2_{i}"]
        if i > 0:
            depe.append(f"gdone_{i-1}")
        if "stage2_done" not in P.marks:
            P.mark_here("vector", "stage2_done")
        P.emit("gpsimd", feid, deps=tuple(depe), label=f"eid_{i}")
        if debug:
            def fdbg(e, sem, i=i):
                e.dma_start(out=dbg4.rearrange("p (a b) -> p a b", a=16)[:, i, :], in_=eidx).then_inc(sem, 16)
            P.emit("sync", fdbg, deps=(f"eid_{i}",), dma=1, label=f"dbge_{i}")
        for h_ in range(8):
            for hb in range(2):
                def fgth(e, sem, i=i, h_=h_, hb=hb):
                    last = None
                    for kk in range(8):
                        kg = h_ * 16 + hb * 8 + kk
                        last = e.indirect_dma_start(
                            out=wdu[:, hb, kk, :].bitcast(U16), out_offset=None,
                            in_=tabdu_bf[0:128, :],
                            in_offset=IndirectOffsetOnAxis(ap=eidx[:, kg:kg + 1], axis=0))
                        last.then_inc(sem, 16)
                depg = [f"eid_{i}", "tables_done"]
                if ng5 >= 2:
                    depg.append(f"updot_{ng5-2}")
                P.emit("gpsimd", fgth, deps=tuple(depg), dma=8, label=f"gth_{ng5}")
                def fdown(e, sem, i=i, hb=hb):
                    return e.tensor_tensor(prod5, wdu[:, hb, :, 0:D],
                                           xn2_t[:, i, :].unsqueeze(1).to_broadcast([128, 8, D]),
                                           ALU.mult)
                P.emit("vector", fdown, deps=(f"gth_{ng5}",), label=f"down_{ng5}")
                def fgel(e, sem, hb=hb):
                    for kk in range(8):
                        e.activation(prod5[:, kk, :], prod5[:, kk, :], AF.Identity,
                                     accum_out=h16[:, hb * 8 + kk:hb * 8 + kk + 1])
                    return e.activation(hh16[:, hb * 8:(hb + 1) * 8], h16[:, hb * 8:(hb + 1) * 8],
                                        AF.Gelu)
                P.emit("scalar", fgel, deps=(f"down_{ng5}",), label=f"gel_{ng5}")
                def fup(e, sem, i=i, h_=h_, hb=hb):
                    sl = slice(hb * 8, (hb + 1) * 8)
                    e.tensor_tensor(h16[:, sl], hh16[:, sl], w16_all[:, i, h_, sl], ALU.mult)
                    e.drain()
                    for kk in range(8):
                        if h_ == 0 and hb == 0 and kk < 4:
                            e.tensor_scalar_mul(g1_acc[:, kk % 4, :], wdu[:, hb, kk, D:2 * D],
                                                h16[:, hb * 8 + kk:hb * 8 + kk + 1])
                        else:
                            e.scalar_tensor_tensor(g1_acc[:, kk % 4, :], wdu[:, hb, kk, D:2 * D],
                                                   h16[:, hb * 8 + kk:hb * 8 + kk + 1],
                                                   g1_acc[:, kk % 4, :], ALU.mult, ALU.add)
                    return e.drain()
                depu = [f"gel_{ng5}"]
                if h_ == 0 and hb == 0 and i > 0:
                    depu.append(f"g1st_{i-1}")
                P.emit("vector", fup, deps=tuple(depu), label=f"updot_{ng5}")
                ng5 += 1
        def fgsum(e, sem, i=i):
            e.tensor_add(g1_acc[:, 0, :], g1_acc[:, 0, :], g1_acc[:, 1, :])
            e.tensor_add(g1_acc[:, 2, :], g1_acc[:, 2, :], g1_acc[:, 3, :])
            e.drain()
            e.tensor_add(g1_tile, g1_acc[:, 0, :], g1_acc[:, 2, :])
            return e.drain()
        P.emit("vector", fgsum, deps=(f"updot_{ng5-1}",), label=f"gdone_{i}")
        def fg1st(e, sem, i=i):
            e.dma_start(out=g_d.rearrange("p (a b) -> p a b", a=NT)[:, i, :], in_=g1_tile).then_inc(sem, 16)
        P.emit("sync", fg1st, deps=(f"gdone_{i}",), dma=1, label=f"g1st_{i}")
    P.mark_here("sync", "g1_stored")

    if nphase < 6:
        _finish_stub(P, out, xn_t)
        P.build()
        return nc, es

    P.barrier("p6")
    A.reset(base_min)
    s1f = A.alloc([128, NC, T], BF16)
    v65 = A.alloc([128, NT, 16 * 65], BF16)
    xn1f = A.alloc([128, NC, T], BF16)
    Qf = A.alloc([128, NC, T], BF16)
    Kf = A.alloc([128, NC, T], BF16)
    m6 = A.mark()

    xn1sc = A.alloc([128, D], BF16)
    x1_tl = A.alloc([128, D], F32)
    pocket = A.mark()
    ntp6 = 0
    for i in range(NT):
        def fldx(e, sem, i=i):
            e.dma_start(out=x1_tl, in_=x1_d.rearrange("p (a b) -> p a b", a=NT)[:, i, :]).then_inc(sem, 16)
        depl = ["p6_sync", "x1_stored"]
        if i > 0:
            depl.append(f"tp6_{ntp6-1}")
        P.emit("sync", fldx, deps=tuple(depl), dma=1, label=f"x1ld_{i}")
        def fxn1(e, sem, i=i):
            e.tensor_scalar(xn1sc, x1_tl, ln1_rstd[:, i:i + 1], ln1_nmr[:, i:i + 1],
                            ALU.mult, ALU.add)
            return e.drain()
        P.emit("vector", fxn1, deps=(f"x1ld_{i}", "p6_vector"), label=f"xn1_{i}")
        for c4 in range(2):
            def ftp(e, sem, i=i, c4=c4):
                for j in range(4):
                    last = e.transpose(pbf(j), xn1sc[:, (c4 * 4 + j) * 128:(c4 * 4 + j + 1) * 128], c_identb)
                return last
            deps = [f"xn1_{i}", "p6_tensor"]
            if ntp6 >= 1:
                deps.append(f"tp6ev_{ntp6-1}")
            P.emit("tensor", ftp, deps=tuple(deps), label=f"tp6_{ntp6}")
            def fev(e, sem, i=i, c4=c4):
                for j in range(4):
                    e.tensor_copy(xn1f[:, c4 * 4 + j, i * 128:(i + 1) * 128], pbf(j))
                return e.drain()
            P.emit("vector", fev, deps=(f"tp6_{ntp6}",), label=f"tp6ev_{ntp6}")
            ntp6 += 1
    P.mark_here("vector", "xn1f_done")

    wv6 = A.alloc_at(pocket, [128, NC, D], BF16)
    def ldwv(e, sem):
        e.dma_start(out=wv6, in_=wqkv1.rearrange("(c p) o -> p c o", p=128)[:, :, 2 * D:3 * D]).then_inc(sem, 16)
    P.emit("gpsimd", ldwv, deps=("p6_gpsimd",), dma=1, label="wv6_loaded")
    v65r = v65.rearrange("p a (h w) -> p a h w", h=16)
    nv6 = 0
    for i in range(NT):
        for hf in range(2):
            b = nv6 % 2
            def f(e, sem, i=i, hf=hf, b=b):
                for kc in range(NC):
                    last = e.matmul(pb[b][:], xn1f[:, kc, i * 128:(i + 1) * 128],
                                    wv6[:, kc, hf * 512:(hf + 1) * 512],
                                    start=(kc == 0), stop=(kc == NC - 1))
                return last
            deps = ["wv6_loaded", "xn1f_done"]
            if nv6 >= 2:
                deps.append(f"v6ev_{nv6-2}")
            P.emit("tensor", f, deps=tuple(deps), label=f"v6mm_{nv6}")
            def fe(e, sem, i=i, hf=hf, b=b):
                e.tensor_copy(v65r[:, i, hf * 8:(hf + 1) * 8, 0:64],
                              pb[b][:].rearrange("p (a b) -> p a b", a=8))
                return e.drain()
            P.emit("vector", fe, deps=(f"v6mm_{nv6}",), label=f"v6ev_{nv6}")
            nv6 += 1
        def fo(e, sem, i=i):
            e.memset(v65r[:, i, :, 64:65], 1.0)
            return e.drain()
        P.emit("vector", fo, deps=(f"v6ev_{nv6-1}",), label=f"v6o_{i}")
    P.mark_here("vector", "v6_done")

    wg6 = A.alloc_at(pocket, [128, NC, 128], BF16)
    wr6 = A.alloc_at(pocket + 512, [128, NC, 128], BF16)
    cosb = A.alloc_at(pocket + 1024, [128, T], F32)
    sinb = A.alloc_at(pocket + 3072, [128, T], F32)
    rp1 = A.alloc_at(pocket + 5120, [128, 512], F32)
    rp2 = A.alloc_at(pocket + 5632, [128, 512], F32)
    nrk = 0
    for qk in range(2):
        dstf = Qf if qk == 0 else Kf
        cos_in = cosq8 if qk == 0 else cosk
        sin_in = sinq8 if qk == 0 else sink
        def ldcs(e, sem, cos_in=cos_in, sin_in=sin_in):
            e.dma_start(out=cosb, in_=cos_in).then_inc(sem, 16)
            e.dma_start(out=sinb, in_=sin_in).then_inc(sem, 16)
        depc = ["p6_gpsimd", f"v6ev_{nv6-1}", "v6_done"] if qk == 0 else [f"rkev_{nrk-1}"]
        P.emit("gpsimd", ldcs, deps=tuple(depc), dma=2, label=f"cs6_{qk}")
        for grp in range(NC):
            def ldwg(e, sem, qk=qk, grp=grp):
                col0 = qk * D + grp * 128
                e.dma_start(out=wg6, in_=wqkv1.rearrange("(c p) o -> p c o", p=128)[:, :, col0:col0 + 128]).then_inc(sem, 16)
            depw = [f"cs6_{qk}"]
            if nrk >= 1:
                deps_guard = f"rkmm_{nrk-1}"
                depw.append(deps_guard)
            P.emit("gpsimd", ldwg, deps=tuple(depw), dma=1, label=f"wg6_{qk}_{grp}")
            def frot(e, sem):
                w3 = wg6.rearrange("p c (a two) -> p c a two", two=2)
                r3 = wr6.rearrange("p c (a two) -> p c a two", two=2)
                e.tensor_scalar_mul(r3[:, :, :, 0:1], w3[:, :, :, 1:2], -1.0)
                e.drain()
                e.tensor_copy(r3[:, :, :, 1:2], w3[:, :, :, 0:1])
                return e.drain()
            P.emit("vector", frot, deps=(f"wg6_{qk}_{grp}",), label=f"rot_{qk}_{grp}")
            for tc in range(4):
                def fmm(e, sem, grp=grp, tc=tc):
                    for kc in range(NC):
                        e.matmul(pb[0][:], wg6[:, kc, :], xn1f[:, kc, tc * 512:(tc + 1) * 512],
                                 start=(kc == 0), stop=(kc == NC - 1))
                    for kc in range(NC):
                        last = e.matmul(pb[1][:], wr6[:, kc, :], xn1f[:, kc, tc * 512:(tc + 1) * 512],
                                        start=(kc == 0), stop=(kc == NC - 1))
                    return last
                depm = [f"rot_{qk}_{grp}", "v6_done", "xn1f_done"]
                if nrk >= 1:
                    depm.append(f"rkev_{nrk-1}")
                P.emit("tensor", fmm, deps=tuple(depm), label=f"rkmm_{nrk}")
                def fev(e, sem, grp=grp, tc=tc, dstf=dstf):
                    sl = slice(tc * 512, (tc + 1) * 512)
                    e.tensor_tensor(rp1, pb[0][:], cosb[:, sl], ALU.mult)
                    e.tensor_tensor(rp2, pb[1][:], sinb[:, sl], ALU.mult)
                    e.drain()
                    e.tensor_tensor(dstf[:, grp, sl], rp1, rp2, ALU.add)
                    return e.drain()
                P.emit("vector", fev, deps=(f"rkmm_{nrk}",), label=f"rkev_{nrk}")
                nrk += 1
    P.mark_here("vector", "qk6_done")

    Qaug = A.alloc_at(pocket, [128, T], BF16)
    Kaug = A.alloc_at(pocket + 1024, [128, T], BF16)
    PT = A.alloc_at(pocket + 2048, [128, T], BF16)
    PT2 = A.alloc_at(pocket + 3072, [128, T], BF16)
    qqrow = A.alloc_at(pocket + 3072, [1, T], F32, rows=1)
    kkrow = A.alloc_at(pocket + 5120, [1, T], F32, rows=1)
    mkk = A.alloc_at(pocket + 7168, [1, 1], F32, rows=1)
    crow = qqrow
    crow_bf = A.alloc_at(pocket + 5120, [1, T], BF16, rows=1)
    ones64 = A.alloc_at(pocket + 7170, [128, 1], BF16)
    trim6 = A.alloc_at(pocket + 7244, [128, 128], F32)
    rl7 = A.alloc_at(pocket + 7372, [128, 8], F32)
    attn_sc7 = A.alloc_at(pocket + 7380, [128, 7, 64], BF16)
    sq6 = PT.bitcast(F32) if False else None

    def ftr6(e, sem):
        e.dma_start(out=trim6, in_=trimask).then_inc(sem, 16)
    P.emit("gpsimd", ftr6, deps=("p6_gpsimd", "qk6_done"), dma=1, label="tri_loaded")
    def fo64(e, sem):
        e.memset(ones64, 1.0)
        return e.drain()
    P.emit("vector", fo64, deps=("qk6_done",), label="ones64")

    nst6 = 0
    npv6 = 0
    for h in range(16):
        grp, hh = h // 2, h % 2
        rb = hh * 64
        def fsq(e, sem, grp=grp, rb=rb):
            e.activation(PT[0:64, :], Qf[rb:rb + 64, grp, :], AF.Square)
            return e.activation(PT[64:128, :], Kf[rb:rb + 64, grp, :], AF.Square)
        dep0 = ["qk6_done"] if h == 0 else [f"aug_{h-1}", f"pv6c_{npv6-1}"]
        P.emit("scalar", fsq, deps=tuple(dep0), label=f"sq6_{h}")
        for tc in range(4):
            def fqq(e, sem, tc=tc):
                e.matmul(pb[6][0:1, 0:512], ones64[0:64, :], PT[0:64, tc * 512:(tc + 1) * 512],
                         start=True, stop=True)
                return e.matmul(pb[7][0:1, 0:512], ones64[64:128, :],
                                PT[64:128, tc * 512:(tc + 1) * 512], start=True, stop=True)
            deps = [f"sq6_{h}", "ones64"]
            if not (h == 0 and tc == 0):
                deps.append(f"qqe_{h}_{tc-1}" if tc > 0 else f"qqe_{h-1}_3")
            P.emit("tensor", fqq, deps=tuple(deps), label=f"qq_{h}_{tc}")
            def fqe(e, sem, tc=tc):
                e.tensor_copy(qqrow[:, tc * 512:(tc + 1) * 512], pb[6][0:1, 0:512])
                e.tensor_copy(kkrow[:, tc * 512:(tc + 1) * 512], pb[7][0:1, 0:512])
                return e.drain()
            P.emit("vector", fqe, deps=(f"qq_{h}_{tc}",), label=f"qqe_{h}_{tc}")
        def fcrow(e, sem):
            e.reduce_max(mkk, kkrow, axis=AX.X)
            return e.drain()
        P.emit("vector", fcrow, deps=(f"qqe_{h}_3",), label=f"mkk_{h}")
        def fcs(e, sem):
            e.activation(mkk, mkk, AF.Sqrt)
            return e.activation(crow, qqrow, AF.Sqrt)
        P.emit("scalar", fcs, deps=(f"mkk_{h}",), label=f"cs_{h}")
        def fcb(e, sem):
            e.tensor_scalar(crow, crow, mkk, None, ALU.mult)
            e.drain()
            e.tensor_scalar(crow, crow, -1.02, -1.0, ALU.mult, ALU.add)
            e.drain()
            e.tensor_copy(crow_bf, crow)
            return e.drain()
        P.emit("vector", fcb, deps=(f"cs_{h}",), label=f"cb_{h}")
        def faug(e, sem, grp=grp, rb=rb):
            e.tensor_copy(Qaug[0:64, :], Qf[rb:rb + 64, grp, :])
            e.tensor_copy(Kaug[0:64, :], Kf[rb:rb + 64, grp, :])
            e.drain()
            e.tensor_copy(Qaug[64:65, :], crow_bf)
            e.memset(Kaug[64:65, :], 1.0)
            return e.drain()
        P.emit("vector", faug, deps=(f"cb_{h}",), label=f"aug_{h}")
        def acc(qt):
            return pb[4 + qt // 7][:, (qt % 7) * 65:(qt % 7) * 65 + 65]
        def emit_evacs(qts):
            nonlocal npv6
            kt_dep = 6 if qts[-1] <= 6 else (13 if qts[-1] <= 13 else 15)
            def fpve(e, sem, qts=qts):
                for j, qt in enumerate(qts):
                    e.reciprocal(rl7[:, j:j + 1], acc(qt)[:, 64:65])
                e.drain()
                for j, qt in enumerate(qts):
                    e.tensor_scalar_mul(attn_sc7[:, j, :], acc(qt)[:, 0:64], rl7[:, j:j + 1])
                return e.drain()
            depe = [f"pvk_{h}_{min(kt_dep, NT-1)}"]
            if npv6 >= 1:
                depe.append(f"pv6t_{npv6-1}")
            P.emit("vector", fpve, deps=tuple(depe), label=f"pv6e_{npv6}")
            def fpvt(e, sem, qts=qts):
                for j in range(len(qts)):
                    last = e.transpose(pb[7][0:64, j * 64:(j + 1) * 64].bitcast(BF16),
                                       attn_sc7[:, j, :], c_identb)
                return last
            dept = [f"pv6e_{npv6}"]
            if npv6 >= 1:
                dept.append(f"pv6c_{npv6-1}")
            P.emit("tensor", fpvt, deps=tuple(dept), label=f"pv6t_{npv6}")
            def fpvc(e, sem, qts=qts, grp=grp, rb=rb):
                for j, qt in enumerate(qts):
                    e.tensor_copy(s1f[rb:rb + 64, grp, qt * 128:(qt + 1) * 128],
                                  pb[7][0:64, j * 64:(j + 1) * 64].bitcast(BF16))
                return e.drain()
            P.emit("vector", fpvc, deps=(f"pv6t_{npv6}",), label=f"pv6c_{npv6}")
            npv6 += 1
        for kt in range(NT):
            qc0 = (kt * 128) // 512
            pts = PT if kt % 2 == 0 else PT2
            for qc in range(qc0, 4):
                b = nst6 % 4
                def fst(e, sem, kt=kt, qc=qc, b=b):
                    return e.matmul(pb[b][:], Kaug[0:65, kt * 128:(kt + 1) * 128],
                                    Qaug[0:65, qc * 512:(qc + 1) * 512], start=True, stop=True)
                deps = [f"aug_{h}"]
                if nst6 >= 4:
                    deps.append(f"stex_{nst6-4}")
                if h == 0 and nst6 < 4:
                    deps.append(f"rkev_{nrk-1}")
                P.emit("tensor", fst, deps=tuple(deps), label=f"st6_{nst6}")
                def fex(e, sem, qc=qc, b=b, pts=pts):
                    return e.activation(pts[:, qc * 512:(qc + 1) * 512], pb[b][:], AF.Exp)
                dex = [f"st6_{nst6}"]
                if kt > 1 and qc == qc0:
                    dex.append(f"pvk_{h}_{kt-2}")
                P.emit("scalar", fex, deps=tuple(dex), label=f"stex_{nst6}")
                nst6 += 1
            def fmsk(e, sem, kt=kt, pts=pts):
                e.tensor_tensor(pts[:, kt * 128:(kt + 1) * 128], pts[:, kt * 128:(kt + 1) * 128],
                                trim6, ALU.mult)
                return e.drain()
            P.emit("vector", fmsk, deps=(f"stex_{nst6-1}", "tri_loaded"), label=f"msk_{h}_{kt}")
            def fpvk(e, sem, h=h, kt=kt, pts=pts):
                for qt in range(kt, NT):
                    last = e.matmul(acc(qt), pts[:, qt * 128:(qt + 1) * 128],
                                    v65r[:, kt, h, :], start=(kt == 0), stop=(kt == qt))
                return last
            deps = [f"msk_{h}_{kt}", "v6_done"]
            if h > 0 and kt == 0:
                deps.append(f"pv6c_{npv6-1}")
            P.emit("tensor", fpvk, deps=tuple(deps), label=f"pvk_{h}_{kt}")
            if kt == 6:
                emit_evacs(list(range(0, 7)))
            elif kt == 13:
                emit_evacs(list(range(7, 14)))
            elif kt == 15:
                emit_evacs([14, 15])
    P.mark_here("vector", "s1f_done")

    wpj = A.alloc_at(pocket, [128, NC, D], BF16)
    g1t = A.alloc_at(pocket + 4096, [128, D], F32)
    sil6 = A.alloc_at(pocket + 5120, [128, 512], F32)
    cmb6s = A.alloc_at(pocket + 5632, [128, 512], F32)
    def ldwp(e, sem):
        e.dma_start(out=wpj, in_=wproj1.rearrange("(c p) o -> p c o", p=128)).then_inc(sem, 16)
    P.emit("gpsimd", ldwp, deps=("s1f_done",), dma=1, label="wpj_loaded")

    ncm6 = 0
    for i in range(NT):
        def fldx2(e, sem, i=i):
            e.dma_start(out=x1_tl, in_=x1_d.rearrange("p (a b) -> p a b", a=NT)[:, i, :]).then_inc(sem, 16)
            e.dma_start(out=g1t, in_=g_d.rearrange("p (a b) -> p a b", a=NT)[:, i, :]).then_inc(sem, 16)
        depl = ["g1_stored", "s1f_done"]
        if i > 0:
            depl.append(f"cmb6_{2*(i-1)+1}")
        P.emit("sync", fldx2, deps=tuple(depl), dma=2, label=f"xg6_{i}")
        for hf in range(2):
            b = (ncm6 % 2)
            def f(e, sem, i=i, hf=hf, b=b):
                for kc in range(NC):
                    last = e.matmul(pb[b][:], s1f[:, kc, i * 128:(i + 1) * 128],
                                    wpj[:, kc, hf * 512:(hf + 1) * 512],
                                    start=(kc == 0), stop=(kc == NC - 1))
                return last
            deps = [f"xg6_{i}", "wpj_loaded"]
            if ncm6 == 0:
                deps.append(f"rkmm_{nrk-1}")
            if ncm6 >= 2:
                deps.append(f"cmb6_{ncm6-2}")
            P.emit("tensor", f, deps=tuple(deps), label=f"s6mm_{ncm6}")
            def fsil(e, sem, i=i, hf=hf):
                return e.activation(sil6, g1t[:, hf * 512:(hf + 1) * 512], AF.Silu)
            P.emit("scalar", fsil, deps=(f"s6mm_{ncm6}",) + (() if ncm6 == 0 else (f"cmb6_{ncm6-1}",)), label=f"sil6_{ncm6}")
            def fcmb(e, sem, i=i, hf=hf, b=b):
                sl = slice(hf * 512, (hf + 1) * 512)
                e.tensor_tensor(cmb6s, pb[b][:], sil6, ALU.mult)
                e.drain()
                e.tensor_tensor(x1_tl[:, sl], x1_tl[:, sl], cmb6s, ALU.add)
                return e.drain()
            P.emit("vector", fcmb, deps=(f"sil6_{ncm6}",), label=f"cmb6_{ncm6}")
            ncm6 += 1
        def fstout(e, sem, i=i):
            e.dma_start(out=out.rearrange("(t p) d -> p t d", p=128)[:, i, :], in_=x1_tl).then_inc(sem, 16)
        P.emit("sync", fstout, deps=(f"cmb6_{ncm6-1}",), dma=1, label=f"out6_{i}")
    P.mark_here("sync", "out_done")

    P.build()
    return nc, es


def _finish_stub(P, out, xn_t):
    def st_out(e, sem):
        e.dma_start(out=out.rearrange("(t p) d -> p t d", p=128), in_=xn_t).then_inc(sem, 16)
    P.emit("sync", st_out, deps=(), dma=1, label="out_stored")



def make_consts():
    ident = np.eye(128, dtype=np.float32)
    bm = np.zeros((128, 3, 128), dtype=np.float32)
    k_idx = np.arange(128)[:, None]
    q_idx = np.arange(128)[None, :]
    for ri, rel in enumerate((-1, 0, 1)):
        bm[:, ri, :] = (np.abs(128 * rel + q_idx - k_idx) <= 3).astype(np.float32)
    return {
        "ident_f": ident,
        "bandmask": np.ascontiguousarray(bm.reshape(128, 384)),
    }


def make_in_map(inputs, core):
    m = {
        "x": np.ascontiguousarray(inputs["L0_attn_x"][core]),
        "w_qkv": inputs["L0_attn_w_qkv"],
        "proj": inputs["L0_attn_proj"],
        "w_out": inputs["L0_attn_w_out"],
        "b_out": inputs["L0_attn_b_out"].reshape(1, -1),
        "a_range": inputs["L0_ffn_adaptive_range"].reshape(1, 1),
    }
    for k in ("cn_w1", "cn_w2", "cn_w3", "gc_w1", "gc_w2"):
        m[k] = inputs[f"L0_ffn_{k}"]
    m["cn_w4"] = np.ascontiguousarray(inputs["L0_ffn_cn_w4"].reshape(2, 128).T)
    m["gc_w3"] = np.ascontiguousarray(inputs["L0_ffn_gc_w3"].reshape(2, 128).T)
    for k in ("cn_b1", "cn_b2", "cn_b3", "gc_b1", "gc_b2"):
        m[k] = np.ascontiguousarray(inputs[f"L0_ffn_{k}"].reshape(-1, 128).T)
    for k in ("cn_b4", "gc_b3"):
        m[k] = inputs[f"L0_ffn_{k}"].reshape(1, 1)
    m["wqkv1"] = inputs["L1_attn_Wqkv"]
    m["wproj1"] = inputs["L1_attn_Wproj"]
    m["gamma"] = inputs["L1_ffn_gamma"].reshape(1, -1)
    m["wqT"] = np.ascontiguousarray(inputs["L1_ffn_wq"].T)
    k = inputs["L1_ffn_keys"].transpose(0, 2, 3, 1).reshape(8, 2, 4, 128, 256)
    m["keysT"] = np.ascontiguousarray(k.transpose(3, 0, 1, 2, 4).reshape(128, 64 * 256))
    m["tab_du"] = np.ascontiguousarray(
        np.concatenate([inputs["L1_ffn_table_down"], inputs["L1_ffn_table_up"]], axis=1))
    dh = 64
    inv = 1.0 / (10000.0 ** (np.arange(0, dh, 2, dtype=np.float64) / dh))
    fr = np.arange(T, dtype=np.float64)[:, None] * inv[None, :]
    cosf = np.repeat(np.cos(fr), 2, axis=-1).T
    sinf = np.repeat(np.sin(fr), 2, axis=-1).T
    m["cosq8"] = np.tile(cosf / 8.0, (2, 1)).astype(np.float32)
    m["sinq8"] = np.tile(sinf / 8.0, (2, 1)).astype(np.float32)
    m["cosk"] = np.tile(cosf, (2, 1)).astype(np.float32)
    m["sink"] = np.tile(sinf, (2, 1)).astype(np.float32)
    kq = np.arange(128)
    m["trimask"] = (kq[:, None] <= kq[None, :]).astype(np.float32)
    m.update(make_consts())
    return m


def kernel(**inputs):
    nc, _es = build_program(nphase=6)
    in_maps = [make_in_map(inputs, c) for c in range(8)]
    res = run_bass_kernel_spmd(nc, in_maps, core_ids=list(range(8)))
    return np.stack([np.asarray(res.results[c]["out"]) for c in range(8)], axis=0)


if __name__ == "__main__":
    nc, _ = build_program(nphase=0)
    print("built phase0 ok")
